# version 02
# baseline (speedup 1.0000x reference)
import numpy as np
import concourse.bass as bass
import concourse.mybir as mybir
from concourse.bass_utils import run_bass_kernel_spmd

F32 = mybir.dt.float32
BF16 = mybir.dt.bfloat16
I32 = mybir.dt.int32
F16 = mybir.dt.float16
AF = mybir.ActivationFunctionType
ALU = mybir.AluOpType
AX = mybir.AxisListType

ENGINES = ("sync", "gpsimd", "tensor", "vector", "scalar")


class Seq:

    def __init__(self, nc):
        self.nc = nc
        self.items = []

    def add(self, eng, fn, dma=False):
        self.items.append((eng, fn, dma))

    def dma(self, eng, out, in_, **kw):
        self.add(eng, lambda e: e.dma_start(out=out, in_=in_, **kw), dma=True)

    def addf(self, eng, fn):
        self.add(eng, fn)
        self.add("scalar" if eng != "scalar" else "vector", lambda e: e.nop())

    def build(self, extra_engines=()):
        nc = self.nc
        n = len(self.items)
        incs = [False] * n
        for i, (eng, fn, dma) in enumerate(self.items):
            if dma:
                incs[i] = True
            elif i == n - 1 or self.items[i + 1][0] != eng:
                incs[i] = True
        sems = {}
        sem_ctx = []
        for e in ENGINES:
            g = nc.semaphore(f"sq_{e}")
            sems[e] = g.__enter__()
            sem_ctx.append(g)
        counts = {e: 0 for e in ENGINES}
        plan = []
        for i, (eng, fn, dma) in enumerate(self.items):
            waits = []
            if i > 0:
                peng, pfn, pdma = self.items[i - 1]
                if peng != eng or pdma:
                    waits.append((sems[peng], counts[peng]))
            inc = None
            if incs[i]:
                amt = 16 if dma else 1
                counts[eng] += amt
                inc = (sems[eng], amt)
            plan.append((waits, inc))

        per_eng = {e: [] for e in ENGINES}
        for i, (eng, fn, dma) in enumerate(self.items):
            per_eng[eng].append((self.items[i], plan[i]))

        waiter = "gpsimd"
        final_waits = [(sems[e], counts[e]) for e in ENGINES
                       if counts[e] > 0 and e != waiter]
        final_self = counts[waiter]

        with nc.Block() as block:
            for e in ENGINES:
                entries = per_eng[e]
                if not entries and e != waiter:
                    continue

                def make(entries, is_waiter):
                    def f(engobj):
                        for (eng, fn, dma), (waits, inc) in entries:
                            for (sem, val) in waits:
                                if val > 0:
                                    engobj.wait_ge(sem, val)
                            inst = fn(engobj)
                            if inc is not None:
                                inst.then_inc(inc[0], inc[1])
                        if is_waiter:
                            for (sem, val) in final_waits:
                                engobj.wait_ge(sem, val)
                            if final_self > 0:
                                engobj.wait_ge(sems[waiter], final_self)
                    return f

                getattr(block, e)(make(entries, e == waiter))
        for g in sem_ctx:
            g.__exit__(None, None, None)
        return self


class Sbuf:

    def __init__(self, nc, pfx=""):
        self.nc = nc
        self.pfx = pfx
        self.ctx = []

    def sb(self, name, shape, dt):
        g = self.nc.sbuf_tensor(self.pfx + name, shape, dt)
        t = g.__enter__()
        self.ctx.append(g)
        return t

    def ps(self, name, shape, dt=F32):
        g = self.nc.psum_tensor(self.pfx + name, shape, dt)
        t = g.__enter__()
        self.ctx.append(g)
        return t

    def close(self):
        for g in reversed(self.ctx):
            g.__exit__(None, None, None)


def emit_consts(seq, A):
    def f(v):
        v.memset(A["c_eps5"][:, :], 1e-5)
        v.memset(A["c_eps6"][:, :], 1e-6)
        return v.memset(A["c_m05"][:, :], -0.5)
    seq.add("vector", f)

    seq.add("gpsimd", lambda g: g.memset(A["ident"][:, :], 1.0))
    seq.add("gpsimd", lambda g: g.affine_select(
        out=A["ident"][:, :], in_=A["ident"][:, :],
        pattern=[[1, 128]], compare_op=ALU.is_equal, fill=0.0,
        base=0, channel_multiplier=-1))


def emit_ln_stats(seq, A, src_ap, scratch, pfx=""):
    D = src_ap.shape[-1]
    s_, ssq = A["s_"], A["ssq"]
    mean, negmean, t1, var, rstd = A["mean"], A["negmean"], A["t1"], A["var"], A["rstd"]

    seq.add("scalar", lambda e: e.activation(out=scratch[:, 0:D], in_=src_ap, func=AF.Copy,
                                             accum_out=s_[:, :]))
    seq.add("scalar", lambda e: e.activation(out=scratch[:, 0:D], in_=src_ap, func=AF.Square,
                                             accum_out=ssq[:, :]))

    def dve(v):
        v.tensor_scalar(out=mean[:, :], in0=s_[:, :], scalar1=1.0 / D, scalar2=None, op0=ALU.mult)
        v.tensor_scalar(out=negmean[:, :], in0=s_[:, :], scalar1=-1.0 / D, scalar2=None, op0=ALU.mult)
        v.tensor_scalar(out=t1[:, :], in0=ssq[:, :], scalar1=1.0 / D, scalar2=None, op0=ALU.mult)
        return v.scalar_tensor_tensor(out=var[:, :], in0=mean[:, :], scalar=negmean[:, :],
                                      in1=t1[:, :], op0=ALU.mult, op1=ALU.add)
    seq.add("vector", dve)

    def act2(e):
        e.activation(out=var[:, :], in_=var[:, :], func=AF.Ln, bias=A["c_eps5"][:, :], scale=1.0)
        return e.activation(out=rstd[:, :], in_=var[:, :], func=AF.Exp, bias=0.0,
                            scale=A["c_m05"][:, :])
    seq.add("scalar", act2)


def emit_l1ffn(nc, seq, xs, xo, pfx="d_", dbg=False):
    TOK = 2048
    D = 2048
    HID = 5632
    HC = HID // 128

    wgT = nc.dram_tensor("wgT", [D, HID], I32, kind="ExternalInput").ap()
    wuT = nc.dram_tensor("wuT", [D, HID], I32, kind="ExternalInput").ap()
    wdT = nc.dram_tensor("wdT", [HID, D], I32, kind="ExternalInput").ap()
    sg = nc.dram_tensor("sg", [128, HC], F32, kind="ExternalInput").ap()
    su = nc.dram_tensor("su", [128, HC], F32, kind="ExternalInput").ap()
    sdbc = nc.dram_tensor("sdbc", [128, D], F32, kind="ExternalInput").ap()
    if dbg:
        d_xlnT = nc.dram_tensor("d_xlnT", [128, 16 * TOK], BF16, kind="ExternalOutput").ap()
        d_hT = nc.dram_tensor("d_hT", [128, 44 * 1024], BF16, kind="ExternalOutput").ap()
        d_y = nc.dram_tensor("d_y", [128, D], F32, kind="ExternalOutput").ap()
        d_wg = nc.dram_tensor("d_wg", [128, 2048], BF16, kind="ExternalOutput").ap()
        d_gs = nc.dram_tensor("d_gs", [128, 512], BF16, kind="ExternalOutput").ap()
        d_up = nc.dram_tensor("d_up", [128, 512], BF16, kind="ExternalOutput").ap()
        d_pg = nc.dram_tensor("d_pg", [128, 1024], F32, kind="ExternalOutput").ap()

    al = Sbuf(nc, pfx)
    A = {}
    A["xlnT"] = al.sb("xlnT", [128, 16 * TOK], BF16)
    A["hT"] = al.sb("hT", [128, HC * 1024], BF16)
    A["xtile"] = al.sb("xtile", [128, D], F32)
    A["lnb"] = al.sb("lnb", [128, D], BF16)
    A["scratch"] = A["lnb"]
    A["wcol"] = al.sb("wcol", [128, 16 * 128], I32)
    A["wg_bf"] = al.sb("wg_bf", [128, 16 * 128], BF16)
    A["wu_bf"] = al.sb("wu_bf", [128, 16 * 128], BF16)
    A["wd_i"] = A["wcol"]
    A["wd_bf"] = al.sb("wd_bf", [128, D], BF16)
    A["sg_s"] = al.sb("sg_s", [128, HC], F32)
    A["su_s"] = al.sb("su_s", [128, HC], F32)
    A["sdbc_s"] = al.sb("sdbc_s", [128, D], F32)
    A["gsil"] = al.sb("gsil", [128, 512], BF16)
    A["upb"] = al.sb("upb", [128, 512], BF16)
    A["ysb"] = al.sb("ysb", [128, D], F32)
    A["ident"] = al.sb("ident", [128, 128], BF16)
    A["sprt"] = al.sb("sprt", [128, 4], F32)
    A["ssqp"] = al.sb("ssqp", [128, 4], F32)
    for nm in ["c_eps5", "c_eps6", "c_m05", "s_", "ssq", "mean", "negmean", "t1", "var", "rstd"]:
        A[nm] = al.sb(nm, [128, 1], F32)
    P1 = al.ps("P1", [128, 2048], F32)
    PT = al.ps("PT", [128, 128], BF16)

    emit_consts(seq, A)
    seq.dma("sync", A["sg_s"][:, :], sg)
    seq.dma("sync", A["su_s"][:, :], su)
    seq.dma("sync", A["sdbc_s"][:, :], sdbc)

    for rt in range(16):
        seq.dma("sync", A["xtile"][:, :], xs[rt * 128:(rt + 1) * 128, :])
        emit_ln_stats(seq, A, A["xtile"][:, :], A["scratch"])
        seq.add("vector", lambda v, A=A: v.tensor_scalar(
            out=A["lnb"][:, :], in0=A["xtile"][:, :], scalar1=A["negmean"][:, :],
            scalar2=A["rstd"][:, :], op0=ALU.add, op1=ALU.mult))
        for c in range(16):
            seq.add("tensor", lambda t, A=A, c=c: t.transpose(
                PT[:, :], A["lnb"][:, c * 128:(c + 1) * 128], A["ident"][:, :]))
            seq.add("vector", lambda v, A=A, c=c, rt=rt: v.tensor_copy(
                out=A["xlnT"][:, c * TOK + rt * 128: c * TOK + (rt + 1) * 128],
                in_=PT[:, :]))

    for grp in range(2):
        for hc in range(HC):
            for (wT, wbf) in ((wgT, "wg_bf"), (wuT, "wu_bf")):
                for q in range(4):
                    seq.dma("sync",
                            A["wcol"][:, q * 4 * 128: (q + 1) * 4 * 128]
                            .rearrange("p (c h) -> p c h", c=4),
                            wT[q * 512:(q + 1) * 512, hc * 128:(hc + 1) * 128]
                            .rearrange("(c p) h -> p c h", p=128))
                seq.add("vector", lambda v, A=A, wbf=wbf: v.tensor_scalar(
                    out=A[wbf][:, :], in0=A["wcol"][:, :], scalar1=128.0, scalar2=None,
                    op0=ALU.subtract))
            for (wbf, off) in (("wg_bf", 0), ("wu_bf", 1024)):
                for kc in range(16):
                    for tg in range(2):
                        seq.add("tensor", lambda t, A=A, wbf=wbf, kc=kc, tg=tg, off=off, grp=grp: t.matmul(
                            P1[:, off + tg * 512: off + (tg + 1) * 512],
                            A[wbf][:, kc * 128:(kc + 1) * 128],
                            A["xlnT"][:, kc * TOK + grp * 1024 + tg * 512: kc * TOK + grp * 1024 + (tg + 1) * 512],
                            start=(kc == 0), stop=(kc == 15)))
            for tg in range(2):
                seq.add("scalar", lambda e, A=A, tg=tg, hc=hc: e.activation(
                    out=A["gsil"][:, :], in_=P1[:, tg * 512:(tg + 1) * 512], func=AF.Silu,
                    bias=0.0, scale=A["sg_s"][:, hc:hc + 1]))
                seq.add("vector", lambda v, A=A, tg=tg, hc=hc: v.tensor_scalar(
                    out=A["upb"][:, :], in0=P1[:, 1024 + tg * 512: 1024 + (tg + 1) * 512],
                    scalar1=A["su_s"][:, hc:hc + 1], scalar2=None, op0=ALU.mult))
                seq.add("vector", lambda v, A=A, tg=tg, hc=hc: v.tensor_tensor(
                    out=A["hT"][:, hc * 1024 + tg * 512: hc * 1024 + (tg + 1) * 512],
                    in0=A["gsil"][:, :], in1=A["upb"][:, :], op=ALU.mult))

        if dbg and grp == 1:
            seq.dma("gpsimd", d_xlnT, A["xlnT"][:, :])
            seq.dma("gpsimd", d_hT, A["hT"][:, :])
            seq.dma("gpsimd", d_wg, A["wg_bf"][:, :])
            seq.dma("gpsimd", d_gs, A["gsil"][:, :])
            seq.dma("gpsimd", d_up, A["upb"][:, :])
            seq.add("vector", lambda v, A=A: v.tensor_copy(out=A["ysb"][:, 0:1024], in_=P1[:, 0:1024]))
            seq.dma("gpsimd", d_pg, A["ysb"][:, 0:1024])
        for tt_ in range(8):
            rt = grp * 8 + tt_
            for hc in range(HC):
                seq.dma("sync", A["wd_i"][:, :], wdT[hc * 128:(hc + 1) * 128, :])
                seq.add("vector", lambda v, A=A: v.tensor_scalar(
                    out=A["wd_bf"][:, :], in0=A["wd_i"][:, :], scalar1=128.0, scalar2=None,
                    op0=ALU.subtract))
                for nn in range(4):
                    seq.add("tensor", lambda t, A=A, hc=hc, tt_=tt_, nn=nn: t.matmul(
                        P1[:, nn * 512:(nn + 1) * 512],
                        A["hT"][:, hc * 1024 + tt_ * 128: hc * 1024 + (tt_ + 1) * 128],
                        A["wd_bf"][:, nn * 512:(nn + 1) * 512],
                        start=(hc == 0), stop=(hc == HC - 1)))
            for nn in range(4):
                seq.add("vector", lambda v, A=A, nn=nn: v.scalar_tensor_tensor(
                    out=A["ysb"][:, nn * 512:(nn + 1) * 512], in0=P1[:, nn * 512:(nn + 1) * 512],
                    scalar=1.0, in1=A["sdbc_s"][:, nn * 512:(nn + 1) * 512],
                    op0=ALU.mult, op1=ALU.mult))
            if dbg and rt == 15:
                seq.dma("gpsimd", d_y, A["ysb"][:, :])
            emit_ln_stats(seq, A, A["ysb"][:, :], A["scratch"])
            seq.dma("sync", A["xtile"][:, :], xs[rt * 128:(rt + 1) * 128, :])
            seq.add("vector", lambda v, A=A: v.tensor_scalar(
                out=A["ysb"][:, :], in0=A["ysb"][:, :], scalar1=A["negmean"][:, :],
                scalar2=A["rstd"][:, :], op0=ALU.add, op1=ALU.mult))
            seq.add("vector", lambda v, A=A: v.tensor_tensor(
                out=A["xtile"][:, :], in0=A["xtile"][:, :], in1=A["ysb"][:, :], op=ALU.add))
            seq.dma("gpsimd", xo[rt * 128:(rt + 1) * 128, :], A["xtile"][:, :])

    al.close()


def l1ffn_host_prep(inputs):
    wg = inputs["L1_ffn_wq_gate"]
    wu = inputs["L1_ffn_wq_up"]
    wd = inputs["L1_ffn_wq_down"]
    shared = {
        "wgT": np.ascontiguousarray(wg.T).astype(np.int32),
        "wuT": np.ascontiguousarray(wu.T).astype(np.int32),
        "wdT": np.ascontiguousarray(wd.T).astype(np.int32),
        "sg": np.ascontiguousarray(
            np.asarray(inputs["L1_ffn_s_gate"], np.float32).reshape(44, 128).T),
        "su": np.ascontiguousarray(
            np.asarray(inputs["L1_ffn_s_up"], np.float32).reshape(44, 128).T),
        "sdbc": np.ascontiguousarray(np.broadcast_to(
            np.asarray(inputs["L1_ffn_s_down"], np.float32)[None, :], (128, 2048))),
    }
    return shared


def run_l1ffn(x, inputs, n_cores=8, dbg=False):
    shared = l1ffn_host_prep(inputs)
    raise NotImplementedError
    in_maps = []
    for c in range(n_cores):
        xs = np.ascontiguousarray(
            x[:, c * 512:(c + 1) * 512, :].reshape(2048, 2048)).astype(np.float32)
        m = {"xs": xs}
        m.update(shared)
        in_maps.append(m)
    res = run_bass_kernel_spmd(nc, in_maps, core_ids=list(range(n_cores)))
    out = np.empty((4, 4096, 2048), np.float32)
    for c in range(n_cores):
        out[:, c * 512:(c + 1) * 512, :] = res.results[c]["xo"].reshape(4, 512, 2048)
    if dbg:
        return out, res.results
    return out


def emit_lnT(seq, A, xs, rt, dest, dest_col, dest_stride, PT):
    seq.dma("sync", A["xtile"][:, :], xs[rt * 128:(rt + 1) * 128, :])
    emit_ln_stats(seq, A, A["xtile"][:, :], A["lnb"])
    seq.add("vector", lambda v: v.tensor_scalar(
        out=A["lnb"][:, :], in0=A["xtile"][:, :], scalar1=A["negmean"][:, :],
        scalar2=A["rstd"][:, :], op0=ALU.add, op1=ALU.mult))
    for c in range(16):
        seq.add("tensor", lambda t, c=c: t.transpose(
            PT[:, :], A["lnb"][:, c * 128:(c + 1) * 128], A["ident"][:, :]))
        seq.add("vector", lambda v, c=c: v.tensor_copy(
            out=dest[:, c * dest_stride + dest_col: c * dest_stride + dest_col + 128],
            in_=PT[:, :]))


def emit_dequant_fp4(seq, A, pk_ap, scf_ap, ngroups, C, wc_dest):
    for j in range(8):
        seq.add("vector", lambda v, j=j: v.tensor_scalar(
            out=A["dq_i"][:, 0:C], in0=pk_ap, scalar1=4 * j, scalar2=7,
            op0=ALU.logical_shift_right, op1=ALU.bitwise_and))
        seq.add("vector", lambda v: v.tensor_copy(
            out=A["dq_mf"][:, 0:C], in_=A["dq_i"][:, 0:C]))
        seq.add("vector", lambda v, j=j: v.tensor_scalar(
            out=A["dq_i"][:, 0:C], in0=pk_ap, scalar1=4 * j + 3, scalar2=1,
            op0=ALU.logical_shift_right, op1=ALU.bitwise_and))
        seq.add("vector", lambda v: v.tensor_scalar(
            out=A["dq_sgn"][:, 0:C], in0=A["dq_i"][:, 0:C], scalar1=-2.0, scalar2=1.0,
            op0=ALU.mult, op1=ALU.add))
        seq.add("scalar", lambda e: e.activation(
            out=A["dq_r1"][:, 0:C], in_=A["dq_mf"][:, 0:C], func=AF.Relu,
            bias=A["c_m2"][:, :], scale=0.5))
        seq.add("scalar", lambda e: e.activation(
            out=A["dq_r2"][:, 0:C], in_=A["dq_mf"][:, 0:C], func=AF.Relu,
            bias=A["c_m6"][:, :], scale=1.0))
        seq.add("vector", lambda v: v.scalar_tensor_tensor(
            out=A["dq_v"][:, 0:C], in0=A["dq_mf"][:, 0:C], scalar=0.5,
            in1=A["dq_r1"][:, 0:C], op0=ALU.mult, op1=ALU.add))
        seq.add("vector", lambda v: v.tensor_tensor(
            out=A["dq_v"][:, 0:C], in0=A["dq_v"][:, 0:C], in1=A["dq_r2"][:, 0:C],
            op=ALU.add))
        seq.add("vector", lambda v: v.tensor_tensor(
            out=A["dq_v"][:, 0:C], in0=A["dq_v"][:, 0:C], in1=A["dq_sgn"][:, 0:C],
            op=ALU.mult))
        seq.add("vector", lambda v, j=j: v.scalar_tensor_tensor(
            out=wc_dest[:, j::8].rearrange("p (g r) -> p g r", g=ngroups),
            in0=A["dq_v"][:, 0:C].rearrange("p (g r) -> p g r", g=ngroups),
            scalar=1.0,
            in1=scf_ap.unsqueeze(2).broadcast_to([128, ngroups, 16]),
            op0=ALU.mult, op1=ALU.mult))


def emit_l0ffn(nc, seq, xs, xo, pfx="b_"):
    TOK = 2048
    D = 2048
    I = 5632
    GUC = 88
    IC = 44

    gup = nc.dram_tensor("gup", [2 * I, 256], I32, kind="ExternalInput").ap()
    gusT = nc.dram_tensor("gusT", [2 * I, 16], F16, kind="ExternalInput").ap()
    dnp = nc.dram_tensor("dnp", [D, 704], I32, kind="ExternalInput").ap()
    dnsT = nc.dram_tensor("dnsT", [D, IC], F16, kind="ExternalInput").ap()
    wguT_d = nc.dram_tensor("wguT_d", [D, 2 * I], BF16).ap()
    wdnT_d = nc.dram_tensor("wdnT_d", [I, D], BF16).ap()

    al = Sbuf(nc, pfx)
    A = {}
    A["xlnT"] = al.sb("xlnT", [128, 16 * 1024], BF16)
    A["hT"] = al.sb("hT", [128, IC * 1024], BF16)
    A["xtile"] = al.sb("xtile", [128, D], F32)
    A["lnb"] = al.sb("lnb", [128, D], BF16)
    A["ysb"] = al.sb("ysb", [128, D], F32)
    A["pk"] = al.sb("pk", [128, 704], I32)
    A["sc16"] = al.sb("sc16", [128, IC], F16)
    A["scf"] = al.sb("scf", [128, IC], F32)
    A["dq_i"] = al.sb("dq_i", [128, 704], I32)
    A["dq_mf"] = al.sb("dq_mf", [128, 704], F32)
    A["dq_sgn"] = al.sb("dq_sgn", [128, 704], F32)
    A["dq_r1"] = al.sb("dq_r1", [128, 704], F32)
    A["dq_r2"] = al.sb("dq_r2", [128, 704], F32)
    A["dq_v"] = al.sb("dq_v", [128, 704], F32)
    A["Wc"] = al.sb("Wc", [128, I], BF16)
    A["WcT"] = al.sb("WcT", [128, IC * 128], BF16)
    A["wstage_g"] = al.sb("wstage_g", [128, 16 * 128], BF16)
    A["wstage_u"] = al.sb("wstage_u", [128, 16 * 128], BF16)
    A["wdstage"] = al.sb("wdstage", [128, D], BF16)
    A["gsil"] = al.sb("gsil", [128, 512], BF16)
    A["upb"] = al.sb("upb", [128, 512], BF16)
    A["ident"] = al.sb("ident", [128, 128], BF16)
    for nm in ["c_eps5", "c_eps6", "c_m05", "c_m2", "c_m6", "c_e11",
               "s_", "ssq", "mean", "negmean", "t1", "var", "rstd"]:
        A[nm] = al.sb(nm, [128, 1], F32)
    PB = al.ps("PB", [128, 2048], F32)
    PT = al.ps("PT", [128, 128], BF16)

    emit_consts(seq, A)
    seq.add("vector", lambda v: v.memset(A["c_m2"][:, :], -2.0))
    seq.add("vector", lambda v: v.memset(A["c_m6"][:, :], -6.0))
    seq.add("vector", lambda v: v.memset(A["c_e11"][:, :], 1.000005e-11))

    for gc in range(GUC):
        seq.dma("sync", A["pk"][:, 0:256], gup[gc * 128:(gc + 1) * 128, :])
        seq.dma("sync", A["sc16"][:, 0:16], gusT[gc * 128:(gc + 1) * 128, :])
        seq.add("vector", lambda v: v.tensor_copy(out=A["scf"][:, 0:16], in_=A["sc16"][:, 0:16]))
        emit_dequant_fp4(seq, A, A["pk"][:, 0:256], A["scf"][:, 0:16], 16, 256,
                         A["Wc"][:, 0:2048])
        for kc in range(16):
            seq.add("tensor", lambda t, kc=kc: t.transpose(
                PT[:, :], A["Wc"][:, kc * 128:(kc + 1) * 128], A["ident"][:, :]))
            seq.add("vector", lambda v, kc=kc: v.tensor_copy(
                out=A["WcT"][:, kc * 128:(kc + 1) * 128], in_=PT[:, :]))
        for half in range(2):
            seq.dma("gpsimd",
                    wguT_d[half * 1024:(half + 1) * 1024, gc * 128:(gc + 1) * 128]
                    .rearrange("(k p) o -> p k o", p=128),
                    A["WcT"][:, half * 1024:(half + 1) * 1024]
                    .rearrange("p (k o) -> p k o", k=8))

    for oc in range(16):
        seq.dma("sync", A["pk"][:, 0:704], dnp[oc * 128:(oc + 1) * 128, :])
        seq.dma("sync", A["sc16"][:, 0:IC], dnsT[oc * 128:(oc + 1) * 128, :])
        seq.add("vector", lambda v: v.tensor_copy(out=A["scf"][:, 0:IC], in_=A["sc16"][:, 0:IC]))
        emit_dequant_fp4(seq, A, A["pk"][:, 0:704], A["scf"][:, 0:IC], IC, 704,
                         A["Wc"][:, 0:I])
        for kc in range(IC):
            seq.add("tensor", lambda t, kc=kc: t.transpose(
                PT[:, :], A["Wc"][:, kc * 128:(kc + 1) * 128], A["ident"][:, :]))
            seq.add("vector", lambda v, kc=kc: v.tensor_copy(
                out=A["WcT"][:, kc * 128:(kc + 1) * 128], in_=PT[:, :]))
        for half in range(4):
            seq.dma("gpsimd",
                    wdnT_d[half * 1408:(half + 1) * 1408, oc * 128:(oc + 1) * 128]
                    .rearrange("(k p) o -> p k o", p=128),
                    A["WcT"][:, half * 1408:(half + 1) * 1408]
                    .rearrange("p (k o) -> p k o", k=11))

    for grp in range(2):
        for rtl in range(8):
            rt = grp * 8 + rtl
            emit_lnT(seq, A, xs, rt, A["xlnT"], rtl * 128, 1024, PT)
        for gcp in range(IC):
            for (gc2, wst) in ((gcp, "wstage_g"), (IC + gcp, "wstage_u")):
                for half in range(2):
                    seq.dma("sync",
                            A[wst][:, half * 1024:(half + 1) * 1024]
                            .rearrange("p (k o) -> p k o", k=8),
                            wguT_d[half * 1024:(half + 1) * 1024, gc2 * 128:(gc2 + 1) * 128]
                            .rearrange("(k p) o -> p k o", p=128))
            for (wst, off) in (("wstage_g", 0), ("wstage_u", 1024)):
                for kc in range(16):
                    for tg in range(2):
                        seq.add("tensor", lambda t, wst=wst, kc=kc, tg=tg, off=off, grp=grp: t.matmul(
                            PB[:, off + tg * 512: off + (tg + 1) * 512],
                            A[wst][:, kc * 128:(kc + 1) * 128],
                            A["xlnT"][:, kc * 1024 + tg * 512: kc * 1024 + (tg + 1) * 512],
                            start=(kc == 0), stop=(kc == 15)))
            for tg in range(2):
                seq.add("scalar", lambda e, tg=tg: e.activation(
                    out=A["gsil"][:, :], in_=PB[:, tg * 512:(tg + 1) * 512], func=AF.Silu,
                    bias=0.0, scale=1.0))
                seq.add("vector", lambda v, tg=tg: v.tensor_copy(
                    out=A["upb"][:, :], in_=PB[:, 1024 + tg * 512:1024 + (tg + 1) * 512]))
                seq.add("vector", lambda v, tg=tg, gcp=gcp: v.tensor_tensor(
                    out=A["hT"][:, gcp * 1024 + tg * 512: gcp * 1024 + (tg + 1) * 512],
                    in0=A["gsil"][:, :], in1=A["upb"][:, :], op=ALU.mult))
        for tt_ in range(8):
            rt = grp * 8 + tt_
            for ic in range(IC):
                seq.dma("sync", A["wdstage"][:, :], wdnT_d[ic * 128:(ic + 1) * 128, :])
                for nn in range(4):
                    seq.add("tensor", lambda t, ic=ic, tt_=tt_, nn=nn: t.matmul(
                        PB[:, nn * 512:(nn + 1) * 512],
                        A["hT"][:, ic * 1024 + tt_ * 128: ic * 1024 + (tt_ + 1) * 128],
                        A["wdstage"][:, nn * 512:(nn + 1) * 512],
                        start=(ic == 0), stop=(ic == IC - 1)))
            seq.add("scalar", lambda e: e.activation(
                out=A["ysb"][:, :], in_=PB[:, :], func=AF.Copy, accum_out=A["s_"][:, :]))
            seq.add("scalar", lambda e: e.activation(
                out=A["lnb"][:, :], in_=A["ysb"][:, :], func=AF.Square, accum_out=A["ssq"][:, :]))

            def dve_stats(v):
                v.tensor_scalar(out=A["mean"][:, :], in0=A["s_"][:, :], scalar1=1.0 / D,
                                scalar2=None, op0=ALU.mult)
                v.tensor_scalar(out=A["negmean"][:, :], in0=A["s_"][:, :], scalar1=-1.0 / D,
                                scalar2=None, op0=ALU.mult)
                v.tensor_scalar(out=A["t1"][:, :], in0=A["ssq"][:, :],
                                scalar1=(1.0 + 1e-5) / D, scalar2=None, op0=ALU.mult)
                return v.scalar_tensor_tensor(out=A["var"][:, :], in0=A["mean"][:, :],
                                              scalar=A["negmean"][:, :], in1=A["t1"][:, :],
                                              op0=ALU.mult, op1=ALU.add)
            seq.add("vector", dve_stats)

            def act_rstd(e):
                e.activation(out=A["var"][:, :], in_=A["var"][:, :], func=AF.Ln,
                             bias=A["c_e11"][:, :], scale=1.0)
                return e.activation(out=A["rstd"][:, :], in_=A["var"][:, :], func=AF.Exp,
                                    bias=0.0, scale=A["c_m05"][:, :])
            seq.add("scalar", act_rstd)
            seq.dma("sync", A["xtile"][:, :], xs[rt * 128:(rt + 1) * 128, :])
            seq.add("vector", lambda v: v.tensor_scalar(
                out=A["ysb"][:, :], in0=A["ysb"][:, :], scalar1=A["negmean"][:, :],
                scalar2=A["rstd"][:, :], op0=ALU.add, op1=ALU.mult))
            seq.add("vector", lambda v: v.tensor_tensor(
                out=A["xtile"][:, :], in0=A["xtile"][:, :], in1=A["ysb"][:, :], op=ALU.add))
            seq.dma("gpsimd", xo[rt * 128:(rt + 1) * 128, :], A["xtile"][:, :])

    al.close()


def l0ffn_host_prep(inputs):
    return {
        "gup": np.ascontiguousarray(inputs["L0_ffn_gate_up_packed"]).astype(np.int32),
        "gusT": np.ascontiguousarray(np.asarray(inputs["L0_ffn_gate_up_scales"]).T),
        "dnp": np.ascontiguousarray(inputs["L0_ffn_down_packed"]).astype(np.int32),
        "dnsT": np.ascontiguousarray(np.asarray(inputs["L0_ffn_down_scales"]).T),
    }


def run_l0ffn(x, inputs, n_cores=8, dbg=False):
    shared = l0ffn_host_prep(inputs)
    raise NotImplementedError
    in_maps = []
    for c in range(n_cores):
        xsh = np.ascontiguousarray(
            x[:, c * 512:(c + 1) * 512, :].reshape(2048, 2048)).astype(np.float32)
        m = {"xs": xsh}
        m.update(shared)
        in_maps.append(m)
    res = run_bass_kernel_spmd(nc, in_maps, core_ids=list(range(n_cores)))
    out = np.empty((4, 4096, 2048), np.float32)
    for c in range(n_cores):
        out[:, c * 512:(c + 1) * 512, :] = res.results[c]["xo"].reshape(4, 512, 2048)
    if dbg:
        return out, res.results
    return out


def emit_l1attn(nc, seq, xs, xo, pfx="c_", dbg=False):
    TOK, D = 2048, 2048
    SCALE = 1.0 / float(np.sqrt(128.0))

    wqkvT = nc.dram_tensor("wqkvT", [D, 3072], F32, kind="ExternalInput").ap()
    wpT = nc.dram_tensor("wpT", [D, D], F32, kind="ExternalInput").ap()
    if dbg:
        d_qkv = nc.dram_tensor("d_qkv", [128, 4 * 3072], BF16, kind="ExternalOutput").ap()
        d_S = nc.dram_tensor("d_S", [128, 160], F32, kind="ExternalOutput").ap()
        d_E = nc.dram_tensor("d_E", [128, 160], BF16, kind="ExternalOutput").ap()
        d_att = nc.dram_tensor("d_att", [128, 4 * 2048], BF16, kind="ExternalOutput").ap()
        d_mx = nc.dram_tensor("d_mx", [128, 64], F32, kind="ExternalOutput").ap()
        d_Dm = nc.dram_tensor("d_Dm", [128, 160], F32, kind="ExternalOutput").ap()

    al = Sbuf(nc, pfx)
    A = {}
    A["xlnT"] = al.sb("xlnT", [128, 16 * TOK], BF16)
    A["qkv"] = al.sb("qkv", [128, 4 * 3072], BF16)
    A["att_tm"] = al.sb("att_tm", [128, 4 * 2048], BF16)
    A["attnT"] = al.sb("attnT", [128, 16 * 512], BF16)
    A["wq_st"] = al.sb("wq_st", [128, 3072], BF16)
    A["wp_st"] = al.sb("wp_st", [128, 2048], BF16)
    A["prod"] = al.sb("prod", [128, 2048], BF16)
    A["acc"] = al.sb("acc", [128, 2048], F32)
    A["tmp2"] = al.sb("tmp2", [128, 2048], F32)
    A["S"] = al.sb("S", [128, 160], F32)
    A["Dm"] = al.sb("Dm", [128, 160], F32)
    A["E"] = al.sb("E", [128, 160], BF16)
    A["mx"] = al.sb("mx", [128, 64], F32)
    A["den"] = al.sb("den", [128, 64], F32)
    A["rden"] = al.sb("rden", [128, 64], F32)
    A["xtile"] = al.sb("xtile", [128, D], F32)
    A["lnb"] = al.sb("lnb", [128, D], BF16)
    A["ysb"] = al.sb("ysb", [128, D], F32)
    A["ident"] = al.sb("ident", [128, 128], BF16)
    for nm in ["c_eps5", "c_eps6", "c_m05", "s_", "ssq", "mean", "negmean", "t1", "var", "rstd"]:
        A[nm] = al.sb(nm, [128, 1], F32)
    PQ = al.ps("PQ", [128, 3072], F32)
    PT = al.ps("PT", [128, 128], BF16)

    pairs = [(i, j) for i in range(4) for j in range(i + 1)]

    def qkv4(b):
        return A["qkv"][:, b * 3072:(b + 1) * 3072].rearrange(
            "p (g x d) -> p g x d", g=4, x=6)

    emit_consts(seq, A)

    for rt in range(16):
        emit_lnT(seq, A, xs, rt, A["xlnT"], rt * 128, TOK, PT)

    for pt in range(4):
        for i in range(4):
            rt = i * 4 + pt
            for c in range(16):
                seq.dma("gpsimd", A["wq_st"][:, :], wqkvT[c * 128:(c + 1) * 128, :])
                for nn in range(6):
                    seq.add("tensor", lambda t, c=c, rt=rt, nn=nn: t.matmul(
                        PQ[:, nn * 512:(nn + 1) * 512],
                        A["xlnT"][:, c * TOK + rt * 128: c * TOK + (rt + 1) * 128],
                        A["wq_st"][:, nn * 512:(nn + 1) * 512],
                        start=(c == 0), stop=(c == 15)))
            seq.add("vector", lambda v, i=i: v.tensor_copy(
                out=A["qkv"][:, i * 3072:(i + 1) * 3072], in_=PQ[:, :]))

        for pidx, (i, j) in enumerate(pairs):
            qa = qkv4(i)[:, :, 0:4, :]
            ka = qkv4(j)[:, :, 4:5, :].broadcast_to([128, 4, 4, 128])
            seq.add("vector", lambda v, qa=qa, ka=ka: v.tensor_tensor(
                out=A["prod"][:, :].rearrange("p (g x d) -> p g x d", g=4, x=4),
                in0=qa, in1=ka, op=ALU.mult))
            seq.add("vector", lambda v, pidx=pidx: v.tensor_reduce(
                out=A["S"][:, pidx * 16:(pidx + 1) * 16].rearrange("p (g x) -> p g x", g=4),
                in_=A["prod"][:, :].rearrange("p (g x d) -> p g x d", g=4, x=4),
                axis=AX.X, op=ALU.add))
        def scale_S(v):
            v.nop()
            return v.tensor_scalar(
                out=A["S"][:, :], in0=A["S"][:, :], scalar1=SCALE, scalar2=None, op0=ALU.mult)
        seq.add("vector", scale_S)

        def softmax(v):
            pofs = {(i, j): pairs.index((i, j)) * 16 for (i, j) in pairs}
            for i in range(4):
                mslice = A["mx"][:, i * 16:(i + 1) * 16]
                v.tensor_copy(out=mslice, in_=A["S"][:, pofs[(i, 0)]:pofs[(i, 0)] + 16])
                v.nop()
                for j in range(1, i + 1):
                    v.tensor_tensor(out=mslice, in0=mslice,
                                    in1=A["S"][:, pofs[(i, j)]:pofs[(i, j)] + 16],
                                    op=ALU.max)
                    v.nop()
            for pidx, (i, j) in enumerate(pairs):
                v.tensor_tensor(out=A["Dm"][:, pidx * 16:(pidx + 1) * 16],
                                in0=A["S"][:, pidx * 16:(pidx + 1) * 16],
                                in1=A["mx"][:, i * 16:(i + 1) * 16], op=ALU.subtract)
                v.nop()
            return v.tensor_copy(out=A["den"][:, :], in_=A["mx"][:, :])
        seq.add("vector", softmax)
        seq.add("scalar", lambda e: e.activation(
            out=A["E"][:, :], in_=A["Dm"][:, :], func=AF.Exp, bias=0.0, scale=1.0))

        def denom(v):
            pofs = {(i, j): pairs.index((i, j)) * 16 for (i, j) in pairs}
            for i in range(4):
                dslice = A["den"][:, i * 16:(i + 1) * 16]
                v.tensor_copy(out=dslice, in_=A["E"][:, pofs[(i, 0)]:pofs[(i, 0)] + 16])
                v.nop()
                for j in range(1, i + 1):
                    v.tensor_tensor(out=dslice, in0=dslice,
                                    in1=A["E"][:, pofs[(i, j)]:pofs[(i, j)] + 16], op=ALU.add)
                    v.nop()
            v.nop()
            return v.reciprocal(out=A["rden"][:, :], in_=A["den"][:, :])
        seq.add("vector", denom)

        for i in range(4):
            for j in range(i + 1):
                pidx = pairs.index((i, j))
                va = qkv4(j)[:, :, 5:6, :].broadcast_to([128, 4, 4, 128])
                ea = A["E"][:, pidx * 16:(pidx + 1) * 16].rearrange(
                    "p (g x) -> p g x", g=4).unsqueeze(3).broadcast_to([128, 4, 4, 128])
                dst = A["acc"] if j == 0 else A["tmp2"]
                seq.add("vector", lambda v, va=va, ea=ea, dst=dst: v.tensor_tensor(
                    out=dst[:, :].rearrange("p (g x d) -> p g x d", g=4, x=4),
                    in0=va, in1=ea, op=ALU.mult))
                if j > 0:
                    seq.add("vector", lambda v: v.tensor_tensor(
                        out=A["acc"][:, :], in0=A["acc"][:, :], in1=A["tmp2"][:, :], op=ALU.add))
            ra = A["rden"][:, i * 16:(i + 1) * 16].unsqueeze(2).broadcast_to([128, 16, 128])
            seq.add("vector", lambda v, i=i, ra=ra: v.tensor_tensor(
                out=A["att_tm"][:, i * 2048:(i + 1) * 2048].rearrange("p (h d) -> p h d", h=16),
                in0=A["acc"][:, :].rearrange("p (h d) -> p h d", h=16),
                in1=ra, op=ALU.mult))

        if dbg and pt == 3:
            seq.dma("gpsimd", d_qkv, A["qkv"][:, :])
            seq.dma("gpsimd", d_S, A["S"][:, :])
            seq.dma("gpsimd", d_E, A["E"][:, :])
            seq.dma("gpsimd", d_mx, A["mx"][:, :])
            seq.dma("gpsimd", d_Dm, A["Dm"][:, :])
            seq.dma("gpsimd", d_att, A["att_tm"][:, :])
        for i in range(4):
            for c in range(16):
                seq.add("tensor", lambda t, i=i, c=c: t.transpose(
                    PT[:, :], A["att_tm"][:, i * 2048 + c * 128: i * 2048 + (c + 1) * 128],
                    A["ident"][:, :]))
                seq.add("vector", lambda v, i=i, c=c: v.tensor_copy(
                    out=A["attnT"][:, c * 512 + i * 128: c * 512 + (i + 1) * 128],
                    in_=PT[:, :]))

        for i in range(4):
            rt = i * 4 + pt
            for c in range(16):
                seq.dma("gpsimd", A["wp_st"][:, :], wpT[c * 128:(c + 1) * 128, :])
                for nn in range(4):
                    seq.add("tensor", lambda t, i=i, c=c, nn=nn: t.matmul(
                        PQ[:, nn * 512:(nn + 1) * 512],
                        A["attnT"][:, c * 512 + i * 128: c * 512 + (i + 1) * 128],
                        A["wp_st"][:, nn * 512:(nn + 1) * 512],
                        start=(c == 0), stop=(c == 15)))
            seq.add("scalar", lambda e: e.activation(
                out=A["ysb"][:, :], in_=PQ[:, 0:2048], func=AF.Copy, accum_out=A["s_"][:, :]))
            seq.add("scalar", lambda e: e.activation(
                out=A["lnb"][:, :], in_=A["ysb"][:, :], func=AF.Square, accum_out=A["ssq"][:, :]))

            def dve_stats(v):
                v.tensor_scalar(out=A["mean"][:, :], in0=A["s_"][:, :], scalar1=1.0 / D,
                                scalar2=None, op0=ALU.mult)
                v.tensor_scalar(out=A["negmean"][:, :], in0=A["s_"][:, :], scalar1=-1.0 / D,
                                scalar2=None, op0=ALU.mult)
                v.tensor_scalar(out=A["t1"][:, :], in0=A["ssq"][:, :], scalar1=1.0 / D,
                                scalar2=None, op0=ALU.mult)
                return v.scalar_tensor_tensor(out=A["var"][:, :], in0=A["mean"][:, :],
                                              scalar=A["negmean"][:, :], in1=A["t1"][:, :],
                                              op0=ALU.mult, op1=ALU.add)
            seq.add("vector", dve_stats)

            def act_rstd(e):
                e.activation(out=A["var"][:, :], in_=A["var"][:, :], func=AF.Ln,
                             bias=A["c_eps5"][:, :], scale=1.0)
                return e.activation(out=A["rstd"][:, :], in_=A["var"][:, :], func=AF.Exp,
                                    bias=0.0, scale=A["c_m05"][:, :])
            seq.add("scalar", act_rstd)
            seq.dma("sync", A["xtile"][:, :], xs[rt * 128:(rt + 1) * 128, :])
            seq.add("vector", lambda v: v.tensor_scalar(
                out=A["ysb"][:, :], in0=A["ysb"][:, :], scalar1=A["negmean"][:, :],
                scalar2=A["rstd"][:, :], op0=ALU.add, op1=ALU.mult))
            seq.add("vector", lambda v: v.tensor_tensor(
                out=A["xtile"][:, :], in0=A["xtile"][:, :], in1=A["ysb"][:, :], op=ALU.add))
            seq.dma("gpsimd", xo[rt * 128:(rt + 1) * 128, :], A["xtile"][:, :])

    al.close()


def run_l1attn(x, inputs, n_cores=8, dbg=False):
    shared = {
        "wqkvT": np.ascontiguousarray(np.asarray(inputs["L1_attn_weight_qkv"], np.float32).T),
        "wpT": np.ascontiguousarray(np.asarray(inputs["L1_attn_weight_proj"], np.float32).T),
    }
    raise NotImplementedError
    in_maps = []
    for c in range(n_cores):
        xsh = np.ascontiguousarray(
            x[:, c * 512:(c + 1) * 512, :].reshape(2048, 2048)).astype(np.float32)
        m = {"xs": xsh}
        m.update(shared)
        in_maps.append(m)
    res = run_bass_kernel_spmd(nc, in_maps, core_ids=list(range(n_cores)))
    out = np.empty((4, 4096, 2048), np.float32)
    for c in range(n_cores):
        out[:, c * 512:(c + 1) * 512, :] = res.results[c]["xo"].reshape(4, 512, 2048)
    if dbg:
        return out, res.results
    return out


def emit_l0attn(nc, seq, xs, xo, pfx="a_", dbg=False):
    TOK, D = 2048, 2048
    SCALE = 0.08838834764831845
    NEG = -1e30

    xh = nc.dram_tensor("xh", [TOK, D], F32, kind="ExternalInput").ap()
    wqc = nc.dram_tensor("wqc", [D, 512], F32, kind="ExternalInput").ap()
    wqu = nc.dram_tensor("wqu", [512, D], F32, kind="ExternalInput").ap()
    wk = nc.dram_tensor("wk", [D, 128], F32, kind="ExternalInput").ap()
    wv = nc.dram_tensor("wv", [D, 128], F32, kind="ExternalInput").ap()
    w1 = nc.dram_tensor("w1", [4, 512, 2048], F32, kind="ExternalInput").ap()
    w2 = nc.dram_tensor("w2", [8192, 2048], F32, kind="ExternalInput").ap()
    qnw = nc.dram_tensor("qnw", [128, 128], F32, kind="ExternalInput").ap()
    knw = nc.dram_tensor("knw", [128, 128], F32, kind="ExternalInput").ap()
    sinkb = nc.dram_tensor("sinkb", [128, 16], F32, kind="ExternalInput").ap()
    mask = nc.dram_tensor("mask", [4, 128, 640], F32, kind="ExternalInput").ap()
    cos_o = nc.dram_tensor("cos_o", [512, 32], F32, kind="ExternalInput").ap()
    sin_o = nc.dram_tensor("sin_o", [512, 32], F32, kind="ExternalInput").ap()
    cos_h = nc.dram_tensor("cos_h", [512, 32], F32, kind="ExternalInput").ap()
    sin_h = nc.dram_tensor("sin_h", [512, 32], F32, kind="ExternalInput").ap()
    if dbg:
        dbgt = {}
        for nm_, sh in [("kT", [128, 1024]), ("vsb", [128, 1024]), ("qT", [128, 8192]),
                        ("attnT", [128, 8192]), ("out1T", [128, 8192]), ("xlnT", [128, 8192]),
                        ("E", [128, 640]), ("smb0", [128, 640])]:
            dbgt[nm_] = nc.dram_tensor("d_" + nm_, sh,
                                       F32 if nm_ in ("smb0",) else BF16, kind="ExternalOutput").ap()
        d_y2 = nc.dram_tensor("d_y2", [128, 8192], BF16, kind="ExternalOutput").ap()
        d_qbuf = nc.dram_tensor("d_qbuf", [128, 2048], F32, kind="ExternalOutput").ap()
        d_qss = nc.dram_tensor("d_qss", [128, 16], F32, kind="ExternalOutput").ap()
        d_qrr = nc.dram_tensor("d_qrr", [128, 16], F32, kind="ExternalOutput").ap()
        d_qn = nc.dram_tensor("d_qn", [128, 2048], F32, kind="ExternalOutput").ap()
        d_kv = nc.dram_tensor("d_kv", [128, 128], F32, kind="ExternalOutput").ap()
        d_rb = nc.dram_tensor("d_rb", [128, 128], F32, kind="ExternalOutput").ap()
        d_s = nc.dram_tensor("d_s", [128, 2], F32, kind="ExternalOutput").ap()

    al = Sbuf(nc, pfx)
    A = {}
    A["mask"] = al.sb("m_mask", [128, 4 * 640], F32)
    A["cos_o"] = al.sb("m_cos_o", [128, 4 * 32], F32)
    A["sin_o"] = al.sb("m_sin_o", [128, 4 * 32], F32)
    A["cos_h"] = al.sb("m_cos_h", [128, 4 * 32], F32)
    A["sin_h"] = al.sb("m_sin_h", [128, 4 * 32], F32)
    A["qnw"] = al.sb("m_qnw", [128, 128], F32)
    A["knw"] = al.sb("m_knw", [128, 128], F32)
    A["sinkb"] = al.sb("m_sinkb", [128, 16], F32)
    A["negsink"] = al.sb("m_negsink", [128, 16], F32)
    A["wqc"] = al.sb("m_wqc", [128, 16 * 512], BF16)
    A["wqu"] = al.sb("m_wqu", [128, 4 * 2048], BF16)
    A["wk"] = al.sb("m_wk", [128, 16 * 128], BF16)
    A["wv"] = al.sb("m_wv", [128, 16 * 128], BF16)
    A["xlnT"] = al.sb("xlnT", [128, 16 * 512], BF16)
    A["lnTt"] = al.sb("lnTt", [128, 16 * 128], BF16)
    A["kT"] = al.sb("kT", [128, 1024], BF16)
    A["vsb"] = al.sb("vsb", [128, 8 * 128], BF16)
    A["kvtmp"] = al.sb("kvtmp", [128, 256], F32)
    A["U"] = al.sb("Ubuf", [128, 2048], F32)
    A["ra"] = A["U"][:, 0:512]
    A["rb"] = A["U"][:, 512:1024]
    A["rc"] = A["U"][:, 1024:1536]
    A["rd"] = A["U"][:, 1536:2048]
    A["qcT"] = al.sb("qcT", [128, 4 * 512], BF16)
    A["qbuf"] = al.sb("qbuf", [128, 2048], F32)
    A["qss"] = al.sb("qss", [128, 16], F32)
    A["qrr"] = al.sb("qrr", [128, 16], F32)
    A["qbf"] = al.sb("qbf", [128, 2048], BF16)
    A["qT"] = al.sb("qT", [128, 16 * 512], BF16)
    A["smb"] = al.sb("smb", [128, 640], F32)
    A["E"] = al.sb("E", [128, 640], BF16)
    A["eT"] = al.sb("eT", [128, 640], BF16)
    A["attnT"] = al.sb("attnT", [128, 16 * 512], BF16)
    A["w1g"] = al.sb("w1g", [128, 4 * 2048], BF16)
    A["out1T"] = al.sb("out1T", [128, 16 * 512], BF16)
    A["w2gh"] = al.sb("w2gh", [128, 16 * 512], BF16)
    A["y2acc"] = al.sb("y2acc", [128, 4 * 2048], BF16)
    A["xtile"] = A["qbuf"]
    A["lnb"] = al.sb("lnb", [128, D], BF16)
    A["ysb"] = A["U"]
    A["qsq"] = A["U"]
    A["ident"] = al.sb("ident", [128, 128], BF16)
    A["_stats"] = al.sb("_stats", [128, 16], F32)
    for _i, nm_ in enumerate(["dsum", "es", "nm", "nm2", "rr", "c_eps5", "c_eps6", "c_m05",
                              "s_", "ssq", "mean", "negmean", "t1", "var", "rstd"]):
        A[nm_] = A["_stats"][:, _i:_i + 1]
    PA = al.ps("PA", [128, 2048], F32)
    PB = al.ps("PB", [128, 1536], F32)
    PT = al.ps("PT2", [128, 128], BF16)

    emit_consts(seq, A)

    seq.dma("sync", A["mask"][:, :].rearrange("p (q w) -> p q w", q=4), mask.transpose([1, 0, 2]))
    for (dst, src) in (("cos_o", cos_o), ("sin_o", sin_o), ("cos_h", cos_h), ("sin_h", sin_h)):
        seq.dma("sync", A[dst][:, :].rearrange("p (q w) -> p q w", q=4),
                src.rearrange("(q p) w -> p q w", p=128))
    seq.dma("sync", A["qnw"][:, :], qnw)
    seq.dma("sync", A["knw"][:, :], knw)
    seq.dma("sync", A["sinkb"][:, :], sinkb)
    seq.add("vector", lambda v: v.tensor_scalar(
        out=A["negsink"][:, :], in0=A["sinkb"][:, :], scalar1=-1.0, scalar2=None, op0=ALU.mult))
    for kc in range(16):
        seq.dma("gpsimd", A["wqc"][:, kc * 512:(kc + 1) * 512], wqc[kc * 128:(kc + 1) * 128, :])
        seq.dma("gpsimd", A["wk"][:, kc * 128:(kc + 1) * 128], wk[kc * 128:(kc + 1) * 128, :])
        seq.dma("gpsimd", A["wv"][:, kc * 128:(kc + 1) * 128], wv[kc * 128:(kc + 1) * 128, :])
    for qc_c in range(4):
        seq.dma("gpsimd", A["wqu"][:, qc_c * 2048:(qc_c + 1) * 2048],
                wqu[qc_c * 128:(qc_c + 1) * 128, :])

    def rope_rms_k(ht, src128, costile, sintile, is_halo):
        k = A["kvtmp"]
        seq.add("vector", lambda v: v.tensor_tensor(out=A["ra"][:, 0:32], in0=k[:, 0:32],
                                                    in1=costile, op=ALU.mult))
        seq.add("vector", lambda v: v.tensor_tensor(out=A["rb"][:, 0:32], in0=k[:, 32:64],
                                                    in1=sintile, op=ALU.mult))
        seq.add("vector", lambda v: v.tensor_tensor(out=A["rc"][:, 0:32], in0=k[:, 0:32],
                                                    in1=sintile, op=ALU.mult))
        seq.addf("vector", lambda v: v.tensor_tensor(out=A["rd"][:, 0:32], in0=k[:, 32:64],
                                                     in1=costile, op=ALU.mult))
        seq.addf("vector", lambda v: v.tensor_tensor(out=k[:, 0:32], in0=A["ra"][:, 0:32],
                                                     in1=A["rb"][:, 0:32], op=ALU.subtract))
        seq.addf("vector", lambda v: v.tensor_tensor(out=k[:, 32:64], in0=A["rc"][:, 0:32],
                                                     in1=A["rd"][:, 0:32], op=ALU.add))
        seq.add("scalar", lambda e: e.activation(out=A["ra"][:, 0:128], in_=k[:, 0:128],
                                                 func=AF.Square, accum_out=A["s_"][:, :]))
        seq.add("vector", lambda v: v.tensor_scalar(out=A["t1"][:, :], in0=A["s_"][:, :],
                                                    scalar1=1.0 / 128, scalar2=None, op0=ALU.mult))

        def act_rr(e):
            e.activation(out=A["var"][:, :], in_=A["t1"][:, :], func=AF.Ln,
                         bias=A["c_eps6"][:, :], scale=1.0)
            return e.activation(out=A["rr"][:, :], in_=A["var"][:, :], func=AF.Exp,
                                bias=0.0, scale=A["c_m05"][:, :])
        seq.add("scalar", act_rr)
        seq.addf("vector", lambda v: v.scalar_tensor_tensor(
            out=A["rb"][:, 0:128], in0=k[:, 0:128], scalar=A["rr"][:, :], in1=A["knw"][:, :],
            op0=ALU.mult, op1=ALU.mult))
        seq.add("vector", lambda v: v.tensor_copy(out=A["lnb"][:, 0:128], in_=A["rb"][:, 0:128]))
        seq.add("tensor", lambda t: t.transpose(PT[:, :], A["lnb"][:, 0:128], A["ident"][:, :]))
        seq.add("vector", lambda v, ht=ht: v.tensor_copy(
            out=A["kT"][:, ht * 128:(ht + 1) * 128], in_=PT[:, :]))

    for b in range(4):
        for tt in range(4):
            rt = b * 4 + tt
            emit_lnT(seq, A, xs, rt, A["xlnT"], tt * 128, 512, PT)
        for ht in range(8):
            if ht < 4:
                rt = b * 4 + ht
                emit_lnT(seq, A, xh, rt, A["lnTt"], 0, 128, PT)
                lnsrc, coff = A["lnTt"], lambda kc: kc * 128
            else:
                tt = ht - 4
                lnsrc, coff = A["xlnT"], lambda kc, tt=tt: kc * 512 + tt * 128
            for kc in range(16):
                seq.add("tensor", lambda t, lnsrc=lnsrc, coff=coff, kc=kc: t.matmul(
                    PA[:, 0:128], lnsrc[:, coff(kc):coff(kc) + 128],
                    A["wk"][:, kc * 128:(kc + 1) * 128], start=(kc == 0), stop=(kc == 15)))
                seq.add("tensor", lambda t, lnsrc=lnsrc, coff=coff, kc=kc: t.matmul(
                    PA[:, 512:640], lnsrc[:, coff(kc):coff(kc) + 128],
                    A["wv"][:, kc * 128:(kc + 1) * 128], start=(kc == 0), stop=(kc == 15)))
            seq.add("scalar", lambda e: e.copy(out=A["kvtmp"][:, 0:128], in_=PA[:, 0:128]))
            seq.add("vector", lambda v, ht=ht: v.tensor_copy(
                out=A["vsb"][:, ht * 128:(ht + 1) * 128], in_=PA[:, 512:640]))
            pt_i = ht % 4
            cs, sn = ("cos_h", "sin_h") if ht < 4 else ("cos_o", "sin_o")
            rope_rms_k(ht, None, A[cs][:, pt_i * 32:(pt_i + 1) * 32],
                       A[sn][:, pt_i * 32:(pt_i + 1) * 32], ht < 4)
            if dbg and b == 0 and ht == 7:
                seq.dma("gpsimd", d_kv, A["kvtmp"][:, 0:128])
                seq.dma("gpsimd", d_rb, A["rb"][:, 0:128])
                seq.add("vector", lambda v: v.tensor_copy(out=A["smb"][:, 0:1], in_=A["s_"][:, :]))
                seq.add("vector", lambda v: v.tensor_copy(out=A["smb"][:, 1:2], in_=A["rr"][:, :]))
                seq.dma("gpsimd", d_s, A["smb"][:, 0:2])

        for mc in range(4):
            for kc in range(16):
                seq.add("tensor", lambda t, mc=mc, kc=kc: t.matmul(
                    PA[:, 0:512],
                    A["wqc"][:, kc * 512 + mc * 128: kc * 512 + (mc + 1) * 128],
                    A["xlnT"][:, kc * 512:(kc + 1) * 512],
                    start=(kc == 0), stop=(kc == 15)))
            seq.add("vector", lambda v, mc=mc: v.tensor_copy(
                out=A["qcT"][:, mc * 512:(mc + 1) * 512], in_=PA[:, 0:512]))
        for tt in range(4):
            for qc_c in range(4):
                for nn in range(4):
                    seq.add("tensor", lambda t, qc_c=qc_c, tt=tt, nn=nn: t.matmul(
                        PA[:, nn * 512:(nn + 1) * 512],
                        A["qcT"][:, qc_c * 512 + tt * 128: qc_c * 512 + (tt + 1) * 128],
                        A["wqu"][:, qc_c * 2048 + nn * 512: qc_c * 2048 + (nn + 1) * 512],
                        start=(qc_c == 0), stop=(qc_c == 3)))
            seq.add("scalar", lambda e: e.copy(out=A["qbuf"][:, :], in_=PA[:, :]))
            co = A["cos_o"][:, tt * 32:(tt + 1) * 32].unsqueeze(1).broadcast_to([128, 16, 32])
            si = A["sin_o"][:, tt * 32:(tt + 1) * 32].unsqueeze(1).broadcast_to([128, 16, 32])
            q3 = A["qbuf"][:, :].rearrange("p (h d) -> p h d", h=16)
            for (dst, xsl, tr) in (("ra", (0, 32), co), ("rb", (32, 64), si),
                                   ("rc", (0, 32), si), ("rd", (32, 64), co)):
                seq.addf("vector", lambda v, dst=dst, xsl=xsl, tr=tr, q3=q3: v.tensor_tensor(
                    out=A[dst][:, 0:512].rearrange("p (h d) -> p h d", h=16),
                    in0=q3[:, :, xsl[0]:xsl[1]], in1=tr, op=ALU.mult))
            seq.addf("vector", lambda v, q3=q3: v.tensor_tensor(
                out=q3[:, :, 0:32],
                in0=A["ra"][:, 0:512].rearrange("p (h d) -> p h d", h=16),
                in1=A["rb"][:, 0:512].rearrange("p (h d) -> p h d", h=16), op=ALU.subtract))
            seq.addf("vector", lambda v, q3=q3: v.tensor_tensor(
                out=q3[:, :, 32:64],
                in0=A["rc"][:, 0:512].rearrange("p (h d) -> p h d", h=16),
                in1=A["rd"][:, 0:512].rearrange("p (h d) -> p h d", h=16), op=ALU.add))
            seq.add("scalar", lambda e: e.nop())
            seq.addf("vector", lambda v: v.tensor_tensor(
                out=A["qsq"][:, :], in0=A["qbuf"][:, :], in1=A["qbuf"][:, :], op=ALU.mult))
            seq.add("scalar", lambda e: e.nop())
            seq.addf("vector", lambda v, q3=q3: v.tensor_reduce(
                out=A["qss"][:, :].rearrange("p (h x) -> p h x", x=1),
                in_=A["qsq"][:, :].rearrange("p (h d) -> p h d", h=16),
                axis=AX.X, op=ALU.add))
            seq.add("vector", lambda v: v.nop())
            seq.addf("vector", lambda v: v.tensor_scalar(
                out=A["qss"][:, :], in0=A["qss"][:, :], scalar1=1.0 / 128, scalar2=None,
                op0=ALU.mult))

            def act_qrr(e):
                e.activation(out=A["qss"][:, :], in_=A["qss"][:, :], func=AF.Ln,
                             bias=A["c_eps6"][:, :], scale=1.0)
                e.nop()
                return e.activation(out=A["qrr"][:, :], in_=A["qss"][:, :], func=AF.Exp,
                                    bias=0.0, scale=A["c_m05"][:, :])
            seq.add("scalar", act_qrr)
            rr3 = A["qrr"][:, :].unsqueeze(2).broadcast_to([128, 16, 128])
            qn3 = A["qnw"][:, :].unsqueeze(1).broadcast_to([128, 16, 128])
            seq.addf("vector", lambda v, q3=q3, rr3=rr3: v.tensor_tensor(
                out=A["qsq"][:, :].rearrange("p (h d) -> p h d", h=16),
                in0=q3, in1=rr3, op=ALU.mult))
            seq.addf("vector", lambda v, qn3=qn3: v.tensor_tensor(
                out=A["qbf"][:, :].rearrange("p (h d) -> p h d", h=16),
                in0=A["qsq"][:, :].rearrange("p (h d) -> p h d", h=16),
                in1=qn3, op=ALU.mult))
            if dbg and b == 0 and tt == 0:
                seq.dma("gpsimd", d_qbuf, A["qbuf"][:, :])
                seq.dma("gpsimd", d_qss, A["qss"][:, :])
                seq.dma("gpsimd", d_qrr, A["qrr"][:, :])
                seq.dma("gpsimd", d_qn, A["qsq"][:, :])
            for h in range(16):
                seq.add("tensor", lambda t, h=h: t.transpose(
                    PT[:, :], A["qbf"][:, h * 128:(h + 1) * 128], A["ident"][:, :]))
                seq.add("vector", lambda v, h=h, tt=tt: v.tensor_copy(
                    out=A["qT"][:, h * 512 + tt * 128: h * 512 + (tt + 1) * 128], in_=PT[:, :]))

        for h in range(16):
            for qt in range(4):
                seq.add("tensor", lambda t, h=h, qt=qt: t.matmul(
                    PA[:, 0:512], A["qT"][:, h * 512 + qt * 128: h * 512 + (qt + 1) * 128],
                    A["kT"][:, qt * 128: qt * 128 + 512], start=True, stop=True))
                seq.add("tensor", lambda t, h=h, qt=qt: t.matmul(
                    PA[:, 512:640], A["qT"][:, h * 512 + qt * 128: h * 512 + (qt + 1) * 128],
                    A["kT"][:, qt * 128 + 512: qt * 128 + 640], start=True, stop=True))
                seq.add("vector", lambda v, qt=qt: v.scalar_tensor_tensor(
                    out=A["smb"][:, :], in0=PA[:, 0:640], scalar=SCALE,
                    in1=A["mask"][:, qt * 640:(qt + 1) * 640], op0=ALU.mult, op1=ALU.add))
                seq.add("scalar", lambda e: e.nop())
                seq.add("vector", lambda v: v.reduce_max(
                    out=A["nm"][:, :], in_=A["smb"][:, :], axis=AX.X, negate=True))
                seq.addf("vector", lambda v, h=h: v.tensor_tensor(
                    out=A["nm2"][:, :], in0=A["nm"][:, :], in1=A["negsink"][:, h:h + 1],
                    op=ALU.min))
                seq.add("scalar", lambda e: e.activation(
                    out=A["E"][:, :], in_=A["smb"][:, :], func=AF.Exp, bias=A["nm2"][:, :],
                    scale=1.0, accum_out=A["dsum"][:, :]))
                seq.add("scalar", lambda e, h=h: e.activation(
                    out=A["es"][:, :], in_=A["sinkb"][:, h:h + 1], func=AF.Exp,
                    bias=A["nm2"][:, :], scale=1.0))
                seq.addf("vector", lambda v: v.tensor_tensor(
                    out=A["dsum"][:, :], in0=A["dsum"][:, :], in1=A["es"][:, :], op=ALU.add))
                seq.addf("vector", lambda v: v.reciprocal(out=A["rr"][:, :], in_=A["dsum"][:, :]))
                seq.addf("vector", lambda v: v.tensor_scalar(
                    out=A["E"][:, :], in0=A["E"][:, :], scalar1=A["rr"][:, :], scalar2=None,
                    op0=ALU.mult))
                for sc_ in range(5):
                    seq.add("tensor", lambda t, sc_=sc_: t.transpose(
                        PT[:, :], A["E"][:, sc_ * 128:(sc_ + 1) * 128], A["ident"][:, :]))
                    seq.add("vector", lambda v, sc_=sc_: v.tensor_copy(
                        out=A["eT"][:, sc_ * 128:(sc_ + 1) * 128], in_=PT[:, :]))
                for sc_ in range(5):
                    seq.add("tensor", lambda t, sc_=sc_, qt=qt: t.matmul(
                        PA[:, 1024:1152],
                        A["vsb"][:, (qt + sc_) * 128: (qt + sc_ + 1) * 128],
                        A["eT"][:, sc_ * 128:(sc_ + 1) * 128],
                        start=(sc_ == 0), stop=(sc_ == 4)))
                seq.add("vector", lambda v, h=h, qt=qt: v.tensor_copy(
                    out=A["attnT"][:, h * 512 + qt * 128: h * 512 + (qt + 1) * 128],
                    in_=PA[:, 1024:1152]))

        if dbg and b == 0:
            seq.dma("gpsimd", dbgt["kT"], A["kT"][:, :])
            seq.dma("gpsimd", dbgt["vsb"], A["vsb"][:, :])
            seq.dma("gpsimd", dbgt["qT"], A["qT"][:, :])
            seq.dma("gpsimd", dbgt["attnT"], A["attnT"][:, :])
            seq.dma("gpsimd", dbgt["xlnT"], A["xlnT"][:, :])
            seq.dma("gpsimd", dbgt["E"], A["E"][:, :])
            seq.dma("gpsimd", dbgt["smb0"], A["smb"][:, :])
        for g in range(4):
            for ic in range(4):
                seq.dma("gpsimd", A["w1g"][:, ic * 2048:(ic + 1) * 2048],
                        w1[g, ic * 128:(ic + 1) * 128, :])
            for oc in range(16):
                for ic in range(4):
                    seq.add("tensor", lambda t, oc=oc, ic=ic, g=g: t.matmul(
                        PB[:, 0:512],
                        A["w1g"][:, ic * 2048 + oc * 128: ic * 2048 + (oc + 1) * 128],
                        A["attnT"][:, (4 * g + ic) * 512:(4 * g + ic + 1) * 512],
                        start=(ic == 0), stop=(ic == 3)))
                seq.add("vector", lambda v, oc=oc: v.tensor_copy(
                    out=A["out1T"][:, oc * 512:(oc + 1) * 512], in_=PB[:, 0:512]))
            for quar in range(4):
                for gic in range(16):
                    seq.dma("gpsimd", A["w2gh"][:, gic * 512:(gic + 1) * 512],
                            w2[g * 2048 + gic * 128: g * 2048 + (gic + 1) * 128,
                               quar * 512:(quar + 1) * 512])
                for tt in range(4):
                    for gic in range(16):
                        seq.add("tensor", lambda t, gic=gic, tt=tt: t.matmul(
                            PB[:, 512:1024],
                            A["out1T"][:, gic * 512 + tt * 128: gic * 512 + (tt + 1) * 128],
                            A["w2gh"][:, gic * 512:(gic + 1) * 512],
                            start=(gic == 0), stop=(gic == 15)))
                    if g == 0:
                        seq.add("vector", lambda v, tt=tt, quar=quar: v.tensor_copy(
                            out=A["y2acc"][:, tt * 2048 + quar * 512: tt * 2048 + (quar + 1) * 512],
                            in_=PB[:, 512:1024]))
                    else:
                        seq.add("vector", lambda v, tt=tt, quar=quar: v.tensor_tensor(
                            out=A["y2acc"][:, tt * 2048 + quar * 512: tt * 2048 + (quar + 1) * 512],
                            in0=A["y2acc"][:, tt * 2048 + quar * 512: tt * 2048 + (quar + 1) * 512],
                            in1=PB[:, 512:1024], op=ALU.add))

        if dbg and b == 0:
            seq.dma("gpsimd", dbgt["out1T"], A["out1T"][:, :])
            seq.dma("gpsimd", d_y2, A["y2acc"][:, :])
        for tt in range(4):
            rt = b * 4 + tt
            emit_ln_stats(seq, A, A["y2acc"][:, tt * 2048:(tt + 1) * 2048], A["lnb"])
            seq.dma("sync", A["xtile"][:, :], xs[rt * 128:(rt + 1) * 128, :])
            seq.addf("vector", lambda v, tt=tt: v.tensor_scalar(
                out=A["ysb"][:, :], in0=A["y2acc"][:, tt * 2048:(tt + 1) * 2048],
                scalar1=A["negmean"][:, :], scalar2=A["rstd"][:, :],
                op0=ALU.add, op1=ALU.mult))
            seq.add("vector", lambda v: v.tensor_tensor(
                out=A["xtile"][:, :], in0=A["xtile"][:, :], in1=A["ysb"][:, :], op=ALU.add))
            seq.dma("gpsimd", xo[rt * 128:(rt + 1) * 128, :], A["xtile"][:, :])

    al.close()


def l0attn_host_prep(inputs):
    shared = {
        "wqc": np.ascontiguousarray(np.asarray(inputs["L0_attn_wq_comp"], np.float32)),
        "wqu": np.ascontiguousarray(np.asarray(inputs["L0_attn_wq_up"], np.float32)),
        "wk": np.ascontiguousarray(np.asarray(inputs["L0_attn_wk"], np.float32)),
        "wv": np.ascontiguousarray(np.asarray(inputs["L0_attn_wv"], np.float32)),
        "w1": np.ascontiguousarray(np.asarray(inputs["L0_attn_out_w1"], np.float32)),
        "w2": np.ascontiguousarray(np.asarray(inputs["L0_attn_out_w2"], np.float32)),
        "qnw": np.ascontiguousarray(np.broadcast_to(
            np.asarray(inputs["L0_attn_q_norm_w"], np.float32)[None, :], (128, 128))),
        "knw": np.ascontiguousarray(np.broadcast_to(
            np.asarray(inputs["L0_attn_k_norm_w"], np.float32)[None, :], (128, 128))),
        "sinkb": np.ascontiguousarray(np.broadcast_to(
            np.asarray(inputs["L0_attn_sink_logit"], np.float32)[None, :], (128, 16))),
    }
    inv = 1.0 / (10000.0 ** (np.arange(0, 64, 2, dtype=np.float32) / 64.0))
    def rope(p0):
        pos = (p0 + np.arange(512, dtype=np.float32))[:, None]
        ang = pos * inv[None, :]
        return np.cos(ang).astype(np.float32), np.sin(ang).astype(np.float32)
    i_ = np.arange(128)[:, None]
    jj = np.arange(640)[None, :]
    base_valid = (jj > i_) & (jj <= i_ + 512)
    per_core = []
    for c in range(8):
        m = np.empty((4, 128, 640), np.float32)
        for qt in range(4):
            valid = base_valid.copy()
            if c == 0:
                valid = valid & (jj >= 512 - qt * 128)
            m[qt] = np.where(valid, 0.0, -1e30)
        cos_o, sin_o = rope(512.0 * c)
        cos_h, sin_h = rope(512.0 * (c - 1))
        per_core.append({"mask": m, "cos_o": cos_o, "sin_o": sin_o,
                         "cos_h": cos_h, "sin_h": sin_h})
    return shared, per_core


def run_l0attn(x, inputs, n_cores=8, dbg=False):
    shared, per_core = l0attn_host_prep(inputs)
    raise NotImplementedError
    in_maps = []
    for c in range(n_cores):
        xsh = np.ascontiguousarray(
            x[:, c * 512:(c + 1) * 512, :].reshape(2048, 2048)).astype(np.float32)
        if c == 0:
            xhh = np.zeros((2048, 2048), np.float32)
        else:
            xhh = np.ascontiguousarray(
                x[:, (c - 1) * 512:c * 512, :].reshape(2048, 2048)).astype(np.float32)
        m = {"xs": xsh, "xh": xhh}
        m.update(shared)
        m.update(per_core[c])
        in_maps.append(m)
    res = run_bass_kernel_spmd(nc, in_maps, core_ids=list(range(n_cores)))
    out = np.empty((4, 4096, 2048), np.float32)
    for c in range(n_cores):
        out[:, c * 512:(c + 1) * 512, :] = res.results[c]["xo"].reshape(4, 512, 2048)
    if dbg:
        return out, res.results
    return out


def build_fused():
    nc = bass.Bass("TRN2", target_bir_lowering=False)
    seq = Seq(nc)
    xs0 = nc.dram_tensor("xs", [2048, 2048], F32, kind="ExternalInput").ap()
    x1d = nc.dram_tensor("x1d", [2048, 2048], F32).ap()
    x2d = nc.dram_tensor("x2d", [2048, 2048], F32).ap()
    x3d = nc.dram_tensor("x3d", [2048, 2048], F32).ap()
    xo = nc.dram_tensor("xo", [2048, 2048], F32, kind="ExternalOutput").ap()
    emit_l0attn(nc, seq, xs0, x1d, pfx="a_")
    emit_l0ffn(nc, seq, x1d, x2d, pfx="b_")
    emit_l1attn(nc, seq, x2d, x3d, pfx="c_")
    emit_l1ffn(nc, seq, x3d, xo, pfx="d_")
    seq.build()
    return nc


def kernel(**inputs):
    x = np.ascontiguousarray(np.asarray(inputs["L0_attn_h"], np.float32))
    n_cores = 8
    shared_a, per_core_a = l0attn_host_prep(inputs)
    shared_b = l0ffn_host_prep(inputs)
    shared_c = {
        "wqkvT": np.ascontiguousarray(np.asarray(inputs["L1_attn_weight_qkv"], np.float32).T),
        "wpT": np.ascontiguousarray(np.asarray(inputs["L1_attn_weight_proj"], np.float32).T),
    }
    shared_d = l1ffn_host_prep(inputs)
    nc = build_fused()
    in_maps = []
    for c in range(n_cores):
        xsh = np.ascontiguousarray(
            x[:, c * 512:(c + 1) * 512, :].reshape(2048, 2048)).astype(np.float32)
        if c == 0:
            xhh = np.zeros((2048, 2048), np.float32)
        else:
            xhh = np.ascontiguousarray(
                x[:, (c - 1) * 512:c * 512, :].reshape(2048, 2048)).astype(np.float32)
        m = {"xs": xsh, "xh": xhh}
        m.update(shared_a)
        m.update(per_core_a[c])
        m.update(shared_b)
        m.update(shared_c)
        m.update(shared_d)
        in_maps.append(m)
    res = run_bass_kernel_spmd(nc, in_maps, core_ids=list(range(n_cores)))
    out = np.empty((4, 4096, 2048), np.float32)
    for c in range(n_cores):
        out[:, c * 512:(c + 1) * 512, :] = res.results[c]["xo"].reshape(4, 512, 2048)
    return out


_NC_CACHE = {}


def _get_nc(name, builder):
    if name not in _NC_CACHE:
        _NC_CACHE[name] = builder()
    return _NC_CACHE[name]


def kernel_multilaunch(**inputs):
    x = np.ascontiguousarray(np.asarray(inputs["L0_attn_h"], np.float32))
    x = run_l0attn(x, inputs)
    x = run_l0ffn(x, inputs)
    x = run_l1attn(x, inputs)
    x = run_l1ffn(x, inputs)
    return np.ascontiguousarray(x.astype(np.float32))


# version 06
# speedup vs baseline: 1.0630x; 1.0630x over previous
import numpy as np
import concourse.bass as bass
import concourse.mybir as mybir
from concourse.bass_utils import run_bass_kernel_spmd

F32 = mybir.dt.float32
BF16 = mybir.dt.bfloat16
I32 = mybir.dt.int32
F16 = mybir.dt.float16
AF = mybir.ActivationFunctionType
ALU = mybir.AluOpType
AX = mybir.AxisListType

ENGINES = ("sync", "gpsimd", "tensor", "vector", "scalar")


class Seq:

    def __init__(self, nc):
        self.nc = nc
        self._dma_counts = {}
        self.items = []

    def add(self, eng, fn, dma=False):
        self.items.append((eng, fn, dma))

    def dma(self, eng, out, in_, **kw):
        self.add(eng, lambda e: e.dma_start(out=out, in_=in_, **kw), dma=True)

    def addf(self, eng, fn):
        self.add(eng, fn)
        self.add("scalar" if eng != "scalar" else "vector", lambda e: e.nop())

    def build(self, extra_engines=()):
        nc = self.nc
        n = len(self.items)
        incs = [False] * n
        for i, (eng, fn, dma) in enumerate(self.items):
            if dma:
                incs[i] = True
            elif i == n - 1 or self.items[i + 1][0] != eng:
                incs[i] = True
        sems = {}
        sem_ctx = []
        for e in ENGINES:
            g = nc.semaphore(f"sq_{e}")
            sems[e] = g.__enter__()
            sem_ctx.append(g)
        counts = {e: 0 for e in ENGINES}
        last_compute = None
        waited = {e: 0 for e in ENGINES}
        import os
        relax = os.environ.get("SEQ_RELAX", "0") == "1"
        plan = []
        for i, (eng, fn, dma) in enumerate(self.items):
            waits = []
            if i > 0:
                peng, pfn, pdma = self.items[i - 1]
                if not relax:
                    if peng != eng or pdma:
                        waits.append((sems[peng], counts[peng]))
                elif dma:
                    if pdma and peng == eng:
                        pass
                    else:
                        waits.append((sems[peng], counts[peng]))
                else:
                    if pdma:
                        for e2 in ("sync", "gpsimd", "scalar", "vector", "tensor"):
                            c2 = self._dma_counts.get(e2, 0)
                            if c2 > waited.get(("c", eng, e2), 0):
                                waits.append((sems[e2], counts[e2]))
                                waited[("c", eng, e2)] = c2
                        if last_compute is not None and last_compute[0] != eng:
                            waits.append((sems[last_compute[0]], last_compute[1]))
                    elif peng != eng:
                        waits.append((sems[peng], counts[peng]))
            inc = None
            if incs[i]:
                amt = 16 if dma else 1
                counts[eng] += amt
                inc = (sems[eng], amt)
                if dma:
                    self._dma_counts[eng] = self._dma_counts.get(eng, 0) + 16
            if not dma:
                last_compute = (eng, counts[eng])
            plan.append((waits, inc))

        per_eng = {e: [] for e in ENGINES}
        for i, (eng, fn, dma) in enumerate(self.items):
            per_eng[eng].append((self.items[i], plan[i]))

        waiter = "gpsimd"
        final_waits = [(sems[e], counts[e]) for e in ENGINES
                       if counts[e] > 0 and e != waiter]
        final_self = counts[waiter]

        with nc.Block() as block:
            for e in ENGINES:
                entries = per_eng[e]
                if not entries and e != waiter:
                    continue

                def make(entries, is_waiter):
                    def f(engobj):
                        for (eng, fn, dma), (waits, inc) in entries:
                            for (sem, val) in waits:
                                if val > 0:
                                    engobj.wait_ge(sem, val)
                            inst = fn(engobj)
                            if inc is not None:
                                inst.then_inc(inc[0], inc[1])
                        if is_waiter:
                            for (sem, val) in final_waits:
                                engobj.wait_ge(sem, val)
                            if final_self > 0:
                                engobj.wait_ge(sems[waiter], final_self)
                    return f

                getattr(block, e)(make(entries, e == waiter))
        for g in sem_ctx:
            g.__exit__(None, None, None)
        return self


class Sbuf:

    def __init__(self, nc, pfx=""):
        self.nc = nc
        self.pfx = pfx
        self.ctx = []

    def sb(self, name, shape, dt):
        g = self.nc.sbuf_tensor(self.pfx + name, shape, dt)
        t = g.__enter__()
        self.ctx.append(g)
        return t

    def ps(self, name, shape, dt=F32):
        g = self.nc.psum_tensor(self.pfx + name, shape, dt)
        t = g.__enter__()
        self.ctx.append(g)
        return t

    def close(self):
        for g in reversed(self.ctx):
            g.__exit__(None, None, None)


def emit_transposes(seq, PT, pairs, batch=8):
    for base in range(0, len(pairs), batch):
        grp = pairs[base:base + batch]

        def tf(t, grp=grp, ident=_IDENT[0]):
            r = None
            for s, (src, dst) in enumerate(grp):
                r = t.transpose(PT[:, s * 128:(s + 1) * 128], src, ident)
            return r
        seq.add("tensor", tf)

        def vf(v, grp=grp):
            r = None
            for s, (src, dst) in enumerate(grp):
                r = v.tensor_copy(out=dst, in_=PT[:, s * 128:(s + 1) * 128])
            return r
        seq.add("vector", vf)


_IDENT = [None]


def emit_consts(seq, A):
    def f(v):
        v.memset(A["c_eps5"][:, :], 1e-5)
        v.memset(A["c_eps6"][:, :], 1e-6)
        return v.memset(A["c_m05"][:, :], -0.5)
    seq.add("vector", f)

    seq.add("gpsimd", lambda g: g.memset(A["ident"][:, :], 1.0))
    seq.add("gpsimd", lambda g: g.affine_select(
        out=A["ident"][:, :], in_=A["ident"][:, :],
        pattern=[[1, 128]], compare_op=ALU.is_equal, fill=0.0,
        base=0, channel_multiplier=-1))
    _IDENT[0] = A["ident"][:, :]


def emit_ln_stats(seq, A, src_ap, scratch, pfx=""):
    D = src_ap.shape[-1]
    s_, ssq = A["s_"], A["ssq"]
    mean, negmean, t1, var, rstd = A["mean"], A["negmean"], A["t1"], A["var"], A["rstd"]

    seq.add("scalar", lambda e: e.activation(out=scratch[:, 0:D], in_=src_ap, func=AF.Copy,
                                             accum_out=s_[:, :]))
    seq.add("scalar", lambda e: e.activation(out=scratch[:, 0:D], in_=src_ap, func=AF.Square,
                                             accum_out=ssq[:, :]))

    def dve(v):
        v.tensor_scalar(out=mean[:, :], in0=s_[:, :], scalar1=1.0 / D, scalar2=None, op0=ALU.mult)
        v.tensor_scalar(out=negmean[:, :], in0=s_[:, :], scalar1=-1.0 / D, scalar2=None, op0=ALU.mult)
        v.tensor_scalar(out=t1[:, :], in0=ssq[:, :], scalar1=1.0 / D, scalar2=None, op0=ALU.mult)
        return v.scalar_tensor_tensor(out=var[:, :], in0=mean[:, :], scalar=negmean[:, :],
                                      in1=t1[:, :], op0=ALU.mult, op1=ALU.add)
    seq.add("vector", dve)

    def act2(e):
        e.activation(out=var[:, :], in_=var[:, :], func=AF.Ln, bias=A["c_eps5"][:, :], scale=1.0)
        return e.activation(out=rstd[:, :], in_=var[:, :], func=AF.Exp, bias=0.0,
                            scale=A["c_m05"][:, :])
    seq.add("scalar", act2)


def emit_l1ffn(nc, seq, xs, xo, pfx="d_", dbg=False):
    TOK = 2048
    D = 2048
    HID = 5632
    HC = HID // 128

    wgT = nc.dram_tensor("wgT", [D, HID], I32, kind="ExternalInput").ap()
    wuT = nc.dram_tensor("wuT", [D, HID], I32, kind="ExternalInput").ap()
    wdT = nc.dram_tensor("wdT", [HID, D], I32, kind="ExternalInput").ap()
    sg = nc.dram_tensor("sg", [128, HC], F32, kind="ExternalInput").ap()
    su = nc.dram_tensor("su", [128, HC], F32, kind="ExternalInput").ap()
    sdbc = nc.dram_tensor("sdbc", [128, D], F32, kind="ExternalInput").ap()
    if dbg:
        d_xlnT = nc.dram_tensor("d_xlnT", [128, 16 * TOK], BF16, kind="ExternalOutput").ap()
        d_hT = nc.dram_tensor("d_hT", [128, 44 * 1024], BF16, kind="ExternalOutput").ap()
        d_y = nc.dram_tensor("d_y", [128, D], F32, kind="ExternalOutput").ap()
        d_wg = nc.dram_tensor("d_wg", [128, 2048], BF16, kind="ExternalOutput").ap()
        d_gs = nc.dram_tensor("d_gs", [128, 512], BF16, kind="ExternalOutput").ap()
        d_up = nc.dram_tensor("d_up", [128, 512], BF16, kind="ExternalOutput").ap()
        d_pg = nc.dram_tensor("d_pg", [128, 1024], F32, kind="ExternalOutput").ap()

    al = Sbuf(nc, pfx)
    A = {}
    A["xlnT"] = al.sb("xlnT", [128, 16 * TOK], BF16)
    A["hT"] = al.sb("hT", [128, HC * 1024], BF16)
    A["xtile"] = al.sb("xtile", [128, D], F32)
    A["lnb"] = al.sb("lnb", [128, D], BF16)
    A["scratch"] = A["lnb"]
    A["wcol"] = al.sb("wcol", [128, 16 * 128], I32)
    A["wg_bf"] = al.sb("wg_bf", [128, 16 * 128], BF16)
    A["wu_bf"] = al.sb("wu_bf", [128, 16 * 128], BF16)
    A["wd_i"] = A["wcol"]
    A["wd_bf"] = al.sb("wd_bf", [128, D], BF16)
    A["sg_s"] = al.sb("sg_s", [128, HC], F32)
    A["su_s"] = al.sb("su_s", [128, HC], F32)
    A["sdbc_s"] = al.sb("sdbc_s", [128, D], F32)
    A["gsil"] = al.sb("gsil", [128, 512], BF16)
    A["upb"] = al.sb("upb", [128, 512], BF16)
    A["ysb"] = al.sb("ysb", [128, D], F32)
    A["ident"] = al.sb("ident", [128, 128], BF16)
    A["sprt"] = al.sb("sprt", [128, 4], F32)
    A["ssqp"] = al.sb("ssqp", [128, 4], F32)
    for nm in ["c_eps5", "c_eps6", "c_m05", "s_", "ssq", "mean", "negmean", "t1", "var", "rstd"]:
        A[nm] = al.sb(nm, [128, 1], F32)
    P1 = al.ps("P1", [128, 2048], F32)
    PT = al.ps("PT", [128, 1024], BF16)

    emit_consts(seq, A)
    seq.dma("sync", A["sg_s"][:, :], sg)
    seq.dma("sync", A["su_s"][:, :], su)
    seq.dma("sync", A["sdbc_s"][:, :], sdbc)

    for rt in range(16):
        seq.dma("sync", A["xtile"][:, :], xs[rt * 128:(rt + 1) * 128, :])
        emit_ln_stats(seq, A, A["xtile"][:, :], A["scratch"])
        seq.add("vector", lambda v, A=A: v.tensor_scalar(
            out=A["lnb"][:, :], in0=A["xtile"][:, :], scalar1=A["negmean"][:, :],
            scalar2=A["rstd"][:, :], op0=ALU.add, op1=ALU.mult))
        emit_transposes(seq, PT, [
            (A["lnb"][:, c * 128:(c + 1) * 128],
             A["xlnT"][:, c * TOK + rt * 128: c * TOK + (rt + 1) * 128])
            for c in range(16)])

    for grp in range(2):
        for hc in range(HC):
            for (wT, wbf) in ((wgT, "wg_bf"), (wuT, "wu_bf")):
                for q in range(4):
                    seq.dma("sync",
                            A["wcol"][:, q * 4 * 128: (q + 1) * 4 * 128]
                            .rearrange("p (c h) -> p c h", c=4),
                            wT[q * 512:(q + 1) * 512, hc * 128:(hc + 1) * 128]
                            .rearrange("(c p) h -> p c h", p=128))
                seq.add("vector", lambda v, A=A, wbf=wbf: v.tensor_scalar(
                    out=A[wbf][:, :], in0=A["wcol"][:, :], scalar1=128.0, scalar2=None,
                    op0=ALU.subtract))
            for (wbf, off) in (("wg_bf", 0), ("wu_bf", 1024)):
                for kc in range(16):
                    for tg in range(2):
                        seq.add("tensor", lambda t, A=A, wbf=wbf, kc=kc, tg=tg, off=off, grp=grp: t.matmul(
                            P1[:, off + tg * 512: off + (tg + 1) * 512],
                            A[wbf][:, kc * 128:(kc + 1) * 128],
                            A["xlnT"][:, kc * TOK + grp * 1024 + tg * 512: kc * TOK + grp * 1024 + (tg + 1) * 512],
                            start=(kc == 0), stop=(kc == 15)))
            for tg in range(2):
                seq.add("scalar", lambda e, A=A, tg=tg, hc=hc: e.activation(
                    out=A["gsil"][:, :], in_=P1[:, tg * 512:(tg + 1) * 512], func=AF.Silu,
                    bias=0.0, scale=A["sg_s"][:, hc:hc + 1]))
                seq.add("vector", lambda v, A=A, tg=tg, hc=hc: v.tensor_scalar(
                    out=A["upb"][:, :], in0=P1[:, 1024 + tg * 512: 1024 + (tg + 1) * 512],
                    scalar1=A["su_s"][:, hc:hc + 1], scalar2=None, op0=ALU.mult))
                seq.add("vector", lambda v, A=A, tg=tg, hc=hc: v.tensor_tensor(
                    out=A["hT"][:, hc * 1024 + tg * 512: hc * 1024 + (tg + 1) * 512],
                    in0=A["gsil"][:, :], in1=A["upb"][:, :], op=ALU.mult))

        if dbg and grp == 1:
            seq.dma("gpsimd", d_xlnT, A["xlnT"][:, :])
            seq.dma("gpsimd", d_hT, A["hT"][:, :])
            seq.dma("gpsimd", d_wg, A["wg_bf"][:, :])
            seq.dma("gpsimd", d_gs, A["gsil"][:, :])
            seq.dma("gpsimd", d_up, A["upb"][:, :])
            seq.add("vector", lambda v, A=A: v.tensor_copy(out=A["ysb"][:, 0:1024], in_=P1[:, 0:1024]))
            seq.dma("gpsimd", d_pg, A["ysb"][:, 0:1024])
        for tt_ in range(8):
            rt = grp * 8 + tt_
            for hc in range(HC):
                seq.dma("sync", A["wd_i"][:, :], wdT[hc * 128:(hc + 1) * 128, :])
                seq.add("vector", lambda v, A=A: v.tensor_scalar(
                    out=A["wd_bf"][:, :], in0=A["wd_i"][:, :], scalar1=128.0, scalar2=None,
                    op0=ALU.subtract))
                for nn in range(4):
                    seq.add("tensor", lambda t, A=A, hc=hc, tt_=tt_, nn=nn: t.matmul(
                        P1[:, nn * 512:(nn + 1) * 512],
                        A["hT"][:, hc * 1024 + tt_ * 128: hc * 1024 + (tt_ + 1) * 128],
                        A["wd_bf"][:, nn * 512:(nn + 1) * 512],
                        start=(hc == 0), stop=(hc == HC - 1)))
            for nn in range(4):
                seq.add("vector", lambda v, A=A, nn=nn: v.scalar_tensor_tensor(
                    out=A["ysb"][:, nn * 512:(nn + 1) * 512], in0=P1[:, nn * 512:(nn + 1) * 512],
                    scalar=1.0, in1=A["sdbc_s"][:, nn * 512:(nn + 1) * 512],
                    op0=ALU.mult, op1=ALU.mult))
            if dbg and rt == 15:
                seq.dma("gpsimd", d_y, A["ysb"][:, :])
            emit_ln_stats(seq, A, A["ysb"][:, :], A["scratch"])
            seq.dma("sync", A["xtile"][:, :], xs[rt * 128:(rt + 1) * 128, :])
            seq.add("vector", lambda v, A=A: v.tensor_scalar(
                out=A["ysb"][:, :], in0=A["ysb"][:, :], scalar1=A["negmean"][:, :],
                scalar2=A["rstd"][:, :], op0=ALU.add, op1=ALU.mult))
            seq.add("vector", lambda v, A=A: v.tensor_tensor(
                out=A["xtile"][:, :], in0=A["xtile"][:, :], in1=A["ysb"][:, :], op=ALU.add))
            seq.dma("gpsimd", xo[rt * 128:(rt + 1) * 128, :], A["xtile"][:, :])

    al.close()


def l1ffn_host_prep(inputs):
    wg = inputs["L1_ffn_wq_gate"]
    wu = inputs["L1_ffn_wq_up"]
    wd = inputs["L1_ffn_wq_down"]
    shared = {
        "wgT": np.ascontiguousarray(wg.T).astype(np.int32),
        "wuT": np.ascontiguousarray(wu.T).astype(np.int32),
        "wdT": np.ascontiguousarray(wd.T).astype(np.int32),
        "sg": np.ascontiguousarray(
            np.asarray(inputs["L1_ffn_s_gate"], np.float32).reshape(44, 128).T),
        "su": np.ascontiguousarray(
            np.asarray(inputs["L1_ffn_s_up"], np.float32).reshape(44, 128).T),
        "sdbc": np.ascontiguousarray(np.broadcast_to(
            np.asarray(inputs["L1_ffn_s_down"], np.float32)[None, :], (128, 2048))),
    }
    return shared


def run_l1ffn(x, inputs, n_cores=8, dbg=False):
    shared = l1ffn_host_prep(inputs)
    raise NotImplementedError
    in_maps = []
    for c in range(n_cores):
        xs = np.ascontiguousarray(
            x[:, c * 512:(c + 1) * 512, :].reshape(2048, 2048)).astype(np.float32)
        m = {"xs": xs}
        m.update(shared)
        in_maps.append(m)
    res = run_bass_kernel_spmd(nc, in_maps, core_ids=list(range(n_cores)))
    out = np.empty((4, 4096, 2048), np.float32)
    for c in range(n_cores):
        out[:, c * 512:(c + 1) * 512, :] = res.results[c]["xo"].reshape(4, 512, 2048)
    if dbg:
        return out, res.results
    return out


def emit_lnT(seq, A, xs, rt, dest, dest_col, dest_stride, PT):
    seq.dma("sync", A["xtile"][:, :], xs[rt * 128:(rt + 1) * 128, :])
    emit_ln_stats(seq, A, A["xtile"][:, :], A["lnb"])
    seq.add("vector", lambda v: v.tensor_scalar(
        out=A["lnb"][:, :], in0=A["xtile"][:, :], scalar1=A["negmean"][:, :],
        scalar2=A["rstd"][:, :], op0=ALU.add, op1=ALU.mult))
    emit_transposes(seq, PT, [
        (A["lnb"][:, c * 128:(c + 1) * 128],
         dest[:, c * dest_stride + dest_col: c * dest_stride + dest_col + 128])
        for c in range(16)])


def emit_dequant_fp4(seq, A, pk_ap, scf_ap, ngroups, C, wc_dest):
    for j in range(8):
        seq.add("vector", lambda v, j=j: v.tensor_scalar(
            out=A["dq_i"][:, 0:C], in0=pk_ap, scalar1=4 * j, scalar2=7,
            op0=ALU.logical_shift_right, op1=ALU.bitwise_and))
        seq.add("vector", lambda v: v.tensor_copy(
            out=A["dq_mf"][:, 0:C], in_=A["dq_i"][:, 0:C]))
        seq.add("vector", lambda v, j=j: v.tensor_scalar(
            out=A["dq_i"][:, 0:C], in0=pk_ap, scalar1=4 * j + 3, scalar2=1,
            op0=ALU.logical_shift_right, op1=ALU.bitwise_and))
        seq.add("vector", lambda v: v.tensor_scalar(
            out=A["dq_sgn"][:, 0:C], in0=A["dq_i"][:, 0:C], scalar1=-2.0, scalar2=1.0,
            op0=ALU.mult, op1=ALU.add))
        seq.add("scalar", lambda e: e.activation(
            out=A["dq_r1"][:, 0:C], in_=A["dq_mf"][:, 0:C], func=AF.Relu,
            bias=A["c_m2"][:, :], scale=0.5))
        seq.add("scalar", lambda e: e.activation(
            out=A["dq_r2"][:, 0:C], in_=A["dq_mf"][:, 0:C], func=AF.Relu,
            bias=A["c_m6"][:, :], scale=1.0))
        seq.add("vector", lambda v: v.scalar_tensor_tensor(
            out=A["dq_v"][:, 0:C], in0=A["dq_mf"][:, 0:C], scalar=0.5,
            in1=A["dq_r1"][:, 0:C], op0=ALU.mult, op1=ALU.add))
        seq.add("vector", lambda v: v.tensor_tensor(
            out=A["dq_v"][:, 0:C], in0=A["dq_v"][:, 0:C], in1=A["dq_r2"][:, 0:C],
            op=ALU.add))
        seq.add("vector", lambda v: v.tensor_tensor(
            out=A["dq_v"][:, 0:C], in0=A["dq_v"][:, 0:C], in1=A["dq_sgn"][:, 0:C],
            op=ALU.mult))
        seq.add("vector", lambda v, j=j: v.scalar_tensor_tensor(
            out=wc_dest[:, j::8].rearrange("p (g r) -> p g r", g=ngroups),
            in0=A["dq_v"][:, 0:C].rearrange("p (g r) -> p g r", g=ngroups),
            scalar=1.0,
            in1=scf_ap.unsqueeze(2).broadcast_to([128, ngroups, 16]),
            op0=ALU.mult, op1=ALU.mult))


def emit_l0ffn(nc, seq, xs, xo, pfx="b_"):
    TOK = 2048
    D = 2048
    I = 5632
    GUC = 88
    IC = 44

    gup = nc.dram_tensor("gup", [2 * I, 256], I32, kind="ExternalInput").ap()
    gusT = nc.dram_tensor("gusT", [2 * I, 16], F16, kind="ExternalInput").ap()
    dnp = nc.dram_tensor("dnp", [D, 704], I32, kind="ExternalInput").ap()
    dnsT = nc.dram_tensor("dnsT", [D, IC], F16, kind="ExternalInput").ap()
    wguT_d = nc.dram_tensor("wguT_d", [D, 2 * I], BF16).ap()
    wdnT_d = nc.dram_tensor("wdnT_d", [I, D], BF16).ap()

    al = Sbuf(nc, pfx)
    A = {}
    A["xlnT"] = al.sb("xlnT", [128, 16 * 1024], BF16)
    A["hT"] = al.sb("hT", [128, IC * 1024], BF16)
    A["xtile"] = al.sb("xtile", [128, D], F32)
    A["lnb"] = al.sb("lnb", [128, D], BF16)
    A["ysb"] = al.sb("ysb", [128, D], F32)
    A["pk"] = al.sb("pk", [128, 704], I32)
    A["sc16"] = al.sb("sc16", [128, IC], F16)
    A["scf"] = al.sb("scf", [128, IC], F32)
    A["dq_i"] = al.sb("dq_i", [128, 704], I32)
    A["dq_mf"] = al.sb("dq_mf", [128, 704], F32)
    A["dq_sgn"] = al.sb("dq_sgn", [128, 704], F32)
    A["dq_r1"] = al.sb("dq_r1", [128, 704], F32)
    A["dq_r2"] = al.sb("dq_r2", [128, 704], F32)
    A["dq_v"] = al.sb("dq_v", [128, 704], F32)
    A["Wc"] = al.sb("Wc", [128, I], BF16)
    A["WcT"] = al.sb("WcT", [128, IC * 128], BF16)
    A["wstage_g"] = al.sb("wstage_g", [128, 16 * 128], BF16)
    A["wstage_u"] = al.sb("wstage_u", [128, 16 * 128], BF16)
    A["wdstage"] = al.sb("wdstage", [128, D], BF16)
    A["gsil"] = al.sb("gsil", [128, 512], BF16)
    A["upb"] = al.sb("upb", [128, 512], BF16)
    A["ident"] = al.sb("ident", [128, 128], BF16)
    for nm in ["c_eps5", "c_eps6", "c_m05", "c_m2", "c_m6", "c_e11",
               "s_", "ssq", "mean", "negmean", "t1", "var", "rstd"]:
        A[nm] = al.sb(nm, [128, 1], F32)
    PB = al.ps("PB", [128, 2048], F32)
    PT = al.ps("PT", [128, 1024], BF16)

    emit_consts(seq, A)
    seq.add("vector", lambda v: v.memset(A["c_m2"][:, :], -2.0))
    seq.add("vector", lambda v: v.memset(A["c_m6"][:, :], -6.0))
    seq.add("vector", lambda v: v.memset(A["c_e11"][:, :], 1.000005e-11))

    for gc in range(GUC):
        seq.dma("sync", A["pk"][:, 0:256], gup[gc * 128:(gc + 1) * 128, :])
        seq.dma("sync", A["sc16"][:, 0:16], gusT[gc * 128:(gc + 1) * 128, :])
        seq.add("vector", lambda v: v.tensor_copy(out=A["scf"][:, 0:16], in_=A["sc16"][:, 0:16]))
        emit_dequant_fp4(seq, A, A["pk"][:, 0:256], A["scf"][:, 0:16], 16, 256,
                         A["Wc"][:, 0:2048])
        emit_transposes(seq, PT, [
            (A["Wc"][:, kc * 128:(kc + 1) * 128], A["WcT"][:, kc * 128:(kc + 1) * 128])
            for kc in range(16)])
        for half in range(2):
            seq.dma("gpsimd",
                    wguT_d[half * 1024:(half + 1) * 1024, gc * 128:(gc + 1) * 128]
                    .rearrange("(k p) o -> p k o", p=128),
                    A["WcT"][:, half * 1024:(half + 1) * 1024]
                    .rearrange("p (k o) -> p k o", k=8))

    for oc in range(16):
        seq.dma("sync", A["pk"][:, 0:704], dnp[oc * 128:(oc + 1) * 128, :])
        seq.dma("sync", A["sc16"][:, 0:IC], dnsT[oc * 128:(oc + 1) * 128, :])
        seq.add("vector", lambda v: v.tensor_copy(out=A["scf"][:, 0:IC], in_=A["sc16"][:, 0:IC]))
        emit_dequant_fp4(seq, A, A["pk"][:, 0:704], A["scf"][:, 0:IC], IC, 704,
                         A["Wc"][:, 0:I])
        emit_transposes(seq, PT, [
            (A["Wc"][:, kc * 128:(kc + 1) * 128], A["WcT"][:, kc * 128:(kc + 1) * 128])
            for kc in range(IC)])
        for half in range(4):
            seq.dma("gpsimd",
                    wdnT_d[half * 1408:(half + 1) * 1408, oc * 128:(oc + 1) * 128]
                    .rearrange("(k p) o -> p k o", p=128),
                    A["WcT"][:, half * 1408:(half + 1) * 1408]
                    .rearrange("p (k o) -> p k o", k=11))

    for grp in range(2):
        for rtl in range(8):
            rt = grp * 8 + rtl
            emit_lnT(seq, A, xs, rt, A["xlnT"], rtl * 128, 1024, PT)
        for gcp in range(IC):
            for (gc2, wst) in ((gcp, "wstage_g"), (IC + gcp, "wstage_u")):
                for half in range(2):
                    seq.dma("sync",
                            A[wst][:, half * 1024:(half + 1) * 1024]
                            .rearrange("p (k o) -> p k o", k=8),
                            wguT_d[half * 1024:(half + 1) * 1024, gc2 * 128:(gc2 + 1) * 128]
                            .rearrange("(k p) o -> p k o", p=128))
            for (wst, off) in (("wstage_g", 0), ("wstage_u", 1024)):
                for kc in range(16):
                    for tg in range(2):
                        seq.add("tensor", lambda t, wst=wst, kc=kc, tg=tg, off=off, grp=grp: t.matmul(
                            PB[:, off + tg * 512: off + (tg + 1) * 512],
                            A[wst][:, kc * 128:(kc + 1) * 128],
                            A["xlnT"][:, kc * 1024 + tg * 512: kc * 1024 + (tg + 1) * 512],
                            start=(kc == 0), stop=(kc == 15)))
            for tg in range(2):
                seq.add("scalar", lambda e, tg=tg: e.activation(
                    out=A["gsil"][:, :], in_=PB[:, tg * 512:(tg + 1) * 512], func=AF.Silu,
                    bias=0.0, scale=1.0))
                seq.add("vector", lambda v, tg=tg: v.tensor_copy(
                    out=A["upb"][:, :], in_=PB[:, 1024 + tg * 512:1024 + (tg + 1) * 512]))
                seq.add("vector", lambda v, tg=tg, gcp=gcp: v.tensor_tensor(
                    out=A["hT"][:, gcp * 1024 + tg * 512: gcp * 1024 + (tg + 1) * 512],
                    in0=A["gsil"][:, :], in1=A["upb"][:, :], op=ALU.mult))
        for tt_ in range(8):
            rt = grp * 8 + tt_
            for ic in range(IC):
                seq.dma("sync", A["wdstage"][:, :], wdnT_d[ic * 128:(ic + 1) * 128, :])
                for nn in range(4):
                    seq.add("tensor", lambda t, ic=ic, tt_=tt_, nn=nn: t.matmul(
                        PB[:, nn * 512:(nn + 1) * 512],
                        A["hT"][:, ic * 1024 + tt_ * 128: ic * 1024 + (tt_ + 1) * 128],
                        A["wdstage"][:, nn * 512:(nn + 1) * 512],
                        start=(ic == 0), stop=(ic == IC - 1)))
            seq.add("scalar", lambda e: e.activation(
                out=A["ysb"][:, :], in_=PB[:, :], func=AF.Copy, accum_out=A["s_"][:, :]))
            seq.add("scalar", lambda e: e.activation(
                out=A["lnb"][:, :], in_=A["ysb"][:, :], func=AF.Square, accum_out=A["ssq"][:, :]))

            def dve_stats(v):
                v.tensor_scalar(out=A["mean"][:, :], in0=A["s_"][:, :], scalar1=1.0 / D,
                                scalar2=None, op0=ALU.mult)
                v.tensor_scalar(out=A["negmean"][:, :], in0=A["s_"][:, :], scalar1=-1.0 / D,
                                scalar2=None, op0=ALU.mult)
                v.tensor_scalar(out=A["t1"][:, :], in0=A["ssq"][:, :],
                                scalar1=(1.0 + 1e-5) / D, scalar2=None, op0=ALU.mult)
                return v.scalar_tensor_tensor(out=A["var"][:, :], in0=A["mean"][:, :],
                                              scalar=A["negmean"][:, :], in1=A["t1"][:, :],
                                              op0=ALU.mult, op1=ALU.add)
            seq.add("vector", dve_stats)

            def act_rstd(e):
                e.activation(out=A["var"][:, :], in_=A["var"][:, :], func=AF.Ln,
                             bias=A["c_e11"][:, :], scale=1.0)
                return e.activation(out=A["rstd"][:, :], in_=A["var"][:, :], func=AF.Exp,
                                    bias=0.0, scale=A["c_m05"][:, :])
            seq.add("scalar", act_rstd)
            seq.dma("sync", A["xtile"][:, :], xs[rt * 128:(rt + 1) * 128, :])
            seq.add("vector", lambda v: v.tensor_scalar(
                out=A["ysb"][:, :], in0=A["ysb"][:, :], scalar1=A["negmean"][:, :],
                scalar2=A["rstd"][:, :], op0=ALU.add, op1=ALU.mult))
            seq.add("vector", lambda v: v.tensor_tensor(
                out=A["xtile"][:, :], in0=A["xtile"][:, :], in1=A["ysb"][:, :], op=ALU.add))
            seq.dma("gpsimd", xo[rt * 128:(rt + 1) * 128, :], A["xtile"][:, :])

    al.close()


def l0ffn_host_prep(inputs):
    return {
        "gup": np.ascontiguousarray(inputs["L0_ffn_gate_up_packed"]).astype(np.int32),
        "gusT": np.ascontiguousarray(np.asarray(inputs["L0_ffn_gate_up_scales"]).T),
        "dnp": np.ascontiguousarray(inputs["L0_ffn_down_packed"]).astype(np.int32),
        "dnsT": np.ascontiguousarray(np.asarray(inputs["L0_ffn_down_scales"]).T),
    }


def run_l0ffn(x, inputs, n_cores=8, dbg=False):
    shared = l0ffn_host_prep(inputs)
    raise NotImplementedError
    in_maps = []
    for c in range(n_cores):
        xsh = np.ascontiguousarray(
            x[:, c * 512:(c + 1) * 512, :].reshape(2048, 2048)).astype(np.float32)
        m = {"xs": xsh}
        m.update(shared)
        in_maps.append(m)
    res = run_bass_kernel_spmd(nc, in_maps, core_ids=list(range(n_cores)))
    out = np.empty((4, 4096, 2048), np.float32)
    for c in range(n_cores):
        out[:, c * 512:(c + 1) * 512, :] = res.results[c]["xo"].reshape(4, 512, 2048)
    if dbg:
        return out, res.results
    return out


def emit_l1attn(nc, seq, xs, xo, pfx="c_", dbg=False):
    TOK, D = 2048, 2048
    SCALE = 1.0 / float(np.sqrt(128.0))

    wqkvT = nc.dram_tensor("wqkvT", [D, 3072], F32, kind="ExternalInput").ap()
    wpT = nc.dram_tensor("wpT", [D, D], F32, kind="ExternalInput").ap()
    if dbg:
        d_qkv = nc.dram_tensor("d_qkv", [128, 4 * 3072], BF16, kind="ExternalOutput").ap()
        d_S = nc.dram_tensor("d_S", [128, 160], F32, kind="ExternalOutput").ap()
        d_E = nc.dram_tensor("d_E", [128, 160], BF16, kind="ExternalOutput").ap()
        d_att = nc.dram_tensor("d_att", [128, 4 * 2048], BF16, kind="ExternalOutput").ap()
        d_mx = nc.dram_tensor("d_mx", [128, 64], F32, kind="ExternalOutput").ap()
        d_Dm = nc.dram_tensor("d_Dm", [128, 160], F32, kind="ExternalOutput").ap()

    al = Sbuf(nc, pfx)
    A = {}
    A["xlnT"] = al.sb("xlnT", [128, 16 * TOK], BF16)
    A["qkv"] = al.sb("qkv", [128, 4 * 3072], BF16)
    A["att_tm"] = al.sb("att_tm", [128, 4 * 2048], BF16)
    A["attnT"] = al.sb("attnT", [128, 16 * 512], BF16)
    A["wq_st"] = al.sb("wq_st", [128, 3072], BF16)
    A["wp_st"] = al.sb("wp_st", [128, 2048], BF16)
    A["prod"] = al.sb("prod", [128, 2048], BF16)
    A["acc"] = al.sb("acc", [128, 2048], F32)
    A["tmp2"] = al.sb("tmp2", [128, 2048], F32)
    A["S"] = al.sb("S", [128, 160], F32)
    A["Dm"] = al.sb("Dm", [128, 160], F32)
    A["E"] = al.sb("E", [128, 160], BF16)
    A["mx"] = al.sb("mx", [128, 64], F32)
    A["den"] = al.sb("den", [128, 64], F32)
    A["rden"] = al.sb("rden", [128, 64], F32)
    A["xtile"] = al.sb("xtile", [128, D], F32)
    A["lnb"] = al.sb("lnb", [128, D], BF16)
    A["ysb"] = al.sb("ysb", [128, D], F32)
    A["ident"] = al.sb("ident", [128, 128], BF16)
    for nm in ["c_eps5", "c_eps6", "c_m05", "s_", "ssq", "mean", "negmean", "t1", "var", "rstd"]:
        A[nm] = al.sb(nm, [128, 1], F32)
    PQ = al.ps("PQ", [128, 3072], F32)
    PT = al.ps("PT", [128, 1024], BF16)

    pairs = [(i, j) for i in range(4) for j in range(i + 1)]

    def qkv4(b):
        return A["qkv"][:, b * 3072:(b + 1) * 3072].rearrange(
            "p (g x d) -> p g x d", g=4, x=6)

    emit_consts(seq, A)

    for rt in range(16):
        emit_lnT(seq, A, xs, rt, A["xlnT"], rt * 128, TOK, PT)

    for pt in range(4):
        for i in range(4):
            rt = i * 4 + pt
            for c in range(16):
                seq.dma("gpsimd", A["wq_st"][:, :], wqkvT[c * 128:(c + 1) * 128, :])
                for nn in range(6):
                    seq.add("tensor", lambda t, c=c, rt=rt, nn=nn: t.matmul(
                        PQ[:, nn * 512:(nn + 1) * 512],
                        A["xlnT"][:, c * TOK + rt * 128: c * TOK + (rt + 1) * 128],
                        A["wq_st"][:, nn * 512:(nn + 1) * 512],
                        start=(c == 0), stop=(c == 15)))
            seq.add("vector", lambda v, i=i: v.tensor_copy(
                out=A["qkv"][:, i * 3072:(i + 1) * 3072], in_=PQ[:, :]))

        for pidx, (i, j) in enumerate(pairs):
            qa = qkv4(i)[:, :, 0:4, :]
            ka = qkv4(j)[:, :, 4:5, :].broadcast_to([128, 4, 4, 128])
            seq.add("vector", lambda v, qa=qa, ka=ka: v.tensor_tensor(
                out=A["prod"][:, :].rearrange("p (g x d) -> p g x d", g=4, x=4),
                in0=qa, in1=ka, op=ALU.mult))
            seq.add("vector", lambda v, pidx=pidx: v.tensor_reduce(
                out=A["S"][:, pidx * 16:(pidx + 1) * 16].rearrange("p (g x) -> p g x", g=4),
                in_=A["prod"][:, :].rearrange("p (g x d) -> p g x d", g=4, x=4),
                axis=AX.X, op=ALU.add))
        def scale_S(v):
            v.nop()
            return v.tensor_scalar(
                out=A["S"][:, :], in0=A["S"][:, :], scalar1=SCALE, scalar2=None, op0=ALU.mult)
        seq.add("vector", scale_S)

        def softmax(v):
            pofs = {(i, j): pairs.index((i, j)) * 16 for (i, j) in pairs}
            for i in range(4):
                mslice = A["mx"][:, i * 16:(i + 1) * 16]
                v.tensor_copy(out=mslice, in_=A["S"][:, pofs[(i, 0)]:pofs[(i, 0)] + 16])
                v.nop()
                for j in range(1, i + 1):
                    v.tensor_tensor(out=mslice, in0=mslice,
                                    in1=A["S"][:, pofs[(i, j)]:pofs[(i, j)] + 16],
                                    op=ALU.max)
                    v.nop()
            for pidx, (i, j) in enumerate(pairs):
                v.tensor_tensor(out=A["Dm"][:, pidx * 16:(pidx + 1) * 16],
                                in0=A["S"][:, pidx * 16:(pidx + 1) * 16],
                                in1=A["mx"][:, i * 16:(i + 1) * 16], op=ALU.subtract)
                v.nop()
            return v.tensor_copy(out=A["den"][:, :], in_=A["mx"][:, :])
        seq.add("vector", softmax)
        seq.add("scalar", lambda e: e.activation(
            out=A["E"][:, :], in_=A["Dm"][:, :], func=AF.Exp, bias=0.0, scale=1.0))

        def denom(v):
            pofs = {(i, j): pairs.index((i, j)) * 16 for (i, j) in pairs}
            for i in range(4):
                dslice = A["den"][:, i * 16:(i + 1) * 16]
                v.tensor_copy(out=dslice, in_=A["E"][:, pofs[(i, 0)]:pofs[(i, 0)] + 16])
                v.nop()
                for j in range(1, i + 1):
                    v.tensor_tensor(out=dslice, in0=dslice,
                                    in1=A["E"][:, pofs[(i, j)]:pofs[(i, j)] + 16], op=ALU.add)
                    v.nop()
            v.nop()
            return v.reciprocal(out=A["rden"][:, :], in_=A["den"][:, :])
        seq.add("vector", denom)

        for i in range(4):
            for j in range(i + 1):
                pidx = pairs.index((i, j))
                va = qkv4(j)[:, :, 5:6, :].broadcast_to([128, 4, 4, 128])
                ea = A["E"][:, pidx * 16:(pidx + 1) * 16].rearrange(
                    "p (g x) -> p g x", g=4).unsqueeze(3).broadcast_to([128, 4, 4, 128])
                dst = A["acc"] if j == 0 else A["tmp2"]
                seq.add("vector", lambda v, va=va, ea=ea, dst=dst: v.tensor_tensor(
                    out=dst[:, :].rearrange("p (g x d) -> p g x d", g=4, x=4),
                    in0=va, in1=ea, op=ALU.mult))
                if j > 0:
                    seq.add("vector", lambda v: v.tensor_tensor(
                        out=A["acc"][:, :], in0=A["acc"][:, :], in1=A["tmp2"][:, :], op=ALU.add))
            ra = A["rden"][:, i * 16:(i + 1) * 16].unsqueeze(2).broadcast_to([128, 16, 128])
            seq.add("vector", lambda v, i=i, ra=ra: v.tensor_tensor(
                out=A["att_tm"][:, i * 2048:(i + 1) * 2048].rearrange("p (h d) -> p h d", h=16),
                in0=A["acc"][:, :].rearrange("p (h d) -> p h d", h=16),
                in1=ra, op=ALU.mult))

        if dbg and pt == 3:
            seq.dma("gpsimd", d_qkv, A["qkv"][:, :])
            seq.dma("gpsimd", d_S, A["S"][:, :])
            seq.dma("gpsimd", d_E, A["E"][:, :])
            seq.dma("gpsimd", d_mx, A["mx"][:, :])
            seq.dma("gpsimd", d_Dm, A["Dm"][:, :])
            seq.dma("gpsimd", d_att, A["att_tm"][:, :])
        emit_transposes(seq, PT, [
            (A["att_tm"][:, i * 2048 + c * 128: i * 2048 + (c + 1) * 128],
             A["attnT"][:, c * 512 + i * 128: c * 512 + (i + 1) * 128])
            for i in range(4) for c in range(16)])

        for i in range(4):
            rt = i * 4 + pt
            for c in range(16):
                seq.dma("gpsimd", A["wp_st"][:, :], wpT[c * 128:(c + 1) * 128, :])
                for nn in range(4):
                    seq.add("tensor", lambda t, i=i, c=c, nn=nn: t.matmul(
                        PQ[:, nn * 512:(nn + 1) * 512],
                        A["attnT"][:, c * 512 + i * 128: c * 512 + (i + 1) * 128],
                        A["wp_st"][:, nn * 512:(nn + 1) * 512],
                        start=(c == 0), stop=(c == 15)))
            seq.add("scalar", lambda e: e.activation(
                out=A["ysb"][:, :], in_=PQ[:, 0:2048], func=AF.Copy, accum_out=A["s_"][:, :]))
            seq.add("scalar", lambda e: e.activation(
                out=A["lnb"][:, :], in_=A["ysb"][:, :], func=AF.Square, accum_out=A["ssq"][:, :]))

            def dve_stats(v):
                v.tensor_scalar(out=A["mean"][:, :], in0=A["s_"][:, :], scalar1=1.0 / D,
                                scalar2=None, op0=ALU.mult)
                v.tensor_scalar(out=A["negmean"][:, :], in0=A["s_"][:, :], scalar1=-1.0 / D,
                                scalar2=None, op0=ALU.mult)
                v.tensor_scalar(out=A["t1"][:, :], in0=A["ssq"][:, :], scalar1=1.0 / D,
                                scalar2=None, op0=ALU.mult)
                return v.scalar_tensor_tensor(out=A["var"][:, :], in0=A["mean"][:, :],
                                              scalar=A["negmean"][:, :], in1=A["t1"][:, :],
                                              op0=ALU.mult, op1=ALU.add)
            seq.add("vector", dve_stats)

            def act_rstd(e):
                e.activation(out=A["var"][:, :], in_=A["var"][:, :], func=AF.Ln,
                             bias=A["c_eps5"][:, :], scale=1.0)
                return e.activation(out=A["rstd"][:, :], in_=A["var"][:, :], func=AF.Exp,
                                    bias=0.0, scale=A["c_m05"][:, :])
            seq.add("scalar", act_rstd)
            seq.dma("sync", A["xtile"][:, :], xs[rt * 128:(rt + 1) * 128, :])
            seq.add("vector", lambda v: v.tensor_scalar(
                out=A["ysb"][:, :], in0=A["ysb"][:, :], scalar1=A["negmean"][:, :],
                scalar2=A["rstd"][:, :], op0=ALU.add, op1=ALU.mult))
            seq.add("vector", lambda v: v.tensor_tensor(
                out=A["xtile"][:, :], in0=A["xtile"][:, :], in1=A["ysb"][:, :], op=ALU.add))
            seq.dma("gpsimd", xo[rt * 128:(rt + 1) * 128, :], A["xtile"][:, :])

    al.close()


def run_l1attn(x, inputs, n_cores=8, dbg=False):
    shared = {
        "wqkvT": np.ascontiguousarray(np.asarray(inputs["L1_attn_weight_qkv"], np.float32).T),
        "wpT": np.ascontiguousarray(np.asarray(inputs["L1_attn_weight_proj"], np.float32).T),
    }
    raise NotImplementedError
    in_maps = []
    for c in range(n_cores):
        xsh = np.ascontiguousarray(
            x[:, c * 512:(c + 1) * 512, :].reshape(2048, 2048)).astype(np.float32)
        m = {"xs": xsh}
        m.update(shared)
        in_maps.append(m)
    res = run_bass_kernel_spmd(nc, in_maps, core_ids=list(range(n_cores)))
    out = np.empty((4, 4096, 2048), np.float32)
    for c in range(n_cores):
        out[:, c * 512:(c + 1) * 512, :] = res.results[c]["xo"].reshape(4, 512, 2048)
    if dbg:
        return out, res.results
    return out


def emit_l0attn(nc, seq, xs, xo, pfx="a_", dbg=False):
    TOK, D = 2048, 2048
    SCALE = 0.08838834764831845
    NEG = -1e30

    xh = nc.dram_tensor("xh", [TOK, D], F32, kind="ExternalInput").ap()
    wqc = nc.dram_tensor("wqc", [D, 512], F32, kind="ExternalInput").ap()
    wqu = nc.dram_tensor("wqu", [512, D], F32, kind="ExternalInput").ap()
    wk = nc.dram_tensor("wk", [D, 128], F32, kind="ExternalInput").ap()
    wv = nc.dram_tensor("wv", [D, 128], F32, kind="ExternalInput").ap()
    w1 = nc.dram_tensor("w1", [4, 512, 2048], F32, kind="ExternalInput").ap()
    w2 = nc.dram_tensor("w2", [8192, 2048], F32, kind="ExternalInput").ap()
    qnw = nc.dram_tensor("qnw", [128, 128], F32, kind="ExternalInput").ap()
    knw = nc.dram_tensor("knw", [128, 128], F32, kind="ExternalInput").ap()
    sinkb = nc.dram_tensor("sinkb", [128, 16], F32, kind="ExternalInput").ap()
    mask = nc.dram_tensor("mask", [4, 128, 640], F32, kind="ExternalInput").ap()
    cos_o = nc.dram_tensor("cos_o", [512, 32], F32, kind="ExternalInput").ap()
    sin_o = nc.dram_tensor("sin_o", [512, 32], F32, kind="ExternalInput").ap()
    cos_h = nc.dram_tensor("cos_h", [512, 32], F32, kind="ExternalInput").ap()
    sin_h = nc.dram_tensor("sin_h", [512, 32], F32, kind="ExternalInput").ap()
    if dbg:
        dbgt = {}
        for nm_, sh in [("kT", [128, 1024]), ("vsb", [128, 1024]), ("qT", [128, 8192]),
                        ("attnT", [128, 8192]), ("out1T", [128, 8192]), ("xlnT", [128, 8192]),
                        ("E", [128, 640]), ("smb0", [128, 640])]:
            dbgt[nm_] = nc.dram_tensor("d_" + nm_, sh,
                                       F32 if nm_ in ("smb0",) else BF16, kind="ExternalOutput").ap()
        d_y2 = nc.dram_tensor("d_y2", [128, 8192], BF16, kind="ExternalOutput").ap()
        d_qbuf = nc.dram_tensor("d_qbuf", [128, 2048], F32, kind="ExternalOutput").ap()
        d_qss = nc.dram_tensor("d_qss", [128, 16], F32, kind="ExternalOutput").ap()
        d_qrr = nc.dram_tensor("d_qrr", [128, 16], F32, kind="ExternalOutput").ap()
        d_qn = nc.dram_tensor("d_qn", [128, 2048], F32, kind="ExternalOutput").ap()
        d_kv = nc.dram_tensor("d_kv", [128, 128], F32, kind="ExternalOutput").ap()
        d_rb = nc.dram_tensor("d_rb", [128, 128], F32, kind="ExternalOutput").ap()
        d_s = nc.dram_tensor("d_s", [128, 2], F32, kind="ExternalOutput").ap()

    al = Sbuf(nc, pfx)
    A = {}
    A["mask"] = al.sb("m_mask", [128, 4 * 640], F32)
    A["cos_o"] = al.sb("m_cos_o", [128, 4 * 32], F32)
    A["sin_o"] = al.sb("m_sin_o", [128, 4 * 32], F32)
    A["cos_h"] = al.sb("m_cos_h", [128, 4 * 32], F32)
    A["sin_h"] = al.sb("m_sin_h", [128, 4 * 32], F32)
    A["qnw"] = al.sb("m_qnw", [128, 128], F32)
    A["knw"] = al.sb("m_knw", [128, 128], F32)
    A["sinkb"] = al.sb("m_sinkb", [128, 16], F32)
    A["negsink"] = al.sb("m_negsink", [128, 16], F32)
    A["wqc"] = al.sb("m_wqc", [128, 16 * 512], BF16)
    A["wqu"] = al.sb("m_wqu", [128, 4 * 2048], BF16)
    A["wk"] = al.sb("m_wk", [128, 16 * 128], BF16)
    A["wv"] = al.sb("m_wv", [128, 16 * 128], BF16)
    A["xlnT"] = al.sb("xlnT", [128, 16 * 512], BF16)
    A["lnTt"] = al.sb("lnTt", [128, 16 * 128], BF16)
    A["kT"] = al.sb("kT", [128, 1024], BF16)
    A["vsb"] = al.sb("vsb", [128, 8 * 128], BF16)
    A["kvtmp"] = al.sb("kvtmp", [128, 256], F32)
    A["U"] = al.sb("Ubuf", [128, 2048], F32)
    A["ra"] = A["U"][:, 0:512]
    A["rb"] = A["U"][:, 512:1024]
    A["rc"] = A["U"][:, 1024:1536]
    A["rd"] = A["U"][:, 1536:2048]
    A["qcT"] = al.sb("qcT", [128, 4 * 512], BF16)
    A["qbuf"] = al.sb("qbuf", [128, 2048], F32)
    A["qss"] = al.sb("qss", [128, 16], F32)
    A["qrr"] = al.sb("qrr", [128, 16], F32)
    A["qbf"] = al.sb("qbf", [128, 2048], BF16)
    A["qT"] = al.sb("qT", [128, 16 * 512], BF16)
    A["smb"] = al.sb("smb", [128, 640], F32)
    A["E"] = al.sb("E", [128, 640], BF16)
    A["eT"] = al.sb("eT", [128, 640], BF16)
    A["attnT"] = al.sb("attnT", [128, 16 * 512], BF16)
    A["w1g"] = al.sb("w1g", [128, 4 * 2048], BF16)
    A["out1T"] = al.sb("out1T", [128, 16 * 512], BF16)
    A["w2gh"] = al.sb("w2gh", [128, 16 * 512], BF16)
    A["y2acc"] = al.sb("y2acc", [128, 4 * 2048], BF16)
    A["xtile"] = A["qbuf"]
    A["lnb"] = al.sb("lnb", [128, D], BF16)
    A["ysb"] = A["U"]
    A["qsq"] = A["U"]
    A["ident"] = al.sb("ident", [128, 128], BF16)
    A["_stats"] = al.sb("_stats", [128, 16], F32)
    for _i, nm_ in enumerate(["dsum", "es", "nm", "nm2", "rr", "c_eps5", "c_eps6", "c_m05",
                              "s_", "ssq", "mean", "negmean", "t1", "var", "rstd"]):
        A[nm_] = A["_stats"][:, _i:_i + 1]
    PA = al.ps("PA", [128, 2048], F32)
    PB = al.ps("PB", [128, 1536], F32)
    PT = al.ps("PT2", [128, 1024], BF16)

    emit_consts(seq, A)

    seq.dma("sync", A["mask"][:, :].rearrange("p (q w) -> p q w", q=4), mask.transpose([1, 0, 2]))
    for (dst, src) in (("cos_o", cos_o), ("sin_o", sin_o), ("cos_h", cos_h), ("sin_h", sin_h)):
        seq.dma("sync", A[dst][:, :].rearrange("p (q w) -> p q w", q=4),
                src.rearrange("(q p) w -> p q w", p=128))
    seq.dma("sync", A["qnw"][:, :], qnw)
    seq.dma("sync", A["knw"][:, :], knw)
    seq.dma("sync", A["sinkb"][:, :], sinkb)
    seq.add("vector", lambda v: v.tensor_scalar(
        out=A["negsink"][:, :], in0=A["sinkb"][:, :], scalar1=-1.0, scalar2=None, op0=ALU.mult))
    for kc in range(16):
        seq.dma("gpsimd", A["wqc"][:, kc * 512:(kc + 1) * 512], wqc[kc * 128:(kc + 1) * 128, :])
        seq.dma("gpsimd", A["wk"][:, kc * 128:(kc + 1) * 128], wk[kc * 128:(kc + 1) * 128, :])
        seq.dma("gpsimd", A["wv"][:, kc * 128:(kc + 1) * 128], wv[kc * 128:(kc + 1) * 128, :])
    for qc_c in range(4):
        seq.dma("gpsimd", A["wqu"][:, qc_c * 2048:(qc_c + 1) * 2048],
                wqu[qc_c * 128:(qc_c + 1) * 128, :])

    def rope_rms_k(ht, src128, costile, sintile, is_halo):
        k = A["kvtmp"]
        seq.add("vector", lambda v: v.tensor_tensor(out=A["ra"][:, 0:32], in0=k[:, 0:32],
                                                    in1=costile, op=ALU.mult))
        seq.add("vector", lambda v: v.tensor_tensor(out=A["rb"][:, 0:32], in0=k[:, 32:64],
                                                    in1=sintile, op=ALU.mult))
        seq.add("vector", lambda v: v.tensor_tensor(out=A["rc"][:, 0:32], in0=k[:, 0:32],
                                                    in1=sintile, op=ALU.mult))
        seq.addf("vector", lambda v: v.tensor_tensor(out=A["rd"][:, 0:32], in0=k[:, 32:64],
                                                     in1=costile, op=ALU.mult))
        seq.addf("vector", lambda v: v.tensor_tensor(out=k[:, 0:32], in0=A["ra"][:, 0:32],
                                                     in1=A["rb"][:, 0:32], op=ALU.subtract))
        seq.addf("vector", lambda v: v.tensor_tensor(out=k[:, 32:64], in0=A["rc"][:, 0:32],
                                                     in1=A["rd"][:, 0:32], op=ALU.add))
        seq.add("scalar", lambda e: e.activation(out=A["ra"][:, 0:128], in_=k[:, 0:128],
                                                 func=AF.Square, accum_out=A["s_"][:, :]))
        seq.add("vector", lambda v: v.tensor_scalar(out=A["t1"][:, :], in0=A["s_"][:, :],
                                                    scalar1=1.0 / 128, scalar2=None, op0=ALU.mult))

        def act_rr(e):
            e.activation(out=A["var"][:, :], in_=A["t1"][:, :], func=AF.Ln,
                         bias=A["c_eps6"][:, :], scale=1.0)
            return e.activation(out=A["rr"][:, :], in_=A["var"][:, :], func=AF.Exp,
                                bias=0.0, scale=A["c_m05"][:, :])
        seq.add("scalar", act_rr)
        seq.addf("vector", lambda v: v.scalar_tensor_tensor(
            out=A["rb"][:, 0:128], in0=k[:, 0:128], scalar=A["rr"][:, :], in1=A["knw"][:, :],
            op0=ALU.mult, op1=ALU.mult))
        seq.add("vector", lambda v: v.tensor_copy(out=A["lnb"][:, 0:128], in_=A["rb"][:, 0:128]))
        seq.add("tensor", lambda t: t.transpose(PT[:, 0:128], A["lnb"][:, 0:128], A["ident"][:, :]))
        seq.add("vector", lambda v, ht=ht: v.tensor_copy(
            out=A["kT"][:, ht * 128:(ht + 1) * 128], in_=PT[:, 0:128]))

    for b in range(4):
        for tt in range(4):
            rt = b * 4 + tt
            emit_lnT(seq, A, xs, rt, A["xlnT"], tt * 128, 512, PT)
        for ht in range(8):
            if ht < 4:
                rt = b * 4 + ht
                emit_lnT(seq, A, xh, rt, A["lnTt"], 0, 128, PT)
                lnsrc, coff = A["lnTt"], lambda kc: kc * 128
            else:
                tt = ht - 4
                lnsrc, coff = A["xlnT"], lambda kc, tt=tt: kc * 512 + tt * 128
            for kc in range(16):
                seq.add("tensor", lambda t, lnsrc=lnsrc, coff=coff, kc=kc: t.matmul(
                    PA[:, 0:128], lnsrc[:, coff(kc):coff(kc) + 128],
                    A["wk"][:, kc * 128:(kc + 1) * 128], start=(kc == 0), stop=(kc == 15)))
                seq.add("tensor", lambda t, lnsrc=lnsrc, coff=coff, kc=kc: t.matmul(
                    PA[:, 512:640], lnsrc[:, coff(kc):coff(kc) + 128],
                    A["wv"][:, kc * 128:(kc + 1) * 128], start=(kc == 0), stop=(kc == 15)))
            seq.add("scalar", lambda e: e.copy(out=A["kvtmp"][:, 0:128], in_=PA[:, 0:128]))
            seq.add("vector", lambda v, ht=ht: v.tensor_copy(
                out=A["vsb"][:, ht * 128:(ht + 1) * 128], in_=PA[:, 512:640]))
            pt_i = ht % 4
            cs, sn = ("cos_h", "sin_h") if ht < 4 else ("cos_o", "sin_o")
            rope_rms_k(ht, None, A[cs][:, pt_i * 32:(pt_i + 1) * 32],
                       A[sn][:, pt_i * 32:(pt_i + 1) * 32], ht < 4)
            if dbg and b == 0 and ht == 7:
                seq.dma("gpsimd", d_kv, A["kvtmp"][:, 0:128])
                seq.dma("gpsimd", d_rb, A["rb"][:, 0:128])
                seq.add("vector", lambda v: v.tensor_copy(out=A["smb"][:, 0:1], in_=A["s_"][:, :]))
                seq.add("vector", lambda v: v.tensor_copy(out=A["smb"][:, 1:2], in_=A["rr"][:, :]))
                seq.dma("gpsimd", d_s, A["smb"][:, 0:2])

        for mc in range(4):
            for kc in range(16):
                seq.add("tensor", lambda t, mc=mc, kc=kc: t.matmul(
                    PA[:, 0:512],
                    A["wqc"][:, kc * 512 + mc * 128: kc * 512 + (mc + 1) * 128],
                    A["xlnT"][:, kc * 512:(kc + 1) * 512],
                    start=(kc == 0), stop=(kc == 15)))
            seq.add("vector", lambda v, mc=mc: v.tensor_copy(
                out=A["qcT"][:, mc * 512:(mc + 1) * 512], in_=PA[:, 0:512]))
        for tt in range(4):
            for qc_c in range(4):
                for nn in range(4):
                    seq.add("tensor", lambda t, qc_c=qc_c, tt=tt, nn=nn: t.matmul(
                        PA[:, nn * 512:(nn + 1) * 512],
                        A["qcT"][:, qc_c * 512 + tt * 128: qc_c * 512 + (tt + 1) * 128],
                        A["wqu"][:, qc_c * 2048 + nn * 512: qc_c * 2048 + (nn + 1) * 512],
                        start=(qc_c == 0), stop=(qc_c == 3)))
            seq.add("scalar", lambda e: e.copy(out=A["qbuf"][:, :], in_=PA[:, :]))
            co = A["cos_o"][:, tt * 32:(tt + 1) * 32].unsqueeze(1).broadcast_to([128, 16, 32])
            si = A["sin_o"][:, tt * 32:(tt + 1) * 32].unsqueeze(1).broadcast_to([128, 16, 32])
            q3 = A["qbuf"][:, :].rearrange("p (h d) -> p h d", h=16)
            for (dst, xsl, tr) in (("ra", (0, 32), co), ("rb", (32, 64), si),
                                   ("rc", (0, 32), si), ("rd", (32, 64), co)):
                seq.addf("vector", lambda v, dst=dst, xsl=xsl, tr=tr, q3=q3: v.tensor_tensor(
                    out=A[dst][:, 0:512].rearrange("p (h d) -> p h d", h=16),
                    in0=q3[:, :, xsl[0]:xsl[1]], in1=tr, op=ALU.mult))
            seq.addf("vector", lambda v, q3=q3: v.tensor_tensor(
                out=q3[:, :, 0:32],
                in0=A["ra"][:, 0:512].rearrange("p (h d) -> p h d", h=16),
                in1=A["rb"][:, 0:512].rearrange("p (h d) -> p h d", h=16), op=ALU.subtract))
            seq.addf("vector", lambda v, q3=q3: v.tensor_tensor(
                out=q3[:, :, 32:64],
                in0=A["rc"][:, 0:512].rearrange("p (h d) -> p h d", h=16),
                in1=A["rd"][:, 0:512].rearrange("p (h d) -> p h d", h=16), op=ALU.add))
            seq.add("scalar", lambda e: e.nop())
            seq.addf("vector", lambda v: v.tensor_tensor(
                out=A["qsq"][:, :], in0=A["qbuf"][:, :], in1=A["qbuf"][:, :], op=ALU.mult))
            seq.add("scalar", lambda e: e.nop())
            seq.addf("vector", lambda v, q3=q3: v.tensor_reduce(
                out=A["qss"][:, :].rearrange("p (h x) -> p h x", x=1),
                in_=A["qsq"][:, :].rearrange("p (h d) -> p h d", h=16),
                axis=AX.X, op=ALU.add))
            seq.add("vector", lambda v: v.nop())
            seq.addf("vector", lambda v: v.tensor_scalar(
                out=A["qss"][:, :], in0=A["qss"][:, :], scalar1=1.0 / 128, scalar2=None,
                op0=ALU.mult))

            def act_qrr(e):
                e.activation(out=A["qss"][:, :], in_=A["qss"][:, :], func=AF.Ln,
                             bias=A["c_eps6"][:, :], scale=1.0)
                e.nop()
                return e.activation(out=A["qrr"][:, :], in_=A["qss"][:, :], func=AF.Exp,
                                    bias=0.0, scale=A["c_m05"][:, :])
            seq.add("scalar", act_qrr)
            rr3 = A["qrr"][:, :].unsqueeze(2).broadcast_to([128, 16, 128])
            qn3 = A["qnw"][:, :].unsqueeze(1).broadcast_to([128, 16, 128])
            seq.addf("vector", lambda v, q3=q3, rr3=rr3: v.tensor_tensor(
                out=A["qsq"][:, :].rearrange("p (h d) -> p h d", h=16),
                in0=q3, in1=rr3, op=ALU.mult))
            seq.addf("vector", lambda v, qn3=qn3: v.tensor_tensor(
                out=A["qbf"][:, :].rearrange("p (h d) -> p h d", h=16),
                in0=A["qsq"][:, :].rearrange("p (h d) -> p h d", h=16),
                in1=qn3, op=ALU.mult))
            if dbg and b == 0 and tt == 0:
                seq.dma("gpsimd", d_qbuf, A["qbuf"][:, :])
                seq.dma("gpsimd", d_qss, A["qss"][:, :])
                seq.dma("gpsimd", d_qrr, A["qrr"][:, :])
                seq.dma("gpsimd", d_qn, A["qsq"][:, :])
            emit_transposes(seq, PT, [
                (A["qbf"][:, h * 128:(h + 1) * 128],
                 A["qT"][:, h * 512 + tt * 128: h * 512 + (tt + 1) * 128])
                for h in range(16)])

        for h in range(16):
            for qt in range(4):
                seq.add("tensor", lambda t, h=h, qt=qt: t.matmul(
                    PA[:, 0:512], A["qT"][:, h * 512 + qt * 128: h * 512 + (qt + 1) * 128],
                    A["kT"][:, qt * 128: qt * 128 + 512], start=True, stop=True))
                seq.add("tensor", lambda t, h=h, qt=qt: t.matmul(
                    PA[:, 512:640], A["qT"][:, h * 512 + qt * 128: h * 512 + (qt + 1) * 128],
                    A["kT"][:, qt * 128 + 512: qt * 128 + 640], start=True, stop=True))
                seq.add("vector", lambda v, qt=qt: v.scalar_tensor_tensor(
                    out=A["smb"][:, :], in0=PA[:, 0:640], scalar=SCALE,
                    in1=A["mask"][:, qt * 640:(qt + 1) * 640], op0=ALU.mult, op1=ALU.add))
                seq.add("scalar", lambda e: e.nop())
                seq.add("vector", lambda v: v.reduce_max(
                    out=A["nm"][:, :], in_=A["smb"][:, :], axis=AX.X, negate=True))
                seq.addf("vector", lambda v, h=h: v.tensor_tensor(
                    out=A["nm2"][:, :], in0=A["nm"][:, :], in1=A["negsink"][:, h:h + 1],
                    op=ALU.min))
                seq.add("scalar", lambda e: e.activation(
                    out=A["E"][:, :], in_=A["smb"][:, :], func=AF.Exp, bias=A["nm2"][:, :],
                    scale=1.0, accum_out=A["dsum"][:, :]))
                seq.add("scalar", lambda e, h=h: e.activation(
                    out=A["es"][:, :], in_=A["sinkb"][:, h:h + 1], func=AF.Exp,
                    bias=A["nm2"][:, :], scale=1.0))
                seq.addf("vector", lambda v: v.tensor_tensor(
                    out=A["dsum"][:, :], in0=A["dsum"][:, :], in1=A["es"][:, :], op=ALU.add))
                seq.addf("vector", lambda v: v.reciprocal(out=A["rr"][:, :], in_=A["dsum"][:, :]))
                seq.addf("vector", lambda v: v.tensor_scalar(
                    out=A["E"][:, :], in0=A["E"][:, :], scalar1=A["rr"][:, :], scalar2=None,
                    op0=ALU.mult))
                emit_transposes(seq, PT, [
                    (A["E"][:, sc_ * 128:(sc_ + 1) * 128],
                     A["eT"][:, sc_ * 128:(sc_ + 1) * 128]) for sc_ in range(5)])
                for sc_ in range(5):
                    seq.add("tensor", lambda t, sc_=sc_, qt=qt: t.matmul(
                        PA[:, 1024:1152],
                        A["vsb"][:, (qt + sc_) * 128: (qt + sc_ + 1) * 128],
                        A["eT"][:, sc_ * 128:(sc_ + 1) * 128],
                        start=(sc_ == 0), stop=(sc_ == 4)))
                seq.add("vector", lambda v, h=h, qt=qt: v.tensor_copy(
                    out=A["attnT"][:, h * 512 + qt * 128: h * 512 + (qt + 1) * 128],
                    in_=PA[:, 1024:1152]))

        if dbg and b == 0:
            seq.dma("gpsimd", dbgt["kT"], A["kT"][:, :])
            seq.dma("gpsimd", dbgt["vsb"], A["vsb"][:, :])
            seq.dma("gpsimd", dbgt["qT"], A["qT"][:, :])
            seq.dma("gpsimd", dbgt["attnT"], A["attnT"][:, :])
            seq.dma("gpsimd", dbgt["xlnT"], A["xlnT"][:, :])
            seq.dma("gpsimd", dbgt["E"], A["E"][:, :])
            seq.dma("gpsimd", dbgt["smb0"], A["smb"][:, :])
        for g in range(4):
            for ic in range(4):
                seq.dma("gpsimd", A["w1g"][:, ic * 2048:(ic + 1) * 2048],
                        w1[g, ic * 128:(ic + 1) * 128, :])
            for oc in range(16):
                for ic in range(4):
                    seq.add("tensor", lambda t, oc=oc, ic=ic, g=g: t.matmul(
                        PB[:, 0:512],
                        A["w1g"][:, ic * 2048 + oc * 128: ic * 2048 + (oc + 1) * 128],
                        A["attnT"][:, (4 * g + ic) * 512:(4 * g + ic + 1) * 512],
                        start=(ic == 0), stop=(ic == 3)))
                seq.add("vector", lambda v, oc=oc: v.tensor_copy(
                    out=A["out1T"][:, oc * 512:(oc + 1) * 512], in_=PB[:, 0:512]))
            for quar in range(4):
                for gic in range(16):
                    seq.dma("gpsimd", A["w2gh"][:, gic * 512:(gic + 1) * 512],
                            w2[g * 2048 + gic * 128: g * 2048 + (gic + 1) * 128,
                               quar * 512:(quar + 1) * 512])
                for tt in range(4):
                    for gic in range(16):
                        seq.add("tensor", lambda t, gic=gic, tt=tt: t.matmul(
                            PB[:, 512:1024],
                            A["out1T"][:, gic * 512 + tt * 128: gic * 512 + (tt + 1) * 128],
                            A["w2gh"][:, gic * 512:(gic + 1) * 512],
                            start=(gic == 0), stop=(gic == 15)))
                    if g == 0:
                        seq.add("vector", lambda v, tt=tt, quar=quar: v.tensor_copy(
                            out=A["y2acc"][:, tt * 2048 + quar * 512: tt * 2048 + (quar + 1) * 512],
                            in_=PB[:, 512:1024]))
                    else:
                        seq.add("vector", lambda v, tt=tt, quar=quar: v.tensor_tensor(
                            out=A["y2acc"][:, tt * 2048 + quar * 512: tt * 2048 + (quar + 1) * 512],
                            in0=A["y2acc"][:, tt * 2048 + quar * 512: tt * 2048 + (quar + 1) * 512],
                            in1=PB[:, 512:1024], op=ALU.add))

        if dbg and b == 0:
            seq.dma("gpsimd", dbgt["out1T"], A["out1T"][:, :])
            seq.dma("gpsimd", d_y2, A["y2acc"][:, :])
        for tt in range(4):
            rt = b * 4 + tt
            emit_ln_stats(seq, A, A["y2acc"][:, tt * 2048:(tt + 1) * 2048], A["lnb"])
            seq.dma("sync", A["xtile"][:, :], xs[rt * 128:(rt + 1) * 128, :])
            seq.addf("vector", lambda v, tt=tt: v.tensor_scalar(
                out=A["ysb"][:, :], in0=A["y2acc"][:, tt * 2048:(tt + 1) * 2048],
                scalar1=A["negmean"][:, :], scalar2=A["rstd"][:, :],
                op0=ALU.add, op1=ALU.mult))
            seq.add("vector", lambda v: v.tensor_tensor(
                out=A["xtile"][:, :], in0=A["xtile"][:, :], in1=A["ysb"][:, :], op=ALU.add))
            seq.dma("gpsimd", xo[rt * 128:(rt + 1) * 128, :], A["xtile"][:, :])

    al.close()


def l0attn_host_prep(inputs):
    shared = {
        "wqc": np.ascontiguousarray(np.asarray(inputs["L0_attn_wq_comp"], np.float32)),
        "wqu": np.ascontiguousarray(np.asarray(inputs["L0_attn_wq_up"], np.float32)),
        "wk": np.ascontiguousarray(np.asarray(inputs["L0_attn_wk"], np.float32)),
        "wv": np.ascontiguousarray(np.asarray(inputs["L0_attn_wv"], np.float32)),
        "w1": np.ascontiguousarray(np.asarray(inputs["L0_attn_out_w1"], np.float32)),
        "w2": np.ascontiguousarray(np.asarray(inputs["L0_attn_out_w2"], np.float32)),
        "qnw": np.ascontiguousarray(np.broadcast_to(
            np.asarray(inputs["L0_attn_q_norm_w"], np.float32)[None, :], (128, 128))),
        "knw": np.ascontiguousarray(np.broadcast_to(
            np.asarray(inputs["L0_attn_k_norm_w"], np.float32)[None, :], (128, 128))),
        "sinkb": np.ascontiguousarray(np.broadcast_to(
            np.asarray(inputs["L0_attn_sink_logit"], np.float32)[None, :], (128, 16))),
    }
    inv = 1.0 / (10000.0 ** (np.arange(0, 64, 2, dtype=np.float32) / 64.0))
    def rope(p0):
        pos = (p0 + np.arange(512, dtype=np.float32))[:, None]
        ang = pos * inv[None, :]
        return np.cos(ang).astype(np.float32), np.sin(ang).astype(np.float32)
    i_ = np.arange(128)[:, None]
    jj = np.arange(640)[None, :]
    base_valid = (jj > i_) & (jj <= i_ + 512)
    per_core = []
    for c in range(8):
        m = np.empty((4, 128, 640), np.float32)
        for qt in range(4):
            valid = base_valid.copy()
            if c == 0:
                valid = valid & (jj >= 512 - qt * 128)
            m[qt] = np.where(valid, 0.0, -1e30)
        cos_o, sin_o = rope(512.0 * c)
        cos_h, sin_h = rope(512.0 * (c - 1))
        per_core.append({"mask": m, "cos_o": cos_o, "sin_o": sin_o,
                         "cos_h": cos_h, "sin_h": sin_h})
    return shared, per_core


def run_l0attn(x, inputs, n_cores=8, dbg=False):
    shared, per_core = l0attn_host_prep(inputs)
    raise NotImplementedError
    in_maps = []
    for c in range(n_cores):
        xsh = np.ascontiguousarray(
            x[:, c * 512:(c + 1) * 512, :].reshape(2048, 2048)).astype(np.float32)
        if c == 0:
            xhh = np.zeros((2048, 2048), np.float32)
        else:
            xhh = np.ascontiguousarray(
                x[:, (c - 1) * 512:c * 512, :].reshape(2048, 2048)).astype(np.float32)
        m = {"xs": xsh, "xh": xhh}
        m.update(shared)
        m.update(per_core[c])
        in_maps.append(m)
    res = run_bass_kernel_spmd(nc, in_maps, core_ids=list(range(n_cores)))
    out = np.empty((4, 4096, 2048), np.float32)
    for c in range(n_cores):
        out[:, c * 512:(c + 1) * 512, :] = res.results[c]["xo"].reshape(4, 512, 2048)
    if dbg:
        return out, res.results
    return out


def build_fused():
    nc = bass.Bass("TRN2", target_bir_lowering=False)
    seq = Seq(nc)
    xs0 = nc.dram_tensor("xs", [2048, 2048], F32, kind="ExternalInput").ap()
    x1d = nc.dram_tensor("x1d", [2048, 2048], F32).ap()
    x2d = nc.dram_tensor("x2d", [2048, 2048], F32).ap()
    x3d = nc.dram_tensor("x3d", [2048, 2048], F32).ap()
    xo = nc.dram_tensor("xo", [2048, 2048], F32, kind="ExternalOutput").ap()
    emit_l0attn(nc, seq, xs0, x1d, pfx="a_")
    emit_l0ffn(nc, seq, x1d, x2d, pfx="b_")
    emit_l1attn(nc, seq, x2d, x3d, pfx="c_")
    emit_l1ffn(nc, seq, x3d, xo, pfx="d_")
    seq.build()
    return nc


def kernel(**inputs):
    x = np.ascontiguousarray(np.asarray(inputs["L0_attn_h"], np.float32))
    n_cores = 8
    shared_a, per_core_a = l0attn_host_prep(inputs)
    shared_b = l0ffn_host_prep(inputs)
    shared_c = {
        "wqkvT": np.ascontiguousarray(np.asarray(inputs["L1_attn_weight_qkv"], np.float32).T),
        "wpT": np.ascontiguousarray(np.asarray(inputs["L1_attn_weight_proj"], np.float32).T),
    }
    shared_d = l1ffn_host_prep(inputs)
    nc = build_fused()
    in_maps = []
    for c in range(n_cores):
        xsh = np.ascontiguousarray(
            x[:, c * 512:(c + 1) * 512, :].reshape(2048, 2048)).astype(np.float32)
        if c == 0:
            xhh = np.zeros((2048, 2048), np.float32)
        else:
            xhh = np.ascontiguousarray(
                x[:, (c - 1) * 512:c * 512, :].reshape(2048, 2048)).astype(np.float32)
        m = {"xs": xsh, "xh": xhh}
        m.update(shared_a)
        m.update(per_core_a[c])
        m.update(shared_b)
        m.update(shared_c)
        m.update(shared_d)
        in_maps.append(m)
    res = run_bass_kernel_spmd(nc, in_maps, core_ids=list(range(n_cores)))
    out = np.empty((4, 4096, 2048), np.float32)
    for c in range(n_cores):
        out[:, c * 512:(c + 1) * 512, :] = res.results[c]["xo"].reshape(4, 512, 2048)
    return out


_NC_CACHE = {}


def _get_nc(name, builder):
    if name not in _NC_CACHE:
        _NC_CACHE[name] = builder()
    return _NC_CACHE[name]


def kernel_multilaunch(**inputs):
    x = np.ascontiguousarray(np.asarray(inputs["L0_attn_h"], np.float32))
    x = run_l0attn(x, inputs)
    x = run_l0ffn(x, inputs)
    x = run_l1attn(x, inputs)
    x = run_l1ffn(x, inputs)
    return np.ascontiguousarray(x.astype(np.float32))


# version 07
# speedup vs baseline: 1.0906x; 1.0260x over previous
import numpy as np
import concourse.bass as bass
import concourse.mybir as mybir
from concourse.bass_utils import run_bass_kernel_spmd

F32 = mybir.dt.float32
BF16 = mybir.dt.bfloat16
I32 = mybir.dt.int32
F16 = mybir.dt.float16
AF = mybir.ActivationFunctionType
ALU = mybir.AluOpType
AX = mybir.AxisListType

ENGINES = ("sync", "gpsimd", "tensor", "vector", "scalar")


class Seq:

    def __init__(self, nc):
        self.nc = nc
        self._dma_counts = {}
        self.items = []

    def add(self, eng, fn, dma=False):
        self.items.append((eng, fn, dma))

    def dma(self, eng, out, in_, **kw):
        self.add(eng, lambda e: e.dma_start(out=out, in_=in_, **kw), dma=True)

    def addf(self, eng, fn):
        self.add(eng, fn)
        self.add("scalar" if eng != "scalar" else "vector", lambda e: e.nop())

    def build(self, extra_engines=()):
        nc = self.nc
        n = len(self.items)
        incs = [False] * n
        for i, (eng, fn, dma) in enumerate(self.items):
            if dma:
                incs[i] = True
            elif i == n - 1 or self.items[i + 1][0] != eng:
                incs[i] = True
        sems = {}
        sem_ctx = []
        for e in ENGINES:
            g = nc.semaphore(f"sq_{e}")
            sems[e] = g.__enter__()
            sem_ctx.append(g)
        counts = {e: 0 for e in ENGINES}
        last_compute = None
        waited = {e: 0 for e in ENGINES}
        import os
        relax = os.environ.get("SEQ_RELAX", "1") == "1"
        plan = []
        for i, (eng, fn, dma) in enumerate(self.items):
            waits = []
            if i > 0:
                peng, pfn, pdma = self.items[i - 1]
                if not relax:
                    if peng != eng or pdma:
                        waits.append((sems[peng], counts[peng]))
                elif dma:
                    if pdma and peng == eng:
                        pass
                    else:
                        waits.append((sems[peng], counts[peng]))
                else:
                    if pdma:
                        for e2 in ("sync", "gpsimd", "scalar", "vector", "tensor"):
                            c2 = self._dma_counts.get(e2, 0)
                            if c2 > waited.get(("c", eng, e2), 0):
                                waits.append((sems[e2], counts[e2]))
                                waited[("c", eng, e2)] = c2
                        if last_compute is not None and last_compute[0] != eng:
                            waits.append((sems[last_compute[0]], last_compute[1]))
                    elif peng != eng:
                        waits.append((sems[peng], counts[peng]))
            inc = None
            if incs[i]:
                amt = 16 if dma else 1
                counts[eng] += amt
                inc = (sems[eng], amt)
                if dma:
                    self._dma_counts[eng] = self._dma_counts.get(eng, 0) + 16
            if not dma:
                last_compute = (eng, counts[eng])
            plan.append((waits, inc))

        per_eng = {e: [] for e in ENGINES}
        for i, (eng, fn, dma) in enumerate(self.items):
            per_eng[eng].append((self.items[i], plan[i]))

        waiter = "gpsimd"
        final_waits = [(sems[e], counts[e]) for e in ENGINES
                       if counts[e] > 0 and e != waiter]
        final_self = counts[waiter]

        with nc.Block() as block:
            for e in ENGINES:
                entries = per_eng[e]
                if not entries and e != waiter:
                    continue

                def make(entries, is_waiter):
                    def f(engobj):
                        for (eng, fn, dma), (waits, inc) in entries:
                            for (sem, val) in waits:
                                if val > 0:
                                    engobj.wait_ge(sem, val)
                            inst = fn(engobj)
                            if inc is not None:
                                inst.then_inc(inc[0], inc[1])
                        if is_waiter:
                            for (sem, val) in final_waits:
                                engobj.wait_ge(sem, val)
                            if final_self > 0:
                                engobj.wait_ge(sems[waiter], final_self)
                    return f

                getattr(block, e)(make(entries, e == waiter))
        for g in sem_ctx:
            g.__exit__(None, None, None)
        return self


class Sbuf:

    def __init__(self, nc, pfx=""):
        self.nc = nc
        self.pfx = pfx
        self.ctx = []

    def sb(self, name, shape, dt):
        g = self.nc.sbuf_tensor(self.pfx + name, shape, dt)
        t = g.__enter__()
        self.ctx.append(g)
        return t

    def ps(self, name, shape, dt=F32):
        g = self.nc.psum_tensor(self.pfx + name, shape, dt)
        t = g.__enter__()
        self.ctx.append(g)
        return t

    def close(self):
        for g in reversed(self.ctx):
            g.__exit__(None, None, None)


def emit_transposes(seq, PT, pairs, batch=8):
    for base in range(0, len(pairs), batch):
        grp = pairs[base:base + batch]

        def tf(t, grp=grp, ident=_IDENT[0]):
            r = None
            for s, (src, dst) in enumerate(grp):
                r = t.transpose(PT[:, s * 128:(s + 1) * 128], src, ident)
            return r
        seq.add("tensor", tf)

        def vf(v, grp=grp):
            r = None
            for s, (src, dst) in enumerate(grp):
                r = v.tensor_copy(out=dst, in_=PT[:, s * 128:(s + 1) * 128])
            return r
        seq.add("vector", vf)


_IDENT = [None]


def emit_consts(seq, A):
    def f(v):
        v.memset(A["c_eps5"][:, :], 1e-5)
        v.memset(A["c_eps6"][:, :], 1e-6)
        return v.memset(A["c_m05"][:, :], -0.5)
    seq.add("vector", f)

    seq.add("gpsimd", lambda g: g.memset(A["ident"][:, :], 1.0))
    seq.add("gpsimd", lambda g: g.affine_select(
        out=A["ident"][:, :], in_=A["ident"][:, :],
        pattern=[[1, 128]], compare_op=ALU.is_equal, fill=0.0,
        base=0, channel_multiplier=-1))
    _IDENT[0] = A["ident"][:, :]


def emit_ln_stats(seq, A, src_ap, scratch, pfx=""):
    D = src_ap.shape[-1]
    s_, ssq = A["s_"], A["ssq"]
    mean, negmean, t1, var, rstd = A["mean"], A["negmean"], A["t1"], A["var"], A["rstd"]

    seq.add("scalar", lambda e: e.activation(out=scratch[:, 0:D], in_=src_ap, func=AF.Copy,
                                             accum_out=s_[:, :]))
    seq.add("scalar", lambda e: e.activation(out=scratch[:, 0:D], in_=src_ap, func=AF.Square,
                                             accum_out=ssq[:, :]))

    def dve(v):
        v.tensor_scalar(out=mean[:, :], in0=s_[:, :], scalar1=1.0 / D, scalar2=None, op0=ALU.mult)
        v.tensor_scalar(out=negmean[:, :], in0=s_[:, :], scalar1=-1.0 / D, scalar2=None, op0=ALU.mult)
        v.tensor_scalar(out=t1[:, :], in0=ssq[:, :], scalar1=1.0 / D, scalar2=None, op0=ALU.mult)
        return v.scalar_tensor_tensor(out=var[:, :], in0=mean[:, :], scalar=negmean[:, :],
                                      in1=t1[:, :], op0=ALU.mult, op1=ALU.add)
    seq.add("vector", dve)

    def act2(e):
        e.activation(out=var[:, :], in_=var[:, :], func=AF.Ln, bias=A["c_eps5"][:, :], scale=1.0)
        return e.activation(out=rstd[:, :], in_=var[:, :], func=AF.Exp, bias=0.0,
                            scale=A["c_m05"][:, :])
    seq.add("scalar", act2)


def emit_l1ffn(nc, seq, xs, xo, pfx="d_", dbg=False):
    TOK = 2048
    D = 2048
    HID = 5632
    HC = HID // 128

    wgT = nc.dram_tensor("wgT", [D, HID], I32, kind="ExternalInput").ap()
    wuT = nc.dram_tensor("wuT", [D, HID], I32, kind="ExternalInput").ap()
    wdT = nc.dram_tensor("wdT", [HID, D], I32, kind="ExternalInput").ap()
    sg = nc.dram_tensor("sg", [128, HC], F32, kind="ExternalInput").ap()
    su = nc.dram_tensor("su", [128, HC], F32, kind="ExternalInput").ap()
    sdbc = nc.dram_tensor("sdbc", [128, D], F32, kind="ExternalInput").ap()
    if dbg:
        d_xlnT = nc.dram_tensor("d_xlnT", [128, 16 * TOK], BF16, kind="ExternalOutput").ap()
        d_hT = nc.dram_tensor("d_hT", [128, 44 * 1024], BF16, kind="ExternalOutput").ap()
        d_y = nc.dram_tensor("d_y", [128, D], F32, kind="ExternalOutput").ap()
        d_wg = nc.dram_tensor("d_wg", [128, 2048], BF16, kind="ExternalOutput").ap()
        d_gs = nc.dram_tensor("d_gs", [128, 512], BF16, kind="ExternalOutput").ap()
        d_up = nc.dram_tensor("d_up", [128, 512], BF16, kind="ExternalOutput").ap()
        d_pg = nc.dram_tensor("d_pg", [128, 1024], F32, kind="ExternalOutput").ap()

    al = Sbuf(nc, pfx)
    A = {}
    A["xlnT"] = al.sb("xlnT", [128, 16 * TOK], BF16)
    A["hT"] = al.sb("hT", [128, HC * 1024], BF16)
    A["xtile"] = al.sb("xtile", [128, D], F32)
    A["lnb"] = al.sb("lnb", [128, D], BF16)
    A["scratch"] = A["lnb"]
    A["wcol"] = al.sb("wcol", [128, 16 * 128], I32)
    A["wg_bf"] = al.sb("wg_bf", [128, 16 * 128], BF16)
    A["wu_bf"] = al.sb("wu_bf", [128, 16 * 128], BF16)
    A["wd_i"] = A["wcol"]
    A["wd_bf"] = al.sb("wd_bf", [128, D], BF16)
    A["sg_s"] = al.sb("sg_s", [128, HC], F32)
    A["su_s"] = al.sb("su_s", [128, HC], F32)
    A["sdbc_s"] = al.sb("sdbc_s", [128, D], F32)
    A["gsil"] = al.sb("gsil", [128, 512], BF16)
    A["upb"] = al.sb("upb", [128, 512], BF16)
    A["ysb"] = al.sb("ysb", [128, D], F32)
    A["ident"] = al.sb("ident", [128, 128], BF16)
    A["sprt"] = al.sb("sprt", [128, 4], F32)
    A["ssqp"] = al.sb("ssqp", [128, 4], F32)
    for nm in ["c_eps5", "c_eps6", "c_m05", "s_", "ssq", "mean", "negmean", "t1", "var", "rstd"]:
        A[nm] = al.sb(nm, [128, 1], F32)
    P1 = al.ps("P1", [128, 2048], F32)
    PT = al.ps("PT", [128, 1024], BF16)

    emit_consts(seq, A)
    seq.dma("sync", A["sg_s"][:, :], sg)
    seq.dma("sync", A["su_s"][:, :], su)
    seq.dma("sync", A["sdbc_s"][:, :], sdbc)

    for rt in range(16):
        seq.dma("sync", A["xtile"][:, :], xs[rt * 128:(rt + 1) * 128, :])
        emit_ln_stats(seq, A, A["xtile"][:, :], A["scratch"])
        seq.add("vector", lambda v, A=A: v.tensor_scalar(
            out=A["lnb"][:, :], in0=A["xtile"][:, :], scalar1=A["negmean"][:, :],
            scalar2=A["rstd"][:, :], op0=ALU.add, op1=ALU.mult))
        emit_transposes(seq, PT, [
            (A["lnb"][:, c * 128:(c + 1) * 128],
             A["xlnT"][:, c * TOK + rt * 128: c * TOK + (rt + 1) * 128])
            for c in range(16)])

    for grp in range(2):
        for hc in range(HC):
            for (wT, wbf) in ((wgT, "wg_bf"), (wuT, "wu_bf")):
                for q in range(4):
                    seq.dma("sync",
                            A["wcol"][:, q * 4 * 128: (q + 1) * 4 * 128]
                            .rearrange("p (c h) -> p c h", c=4),
                            wT[q * 512:(q + 1) * 512, hc * 128:(hc + 1) * 128]
                            .rearrange("(c p) h -> p c h", p=128))
                seq.add("vector", lambda v, A=A, wbf=wbf: v.tensor_scalar(
                    out=A[wbf][:, :], in0=A["wcol"][:, :], scalar1=128.0, scalar2=None,
                    op0=ALU.subtract))
            for (wbf, off) in (("wg_bf", 0), ("wu_bf", 1024)):
                for kc in range(16):
                    for tg in range(2):
                        seq.add("tensor", lambda t, A=A, wbf=wbf, kc=kc, tg=tg, off=off, grp=grp: t.matmul(
                            P1[:, off + tg * 512: off + (tg + 1) * 512],
                            A[wbf][:, kc * 128:(kc + 1) * 128],
                            A["xlnT"][:, kc * TOK + grp * 1024 + tg * 512: kc * TOK + grp * 1024 + (tg + 1) * 512],
                            start=(kc == 0), stop=(kc == 15)))
            for tg in range(2):
                seq.add("scalar", lambda e, A=A, tg=tg, hc=hc: e.activation(
                    out=A["gsil"][:, :], in_=P1[:, tg * 512:(tg + 1) * 512], func=AF.Silu,
                    bias=0.0, scale=A["sg_s"][:, hc:hc + 1]))
                seq.add("vector", lambda v, A=A, tg=tg, hc=hc: v.tensor_scalar(
                    out=A["upb"][:, :], in0=P1[:, 1024 + tg * 512: 1024 + (tg + 1) * 512],
                    scalar1=A["su_s"][:, hc:hc + 1], scalar2=None, op0=ALU.mult))
                seq.add("vector", lambda v, A=A, tg=tg, hc=hc: v.tensor_tensor(
                    out=A["hT"][:, hc * 1024 + tg * 512: hc * 1024 + (tg + 1) * 512],
                    in0=A["gsil"][:, :], in1=A["upb"][:, :], op=ALU.mult))

        if dbg and grp == 1:
            seq.dma("gpsimd", d_xlnT, A["xlnT"][:, :])
            seq.dma("gpsimd", d_hT, A["hT"][:, :])
            seq.dma("gpsimd", d_wg, A["wg_bf"][:, :])
            seq.dma("gpsimd", d_gs, A["gsil"][:, :])
            seq.dma("gpsimd", d_up, A["upb"][:, :])
            seq.add("vector", lambda v, A=A: v.tensor_copy(out=A["ysb"][:, 0:1024], in_=P1[:, 0:1024]))
            seq.dma("gpsimd", d_pg, A["ysb"][:, 0:1024])
        for tt_ in range(8):
            rt = grp * 8 + tt_
            for hc in range(HC):
                seq.dma("sync", A["wd_i"][:, :], wdT[hc * 128:(hc + 1) * 128, :])
                seq.add("vector", lambda v, A=A: v.tensor_scalar(
                    out=A["wd_bf"][:, :], in0=A["wd_i"][:, :], scalar1=128.0, scalar2=None,
                    op0=ALU.subtract))
                for nn in range(4):
                    seq.add("tensor", lambda t, A=A, hc=hc, tt_=tt_, nn=nn: t.matmul(
                        P1[:, nn * 512:(nn + 1) * 512],
                        A["hT"][:, hc * 1024 + tt_ * 128: hc * 1024 + (tt_ + 1) * 128],
                        A["wd_bf"][:, nn * 512:(nn + 1) * 512],
                        start=(hc == 0), stop=(hc == HC - 1)))
            for nn in range(4):
                seq.add("vector", lambda v, A=A, nn=nn: v.scalar_tensor_tensor(
                    out=A["ysb"][:, nn * 512:(nn + 1) * 512], in0=P1[:, nn * 512:(nn + 1) * 512],
                    scalar=1.0, in1=A["sdbc_s"][:, nn * 512:(nn + 1) * 512],
                    op0=ALU.mult, op1=ALU.mult))
            if dbg and rt == 15:
                seq.dma("gpsimd", d_y, A["ysb"][:, :])
            emit_ln_stats(seq, A, A["ysb"][:, :], A["scratch"])
            seq.dma("sync", A["xtile"][:, :], xs[rt * 128:(rt + 1) * 128, :])
            seq.add("vector", lambda v, A=A: v.tensor_scalar(
                out=A["ysb"][:, :], in0=A["ysb"][:, :], scalar1=A["negmean"][:, :],
                scalar2=A["rstd"][:, :], op0=ALU.add, op1=ALU.mult))
            seq.add("vector", lambda v, A=A: v.tensor_tensor(
                out=A["xtile"][:, :], in0=A["xtile"][:, :], in1=A["ysb"][:, :], op=ALU.add))
            seq.dma("gpsimd", xo[rt * 128:(rt + 1) * 128, :], A["xtile"][:, :])

    al.close()


def l1ffn_host_prep(inputs):
    wg = inputs["L1_ffn_wq_gate"]
    wu = inputs["L1_ffn_wq_up"]
    wd = inputs["L1_ffn_wq_down"]
    shared = {
        "wgT": np.ascontiguousarray(wg.T).astype(np.int32),
        "wuT": np.ascontiguousarray(wu.T).astype(np.int32),
        "wdT": np.ascontiguousarray(wd.T).astype(np.int32),
        "sg": np.ascontiguousarray(
            np.asarray(inputs["L1_ffn_s_gate"], np.float32).reshape(44, 128).T),
        "su": np.ascontiguousarray(
            np.asarray(inputs["L1_ffn_s_up"], np.float32).reshape(44, 128).T),
        "sdbc": np.ascontiguousarray(np.broadcast_to(
            np.asarray(inputs["L1_ffn_s_down"], np.float32)[None, :], (128, 2048))),
    }
    return shared


def run_l1ffn(x, inputs, n_cores=8, dbg=False):
    shared = l1ffn_host_prep(inputs)
    raise NotImplementedError
    in_maps = []
    for c in range(n_cores):
        xs = np.ascontiguousarray(
            x[:, c * 512:(c + 1) * 512, :].reshape(2048, 2048)).astype(np.float32)
        m = {"xs": xs}
        m.update(shared)
        in_maps.append(m)
    res = run_bass_kernel_spmd(nc, in_maps, core_ids=list(range(n_cores)))
    out = np.empty((4, 4096, 2048), np.float32)
    for c in range(n_cores):
        out[:, c * 512:(c + 1) * 512, :] = res.results[c]["xo"].reshape(4, 512, 2048)
    if dbg:
        return out, res.results
    return out


def emit_lnT(seq, A, xs, rt, dest, dest_col, dest_stride, PT):
    seq.dma("sync", A["xtile"][:, :], xs[rt * 128:(rt + 1) * 128, :])
    emit_ln_stats(seq, A, A["xtile"][:, :], A["lnb"])
    seq.add("vector", lambda v: v.tensor_scalar(
        out=A["lnb"][:, :], in0=A["xtile"][:, :], scalar1=A["negmean"][:, :],
        scalar2=A["rstd"][:, :], op0=ALU.add, op1=ALU.mult))
    emit_transposes(seq, PT, [
        (A["lnb"][:, c * 128:(c + 1) * 128],
         dest[:, c * dest_stride + dest_col: c * dest_stride + dest_col + 128])
        for c in range(16)])


def emit_dequant_fp4(seq, A, pk_ap, scf_ap, ngroups, C, wc_dest):
    for j in range(8):
        seq.add("vector", lambda v, j=j: v.tensor_scalar(
            out=A["dq_i"][:, 0:C], in0=pk_ap, scalar1=4 * j, scalar2=7,
            op0=ALU.logical_shift_right, op1=ALU.bitwise_and))
        seq.add("vector", lambda v: v.tensor_copy(
            out=A["dq_mf"][:, 0:C], in_=A["dq_i"][:, 0:C]))
        seq.add("vector", lambda v, j=j: v.tensor_scalar(
            out=A["dq_i"][:, 0:C], in0=pk_ap, scalar1=4 * j + 3, scalar2=1,
            op0=ALU.logical_shift_right, op1=ALU.bitwise_and))
        seq.add("vector", lambda v: v.tensor_scalar(
            out=A["dq_sgn"][:, 0:C], in0=A["dq_i"][:, 0:C], scalar1=-2.0, scalar2=1.0,
            op0=ALU.mult, op1=ALU.add))
        seq.add("scalar", lambda e: e.activation(
            out=A["dq_r1"][:, 0:C], in_=A["dq_mf"][:, 0:C], func=AF.Relu,
            bias=A["c_m2"][:, :], scale=0.5))
        seq.add("scalar", lambda e: e.activation(
            out=A["dq_r2"][:, 0:C], in_=A["dq_mf"][:, 0:C], func=AF.Relu,
            bias=A["c_m6"][:, :], scale=1.0))
        seq.add("vector", lambda v: v.scalar_tensor_tensor(
            out=A["dq_v"][:, 0:C], in0=A["dq_mf"][:, 0:C], scalar=0.5,
            in1=A["dq_r1"][:, 0:C], op0=ALU.mult, op1=ALU.add))
        seq.add("vector", lambda v: v.tensor_tensor(
            out=A["dq_v"][:, 0:C], in0=A["dq_v"][:, 0:C], in1=A["dq_r2"][:, 0:C],
            op=ALU.add))
        seq.add("vector", lambda v: v.tensor_tensor(
            out=A["dq_v"][:, 0:C], in0=A["dq_v"][:, 0:C], in1=A["dq_sgn"][:, 0:C],
            op=ALU.mult))
        seq.add("vector", lambda v, j=j: v.scalar_tensor_tensor(
            out=wc_dest[:, j::8].rearrange("p (g r) -> p g r", g=ngroups),
            in0=A["dq_v"][:, 0:C].rearrange("p (g r) -> p g r", g=ngroups),
            scalar=1.0,
            in1=scf_ap.unsqueeze(2).broadcast_to([128, ngroups, 16]),
            op0=ALU.mult, op1=ALU.mult))


def emit_l0ffn(nc, seq, xs, xo, pfx="b_"):
    TOK = 2048
    D = 2048
    I = 5632
    GUC = 88
    IC = 44

    gup = nc.dram_tensor("gup", [2 * I, 256], I32, kind="ExternalInput").ap()
    gusT = nc.dram_tensor("gusT", [2 * I, 16], F16, kind="ExternalInput").ap()
    dnp = nc.dram_tensor("dnp", [D, 704], I32, kind="ExternalInput").ap()
    dnsT = nc.dram_tensor("dnsT", [D, IC], F16, kind="ExternalInput").ap()
    wguT_d = nc.dram_tensor("wguT_d", [D, 2 * I], BF16).ap()
    wdnT_d = nc.dram_tensor("wdnT_d", [I, D], BF16).ap()

    al = Sbuf(nc, pfx)
    A = {}
    A["xlnT"] = al.sb("xlnT", [128, 16 * 1024], BF16)
    A["hT"] = al.sb("hT", [128, IC * 1024], BF16)
    A["xtile"] = al.sb("xtile", [128, D], F32)
    A["lnb"] = al.sb("lnb", [128, D], BF16)
    A["ysb"] = al.sb("ysb", [128, D], F32)
    A["pk"] = al.sb("pk", [128, 704], I32)
    A["sc16"] = al.sb("sc16", [128, IC], F16)
    A["scf"] = al.sb("scf", [128, IC], F32)
    A["dq_i"] = al.sb("dq_i", [128, 704], I32)
    A["dq_mf"] = al.sb("dq_mf", [128, 704], F32)
    A["dq_sgn"] = al.sb("dq_sgn", [128, 704], F32)
    A["dq_r1"] = al.sb("dq_r1", [128, 704], F32)
    A["dq_r2"] = al.sb("dq_r2", [128, 704], F32)
    A["dq_v"] = al.sb("dq_v", [128, 704], F32)
    A["Wc"] = al.sb("Wc", [128, I], BF16)
    A["WcT"] = al.sb("WcT", [128, IC * 128], BF16)
    A["wstage_g"] = al.sb("wstage_g", [128, 16 * 128], BF16)
    A["wstage_u"] = al.sb("wstage_u", [128, 16 * 128], BF16)
    A["wdstage"] = al.sb("wdstage", [128, D], BF16)
    A["gsil"] = al.sb("gsil", [128, 512], BF16)
    A["upb"] = al.sb("upb", [128, 512], BF16)
    A["ident"] = al.sb("ident", [128, 128], BF16)
    for nm in ["c_eps5", "c_eps6", "c_m05", "c_m2", "c_m6", "c_e11",
               "s_", "ssq", "mean", "negmean", "t1", "var", "rstd"]:
        A[nm] = al.sb(nm, [128, 1], F32)
    PB = al.ps("PB", [128, 2048], F32)
    PT = al.ps("PT", [128, 1024], BF16)

    emit_consts(seq, A)
    seq.add("vector", lambda v: v.memset(A["c_m2"][:, :], -2.0))
    seq.add("vector", lambda v: v.memset(A["c_m6"][:, :], -6.0))
    seq.add("vector", lambda v: v.memset(A["c_e11"][:, :], 1.000005e-11))

    for gc in range(GUC):
        seq.dma("sync", A["pk"][:, 0:256], gup[gc * 128:(gc + 1) * 128, :])
        seq.dma("sync", A["sc16"][:, 0:16], gusT[gc * 128:(gc + 1) * 128, :])
        seq.add("vector", lambda v: v.tensor_copy(out=A["scf"][:, 0:16], in_=A["sc16"][:, 0:16]))
        emit_dequant_fp4(seq, A, A["pk"][:, 0:256], A["scf"][:, 0:16], 16, 256,
                         A["Wc"][:, 0:2048])
        emit_transposes(seq, PT, [
            (A["Wc"][:, kc * 128:(kc + 1) * 128], A["WcT"][:, kc * 128:(kc + 1) * 128])
            for kc in range(16)])
        for half in range(2):
            seq.dma("gpsimd",
                    wguT_d[half * 1024:(half + 1) * 1024, gc * 128:(gc + 1) * 128]
                    .rearrange("(k p) o -> p k o", p=128),
                    A["WcT"][:, half * 1024:(half + 1) * 1024]
                    .rearrange("p (k o) -> p k o", k=8))

    for oc in range(16):
        seq.dma("sync", A["pk"][:, 0:704], dnp[oc * 128:(oc + 1) * 128, :])
        seq.dma("sync", A["sc16"][:, 0:IC], dnsT[oc * 128:(oc + 1) * 128, :])
        seq.add("vector", lambda v: v.tensor_copy(out=A["scf"][:, 0:IC], in_=A["sc16"][:, 0:IC]))
        emit_dequant_fp4(seq, A, A["pk"][:, 0:704], A["scf"][:, 0:IC], IC, 704,
                         A["Wc"][:, 0:I])
        emit_transposes(seq, PT, [
            (A["Wc"][:, kc * 128:(kc + 1) * 128], A["WcT"][:, kc * 128:(kc + 1) * 128])
            for kc in range(IC)])
        for half in range(4):
            seq.dma("gpsimd",
                    wdnT_d[half * 1408:(half + 1) * 1408, oc * 128:(oc + 1) * 128]
                    .rearrange("(k p) o -> p k o", p=128),
                    A["WcT"][:, half * 1408:(half + 1) * 1408]
                    .rearrange("p (k o) -> p k o", k=11))

    for grp in range(2):
        for rtl in range(8):
            rt = grp * 8 + rtl
            emit_lnT(seq, A, xs, rt, A["xlnT"], rtl * 128, 1024, PT)
        for gcp in range(IC):
            for (gc2, wst) in ((gcp, "wstage_g"), (IC + gcp, "wstage_u")):
                for half in range(2):
                    seq.dma("sync",
                            A[wst][:, half * 1024:(half + 1) * 1024]
                            .rearrange("p (k o) -> p k o", k=8),
                            wguT_d[half * 1024:(half + 1) * 1024, gc2 * 128:(gc2 + 1) * 128]
                            .rearrange("(k p) o -> p k o", p=128))
            for (wst, off) in (("wstage_g", 0), ("wstage_u", 1024)):
                for kc in range(16):
                    for tg in range(2):
                        seq.add("tensor", lambda t, wst=wst, kc=kc, tg=tg, off=off, grp=grp: t.matmul(
                            PB[:, off + tg * 512: off + (tg + 1) * 512],
                            A[wst][:, kc * 128:(kc + 1) * 128],
                            A["xlnT"][:, kc * 1024 + tg * 512: kc * 1024 + (tg + 1) * 512],
                            start=(kc == 0), stop=(kc == 15)))
            for tg in range(2):
                seq.add("scalar", lambda e, tg=tg: e.activation(
                    out=A["gsil"][:, :], in_=PB[:, tg * 512:(tg + 1) * 512], func=AF.Silu,
                    bias=0.0, scale=1.0))
                seq.add("vector", lambda v, tg=tg: v.tensor_copy(
                    out=A["upb"][:, :], in_=PB[:, 1024 + tg * 512:1024 + (tg + 1) * 512]))
                seq.add("vector", lambda v, tg=tg, gcp=gcp: v.tensor_tensor(
                    out=A["hT"][:, gcp * 1024 + tg * 512: gcp * 1024 + (tg + 1) * 512],
                    in0=A["gsil"][:, :], in1=A["upb"][:, :], op=ALU.mult))
        for tt_ in range(8):
            rt = grp * 8 + tt_
            for ic in range(IC):
                seq.dma("sync", A["wdstage"][:, :], wdnT_d[ic * 128:(ic + 1) * 128, :])
                for nn in range(4):
                    seq.add("tensor", lambda t, ic=ic, tt_=tt_, nn=nn: t.matmul(
                        PB[:, nn * 512:(nn + 1) * 512],
                        A["hT"][:, ic * 1024 + tt_ * 128: ic * 1024 + (tt_ + 1) * 128],
                        A["wdstage"][:, nn * 512:(nn + 1) * 512],
                        start=(ic == 0), stop=(ic == IC - 1)))
            seq.add("scalar", lambda e: e.activation(
                out=A["ysb"][:, :], in_=PB[:, :], func=AF.Copy, accum_out=A["s_"][:, :]))
            seq.add("scalar", lambda e: e.activation(
                out=A["lnb"][:, :], in_=A["ysb"][:, :], func=AF.Square, accum_out=A["ssq"][:, :]))

            def dve_stats(v):
                v.tensor_scalar(out=A["mean"][:, :], in0=A["s_"][:, :], scalar1=1.0 / D,
                                scalar2=None, op0=ALU.mult)
                v.tensor_scalar(out=A["negmean"][:, :], in0=A["s_"][:, :], scalar1=-1.0 / D,
                                scalar2=None, op0=ALU.mult)
                v.tensor_scalar(out=A["t1"][:, :], in0=A["ssq"][:, :],
                                scalar1=(1.0 + 1e-5) / D, scalar2=None, op0=ALU.mult)
                return v.scalar_tensor_tensor(out=A["var"][:, :], in0=A["mean"][:, :],
                                              scalar=A["negmean"][:, :], in1=A["t1"][:, :],
                                              op0=ALU.mult, op1=ALU.add)
            seq.add("vector", dve_stats)

            def act_rstd(e):
                e.activation(out=A["var"][:, :], in_=A["var"][:, :], func=AF.Ln,
                             bias=A["c_e11"][:, :], scale=1.0)
                return e.activation(out=A["rstd"][:, :], in_=A["var"][:, :], func=AF.Exp,
                                    bias=0.0, scale=A["c_m05"][:, :])
            seq.add("scalar", act_rstd)
            seq.dma("sync", A["xtile"][:, :], xs[rt * 128:(rt + 1) * 128, :])
            seq.add("vector", lambda v: v.tensor_scalar(
                out=A["ysb"][:, :], in0=A["ysb"][:, :], scalar1=A["negmean"][:, :],
                scalar2=A["rstd"][:, :], op0=ALU.add, op1=ALU.mult))
            seq.add("vector", lambda v: v.tensor_tensor(
                out=A["xtile"][:, :], in0=A["xtile"][:, :], in1=A["ysb"][:, :], op=ALU.add))
            seq.dma("gpsimd", xo[rt * 128:(rt + 1) * 128, :], A["xtile"][:, :])

    al.close()


def l0ffn_host_prep(inputs):
    return {
        "gup": np.ascontiguousarray(inputs["L0_ffn_gate_up_packed"]).astype(np.int32),
        "gusT": np.ascontiguousarray(np.asarray(inputs["L0_ffn_gate_up_scales"]).T),
        "dnp": np.ascontiguousarray(inputs["L0_ffn_down_packed"]).astype(np.int32),
        "dnsT": np.ascontiguousarray(np.asarray(inputs["L0_ffn_down_scales"]).T),
    }


def run_l0ffn(x, inputs, n_cores=8, dbg=False):
    shared = l0ffn_host_prep(inputs)
    raise NotImplementedError
    in_maps = []
    for c in range(n_cores):
        xsh = np.ascontiguousarray(
            x[:, c * 512:(c + 1) * 512, :].reshape(2048, 2048)).astype(np.float32)
        m = {"xs": xsh}
        m.update(shared)
        in_maps.append(m)
    res = run_bass_kernel_spmd(nc, in_maps, core_ids=list(range(n_cores)))
    out = np.empty((4, 4096, 2048), np.float32)
    for c in range(n_cores):
        out[:, c * 512:(c + 1) * 512, :] = res.results[c]["xo"].reshape(4, 512, 2048)
    if dbg:
        return out, res.results
    return out


def emit_l1attn(nc, seq, xs, xo, pfx="c_", dbg=False):
    TOK, D = 2048, 2048
    SCALE = 1.0 / float(np.sqrt(128.0))

    wqkvT = nc.dram_tensor("wqkvT", [D, 3072], F32, kind="ExternalInput").ap()
    wpT = nc.dram_tensor("wpT", [D, D], F32, kind="ExternalInput").ap()
    if dbg:
        d_qkv = nc.dram_tensor("d_qkv", [128, 4 * 3072], BF16, kind="ExternalOutput").ap()
        d_S = nc.dram_tensor("d_S", [128, 160], F32, kind="ExternalOutput").ap()
        d_E = nc.dram_tensor("d_E", [128, 160], BF16, kind="ExternalOutput").ap()
        d_att = nc.dram_tensor("d_att", [128, 4 * 2048], BF16, kind="ExternalOutput").ap()
        d_mx = nc.dram_tensor("d_mx", [128, 64], F32, kind="ExternalOutput").ap()
        d_Dm = nc.dram_tensor("d_Dm", [128, 160], F32, kind="ExternalOutput").ap()

    al = Sbuf(nc, pfx)
    A = {}
    A["xlnT"] = al.sb("xlnT", [128, 16 * TOK], BF16)
    A["qkv"] = al.sb("qkv", [128, 4 * 3072], BF16)
    A["att_tm"] = al.sb("att_tm", [128, 4 * 2048], BF16)
    A["attnT"] = al.sb("attnT", [128, 16 * 512], BF16)
    A["wq_st"] = al.sb("wq_st", [128, 3072], BF16)
    A["wp_st"] = al.sb("wp_st", [128, 2048], BF16)
    A["prod"] = al.sb("prod", [128, 2048], BF16)
    A["acc"] = al.sb("acc", [128, 2048], F32)
    A["tmp2"] = al.sb("tmp2", [128, 2048], F32)
    A["S"] = al.sb("S", [128, 160], F32)
    A["Dm"] = al.sb("Dm", [128, 160], F32)
    A["E"] = al.sb("E", [128, 160], BF16)
    A["mx"] = al.sb("mx", [128, 64], F32)
    A["den"] = al.sb("den", [128, 64], F32)
    A["rden"] = al.sb("rden", [128, 64], F32)
    A["xtile"] = al.sb("xtile", [128, D], F32)
    A["lnb"] = al.sb("lnb", [128, D], BF16)
    A["ysb"] = al.sb("ysb", [128, D], F32)
    A["ident"] = al.sb("ident", [128, 128], BF16)
    for nm in ["c_eps5", "c_eps6", "c_m05", "s_", "ssq", "mean", "negmean", "t1", "var", "rstd"]:
        A[nm] = al.sb(nm, [128, 1], F32)
    PQ = al.ps("PQ", [128, 3072], F32)
    PT = al.ps("PT", [128, 1024], BF16)

    pairs = [(i, j) for i in range(4) for j in range(i + 1)]

    def qkv4(b):
        return A["qkv"][:, b * 3072:(b + 1) * 3072].rearrange(
            "p (g x d) -> p g x d", g=4, x=6)

    emit_consts(seq, A)

    for rt in range(16):
        emit_lnT(seq, A, xs, rt, A["xlnT"], rt * 128, TOK, PT)

    for pt in range(4):
        for i in range(4):
            rt = i * 4 + pt
            for c in range(16):
                seq.dma("gpsimd", A["wq_st"][:, :], wqkvT[c * 128:(c + 1) * 128, :])
                for nn in range(6):
                    seq.add("tensor", lambda t, c=c, rt=rt, nn=nn: t.matmul(
                        PQ[:, nn * 512:(nn + 1) * 512],
                        A["xlnT"][:, c * TOK + rt * 128: c * TOK + (rt + 1) * 128],
                        A["wq_st"][:, nn * 512:(nn + 1) * 512],
                        start=(c == 0), stop=(c == 15)))
            seq.add("vector", lambda v, i=i: v.tensor_copy(
                out=A["qkv"][:, i * 3072:(i + 1) * 3072], in_=PQ[:, :]))

        for pidx, (i, j) in enumerate(pairs):
            qa = qkv4(i)[:, :, 0:4, :]
            ka = qkv4(j)[:, :, 4:5, :].broadcast_to([128, 4, 4, 128])
            seq.add("vector", lambda v, qa=qa, ka=ka: v.tensor_tensor(
                out=A["prod"][:, :].rearrange("p (g x d) -> p g x d", g=4, x=4),
                in0=qa, in1=ka, op=ALU.mult))
            seq.add("vector", lambda v, pidx=pidx: v.tensor_reduce(
                out=A["S"][:, pidx * 16:(pidx + 1) * 16].rearrange("p (g x) -> p g x", g=4),
                in_=A["prod"][:, :].rearrange("p (g x d) -> p g x d", g=4, x=4),
                axis=AX.X, op=ALU.add))
        def scale_S(v):
            v.nop()
            return v.tensor_scalar(
                out=A["S"][:, :], in0=A["S"][:, :], scalar1=SCALE, scalar2=None, op0=ALU.mult)
        seq.add("vector", scale_S)

        def softmax(v):
            pofs = {(i, j): pairs.index((i, j)) * 16 for (i, j) in pairs}
            for i in range(4):
                mslice = A["mx"][:, i * 16:(i + 1) * 16]
                v.tensor_copy(out=mslice, in_=A["S"][:, pofs[(i, 0)]:pofs[(i, 0)] + 16])
                v.nop()
                for j in range(1, i + 1):
                    v.tensor_tensor(out=mslice, in0=mslice,
                                    in1=A["S"][:, pofs[(i, j)]:pofs[(i, j)] + 16],
                                    op=ALU.max)
                    v.nop()
            for pidx, (i, j) in enumerate(pairs):
                v.tensor_tensor(out=A["Dm"][:, pidx * 16:(pidx + 1) * 16],
                                in0=A["S"][:, pidx * 16:(pidx + 1) * 16],
                                in1=A["mx"][:, i * 16:(i + 1) * 16], op=ALU.subtract)
                v.nop()
            return v.tensor_copy(out=A["den"][:, :], in_=A["mx"][:, :])
        seq.add("vector", softmax)
        seq.add("scalar", lambda e: e.activation(
            out=A["E"][:, :], in_=A["Dm"][:, :], func=AF.Exp, bias=0.0, scale=1.0))

        def denom(v):
            pofs = {(i, j): pairs.index((i, j)) * 16 for (i, j) in pairs}
            for i in range(4):
                dslice = A["den"][:, i * 16:(i + 1) * 16]
                v.tensor_copy(out=dslice, in_=A["E"][:, pofs[(i, 0)]:pofs[(i, 0)] + 16])
                v.nop()
                for j in range(1, i + 1):
                    v.tensor_tensor(out=dslice, in0=dslice,
                                    in1=A["E"][:, pofs[(i, j)]:pofs[(i, j)] + 16], op=ALU.add)
                    v.nop()
            v.nop()
            return v.reciprocal(out=A["rden"][:, :], in_=A["den"][:, :])
        seq.add("vector", denom)

        for i in range(4):
            for j in range(i + 1):
                pidx = pairs.index((i, j))
                va = qkv4(j)[:, :, 5:6, :].broadcast_to([128, 4, 4, 128])
                ea = A["E"][:, pidx * 16:(pidx + 1) * 16].rearrange(
                    "p (g x) -> p g x", g=4).unsqueeze(3).broadcast_to([128, 4, 4, 128])
                dst = A["acc"] if j == 0 else A["tmp2"]
                seq.add("vector", lambda v, va=va, ea=ea, dst=dst: v.tensor_tensor(
                    out=dst[:, :].rearrange("p (g x d) -> p g x d", g=4, x=4),
                    in0=va, in1=ea, op=ALU.mult))
                if j > 0:
                    seq.add("vector", lambda v: v.tensor_tensor(
                        out=A["acc"][:, :], in0=A["acc"][:, :], in1=A["tmp2"][:, :], op=ALU.add))
            ra = A["rden"][:, i * 16:(i + 1) * 16].unsqueeze(2).broadcast_to([128, 16, 128])
            seq.add("vector", lambda v, i=i, ra=ra: v.tensor_tensor(
                out=A["att_tm"][:, i * 2048:(i + 1) * 2048].rearrange("p (h d) -> p h d", h=16),
                in0=A["acc"][:, :].rearrange("p (h d) -> p h d", h=16),
                in1=ra, op=ALU.mult))

        if dbg and pt == 3:
            seq.dma("gpsimd", d_qkv, A["qkv"][:, :])
            seq.dma("gpsimd", d_S, A["S"][:, :])
            seq.dma("gpsimd", d_E, A["E"][:, :])
            seq.dma("gpsimd", d_mx, A["mx"][:, :])
            seq.dma("gpsimd", d_Dm, A["Dm"][:, :])
            seq.dma("gpsimd", d_att, A["att_tm"][:, :])
        emit_transposes(seq, PT, [
            (A["att_tm"][:, i * 2048 + c * 128: i * 2048 + (c + 1) * 128],
             A["attnT"][:, c * 512 + i * 128: c * 512 + (i + 1) * 128])
            for i in range(4) for c in range(16)])

        for i in range(4):
            rt = i * 4 + pt
            for c in range(16):
                seq.dma("gpsimd", A["wp_st"][:, :], wpT[c * 128:(c + 1) * 128, :])
                for nn in range(4):
                    seq.add("tensor", lambda t, i=i, c=c, nn=nn: t.matmul(
                        PQ[:, nn * 512:(nn + 1) * 512],
                        A["attnT"][:, c * 512 + i * 128: c * 512 + (i + 1) * 128],
                        A["wp_st"][:, nn * 512:(nn + 1) * 512],
                        start=(c == 0), stop=(c == 15)))
            seq.add("scalar", lambda e: e.activation(
                out=A["ysb"][:, :], in_=PQ[:, 0:2048], func=AF.Copy, accum_out=A["s_"][:, :]))
            seq.add("scalar", lambda e: e.activation(
                out=A["lnb"][:, :], in_=A["ysb"][:, :], func=AF.Square, accum_out=A["ssq"][:, :]))

            def dve_stats(v):
                v.tensor_scalar(out=A["mean"][:, :], in0=A["s_"][:, :], scalar1=1.0 / D,
                                scalar2=None, op0=ALU.mult)
                v.tensor_scalar(out=A["negmean"][:, :], in0=A["s_"][:, :], scalar1=-1.0 / D,
                                scalar2=None, op0=ALU.mult)
                v.tensor_scalar(out=A["t1"][:, :], in0=A["ssq"][:, :], scalar1=1.0 / D,
                                scalar2=None, op0=ALU.mult)
                return v.scalar_tensor_tensor(out=A["var"][:, :], in0=A["mean"][:, :],
                                              scalar=A["negmean"][:, :], in1=A["t1"][:, :],
                                              op0=ALU.mult, op1=ALU.add)
            seq.add("vector", dve_stats)

            def act_rstd(e):
                e.activation(out=A["var"][:, :], in_=A["var"][:, :], func=AF.Ln,
                             bias=A["c_eps5"][:, :], scale=1.0)
                return e.activation(out=A["rstd"][:, :], in_=A["var"][:, :], func=AF.Exp,
                                    bias=0.0, scale=A["c_m05"][:, :])
            seq.add("scalar", act_rstd)
            seq.dma("sync", A["xtile"][:, :], xs[rt * 128:(rt + 1) * 128, :])
            seq.add("vector", lambda v: v.tensor_scalar(
                out=A["ysb"][:, :], in0=A["ysb"][:, :], scalar1=A["negmean"][:, :],
                scalar2=A["rstd"][:, :], op0=ALU.add, op1=ALU.mult))
            seq.add("vector", lambda v: v.tensor_tensor(
                out=A["xtile"][:, :], in0=A["xtile"][:, :], in1=A["ysb"][:, :], op=ALU.add))
            seq.dma("gpsimd", xo[rt * 128:(rt + 1) * 128, :], A["xtile"][:, :])

    al.close()


def run_l1attn(x, inputs, n_cores=8, dbg=False):
    shared = {
        "wqkvT": np.ascontiguousarray(np.asarray(inputs["L1_attn_weight_qkv"], np.float32).T),
        "wpT": np.ascontiguousarray(np.asarray(inputs["L1_attn_weight_proj"], np.float32).T),
    }
    raise NotImplementedError
    in_maps = []
    for c in range(n_cores):
        xsh = np.ascontiguousarray(
            x[:, c * 512:(c + 1) * 512, :].reshape(2048, 2048)).astype(np.float32)
        m = {"xs": xsh}
        m.update(shared)
        in_maps.append(m)
    res = run_bass_kernel_spmd(nc, in_maps, core_ids=list(range(n_cores)))
    out = np.empty((4, 4096, 2048), np.float32)
    for c in range(n_cores):
        out[:, c * 512:(c + 1) * 512, :] = res.results[c]["xo"].reshape(4, 512, 2048)
    if dbg:
        return out, res.results
    return out


def emit_l0attn(nc, seq, xs, xo, pfx="a_", dbg=False):
    TOK, D = 2048, 2048
    SCALE = 0.08838834764831845
    NEG = -1e30

    xh = nc.dram_tensor("xh", [TOK, D], F32, kind="ExternalInput").ap()
    wqc = nc.dram_tensor("wqc", [D, 512], F32, kind="ExternalInput").ap()
    wqu = nc.dram_tensor("wqu", [512, D], F32, kind="ExternalInput").ap()
    wk = nc.dram_tensor("wk", [D, 128], F32, kind="ExternalInput").ap()
    wv = nc.dram_tensor("wv", [D, 128], F32, kind="ExternalInput").ap()
    w1 = nc.dram_tensor("w1", [4, 512, 2048], F32, kind="ExternalInput").ap()
    w2 = nc.dram_tensor("w2", [8192, 2048], F32, kind="ExternalInput").ap()
    qnw = nc.dram_tensor("qnw", [128, 128], F32, kind="ExternalInput").ap()
    knw = nc.dram_tensor("knw", [128, 128], F32, kind="ExternalInput").ap()
    sinkb = nc.dram_tensor("sinkb", [128, 16], F32, kind="ExternalInput").ap()
    mask = nc.dram_tensor("mask", [4, 128, 640], F32, kind="ExternalInput").ap()
    cos_o = nc.dram_tensor("cos_o", [512, 32], F32, kind="ExternalInput").ap()
    sin_o = nc.dram_tensor("sin_o", [512, 32], F32, kind="ExternalInput").ap()
    cos_h = nc.dram_tensor("cos_h", [512, 32], F32, kind="ExternalInput").ap()
    sin_h = nc.dram_tensor("sin_h", [512, 32], F32, kind="ExternalInput").ap()
    if dbg:
        dbgt = {}
        for nm_, sh in [("kT", [128, 1024]), ("vsb", [128, 1024]), ("qT", [128, 8192]),
                        ("attnT", [128, 8192]), ("out1T", [128, 8192]), ("xlnT", [128, 8192]),
                        ("E", [128, 640]), ("smb0", [128, 640])]:
            dbgt[nm_] = nc.dram_tensor("d_" + nm_, sh,
                                       F32 if nm_ in ("smb0",) else BF16, kind="ExternalOutput").ap()
        d_y2 = nc.dram_tensor("d_y2", [128, 8192], BF16, kind="ExternalOutput").ap()
        d_qbuf = nc.dram_tensor("d_qbuf", [128, 2048], F32, kind="ExternalOutput").ap()
        d_qss = nc.dram_tensor("d_qss", [128, 16], F32, kind="ExternalOutput").ap()
        d_qrr = nc.dram_tensor("d_qrr", [128, 16], F32, kind="ExternalOutput").ap()
        d_qn = nc.dram_tensor("d_qn", [128, 2048], F32, kind="ExternalOutput").ap()
        d_kv = nc.dram_tensor("d_kv", [128, 128], F32, kind="ExternalOutput").ap()
        d_rb = nc.dram_tensor("d_rb", [128, 128], F32, kind="ExternalOutput").ap()
        d_s = nc.dram_tensor("d_s", [128, 2], F32, kind="ExternalOutput").ap()

    al = Sbuf(nc, pfx)
    A = {}
    A["mask"] = al.sb("m_mask", [128, 4 * 640], F32)
    A["cos_o"] = al.sb("m_cos_o", [128, 4 * 32], F32)
    A["sin_o"] = al.sb("m_sin_o", [128, 4 * 32], F32)
    A["cos_h"] = al.sb("m_cos_h", [128, 4 * 32], F32)
    A["sin_h"] = al.sb("m_sin_h", [128, 4 * 32], F32)
    A["qnw"] = al.sb("m_qnw", [128, 128], F32)
    A["knw"] = al.sb("m_knw", [128, 128], F32)
    A["sinkb"] = al.sb("m_sinkb", [128, 16], F32)
    A["negsink"] = al.sb("m_negsink", [128, 16], F32)
    A["wqc"] = al.sb("m_wqc", [128, 16 * 512], BF16)
    A["wqu"] = al.sb("m_wqu", [128, 4 * 2048], BF16)
    A["wk"] = al.sb("m_wk", [128, 16 * 128], BF16)
    A["wv"] = al.sb("m_wv", [128, 16 * 128], BF16)
    A["xlnT"] = al.sb("xlnT", [128, 16 * 512], BF16)
    A["lnTt"] = al.sb("lnTt", [128, 16 * 128], BF16)
    A["kT"] = al.sb("kT", [128, 1024], BF16)
    A["vsb"] = al.sb("vsb", [128, 8 * 128], BF16)
    A["kvtmp"] = al.sb("kvtmp", [128, 256], F32)
    A["U"] = al.sb("Ubuf", [128, 2048], F32)
    A["ra"] = A["U"][:, 0:512]
    A["rb"] = A["U"][:, 512:1024]
    A["rc"] = A["U"][:, 1024:1536]
    A["rd"] = A["U"][:, 1536:2048]
    A["qcT"] = al.sb("qcT", [128, 4 * 512], BF16)
    A["qbuf"] = al.sb("qbuf", [128, 2048], F32)
    A["qss"] = al.sb("qss", [128, 16], F32)
    A["qrr"] = al.sb("qrr", [128, 16], F32)
    A["qbf"] = al.sb("qbf", [128, 2048], BF16)
    A["qT"] = al.sb("qT", [128, 16 * 512], BF16)
    A["smb"] = al.sb("smb", [128, 640], F32)
    A["E"] = al.sb("E", [128, 640], BF16)
    A["eT"] = al.sb("eT", [128, 640], BF16)
    A["attnT"] = al.sb("attnT", [128, 16 * 512], BF16)
    A["w1g"] = al.sb("w1g", [128, 4 * 2048], BF16)
    A["out1T"] = al.sb("out1T", [128, 16 * 512], BF16)
    A["w2gh"] = al.sb("w2gh", [128, 16 * 512], BF16)
    A["y2acc"] = al.sb("y2acc", [128, 4 * 2048], BF16)
    A["xtile"] = A["qbuf"]
    A["lnb"] = al.sb("lnb", [128, D], BF16)
    A["ysb"] = A["U"]
    A["qsq"] = A["U"]
    A["ident"] = al.sb("ident", [128, 128], BF16)
    A["_stats"] = al.sb("_stats", [128, 16], F32)
    for _i, nm_ in enumerate(["dsum", "es", "nm", "nm2", "rr", "c_eps5", "c_eps6", "c_m05",
                              "s_", "ssq", "mean", "negmean", "t1", "var", "rstd"]):
        A[nm_] = A["_stats"][:, _i:_i + 1]
    PA = al.ps("PA", [128, 2048], F32)
    PB = al.ps("PB", [128, 1536], F32)
    PT = al.ps("PT2", [128, 1024], BF16)

    emit_consts(seq, A)

    seq.dma("sync", A["mask"][:, :].rearrange("p (q w) -> p q w", q=4), mask.transpose([1, 0, 2]))
    for (dst, src) in (("cos_o", cos_o), ("sin_o", sin_o), ("cos_h", cos_h), ("sin_h", sin_h)):
        seq.dma("sync", A[dst][:, :].rearrange("p (q w) -> p q w", q=4),
                src.rearrange("(q p) w -> p q w", p=128))
    seq.dma("sync", A["qnw"][:, :], qnw)
    seq.dma("sync", A["knw"][:, :], knw)
    seq.dma("sync", A["sinkb"][:, :], sinkb)
    seq.add("vector", lambda v: v.tensor_scalar(
        out=A["negsink"][:, :], in0=A["sinkb"][:, :], scalar1=-1.0, scalar2=None, op0=ALU.mult))
    for kc in range(16):
        seq.dma("gpsimd", A["wqc"][:, kc * 512:(kc + 1) * 512], wqc[kc * 128:(kc + 1) * 128, :])
        seq.dma("gpsimd", A["wk"][:, kc * 128:(kc + 1) * 128], wk[kc * 128:(kc + 1) * 128, :])
        seq.dma("gpsimd", A["wv"][:, kc * 128:(kc + 1) * 128], wv[kc * 128:(kc + 1) * 128, :])
    for qc_c in range(4):
        seq.dma("gpsimd", A["wqu"][:, qc_c * 2048:(qc_c + 1) * 2048],
                wqu[qc_c * 128:(qc_c + 1) * 128, :])

    def rope_rms_k(ht, src128, costile, sintile, is_halo):
        k = A["kvtmp"]
        seq.add("vector", lambda v: v.tensor_tensor(out=A["ra"][:, 0:32], in0=k[:, 0:32],
                                                    in1=costile, op=ALU.mult))
        seq.add("vector", lambda v: v.tensor_tensor(out=A["rb"][:, 0:32], in0=k[:, 32:64],
                                                    in1=sintile, op=ALU.mult))
        seq.add("vector", lambda v: v.tensor_tensor(out=A["rc"][:, 0:32], in0=k[:, 0:32],
                                                    in1=sintile, op=ALU.mult))
        seq.addf("vector", lambda v: v.tensor_tensor(out=A["rd"][:, 0:32], in0=k[:, 32:64],
                                                     in1=costile, op=ALU.mult))
        seq.addf("vector", lambda v: v.tensor_tensor(out=k[:, 0:32], in0=A["ra"][:, 0:32],
                                                     in1=A["rb"][:, 0:32], op=ALU.subtract))
        seq.addf("vector", lambda v: v.tensor_tensor(out=k[:, 32:64], in0=A["rc"][:, 0:32],
                                                     in1=A["rd"][:, 0:32], op=ALU.add))
        seq.add("scalar", lambda e: e.activation(out=A["ra"][:, 0:128], in_=k[:, 0:128],
                                                 func=AF.Square, accum_out=A["s_"][:, :]))
        seq.add("vector", lambda v: v.tensor_scalar(out=A["t1"][:, :], in0=A["s_"][:, :],
                                                    scalar1=1.0 / 128, scalar2=None, op0=ALU.mult))

        def act_rr(e):
            e.activation(out=A["var"][:, :], in_=A["t1"][:, :], func=AF.Ln,
                         bias=A["c_eps6"][:, :], scale=1.0)
            return e.activation(out=A["rr"][:, :], in_=A["var"][:, :], func=AF.Exp,
                                bias=0.0, scale=A["c_m05"][:, :])
        seq.add("scalar", act_rr)
        seq.addf("vector", lambda v: v.scalar_tensor_tensor(
            out=A["rb"][:, 0:128], in0=k[:, 0:128], scalar=A["rr"][:, :], in1=A["knw"][:, :],
            op0=ALU.mult, op1=ALU.mult))
        seq.add("vector", lambda v: v.tensor_copy(out=A["lnb"][:, 0:128], in_=A["rb"][:, 0:128]))
        seq.add("tensor", lambda t: t.transpose(PT[:, 0:128], A["lnb"][:, 0:128], A["ident"][:, :]))
        seq.add("vector", lambda v, ht=ht: v.tensor_copy(
            out=A["kT"][:, ht * 128:(ht + 1) * 128], in_=PT[:, 0:128]))

    for b in range(4):
        for tt in range(4):
            rt = b * 4 + tt
            emit_lnT(seq, A, xs, rt, A["xlnT"], tt * 128, 512, PT)
        for ht in range(8):
            if ht < 4:
                rt = b * 4 + ht
                emit_lnT(seq, A, xh, rt, A["lnTt"], 0, 128, PT)
                lnsrc, coff = A["lnTt"], lambda kc: kc * 128
            else:
                tt = ht - 4
                lnsrc, coff = A["xlnT"], lambda kc, tt=tt: kc * 512 + tt * 128
            for kc in range(16):
                seq.add("tensor", lambda t, lnsrc=lnsrc, coff=coff, kc=kc: t.matmul(
                    PA[:, 0:128], lnsrc[:, coff(kc):coff(kc) + 128],
                    A["wk"][:, kc * 128:(kc + 1) * 128], start=(kc == 0), stop=(kc == 15)))
                seq.add("tensor", lambda t, lnsrc=lnsrc, coff=coff, kc=kc: t.matmul(
                    PA[:, 512:640], lnsrc[:, coff(kc):coff(kc) + 128],
                    A["wv"][:, kc * 128:(kc + 1) * 128], start=(kc == 0), stop=(kc == 15)))
            seq.add("scalar", lambda e: e.copy(out=A["kvtmp"][:, 0:128], in_=PA[:, 0:128]))
            seq.add("vector", lambda v, ht=ht: v.tensor_copy(
                out=A["vsb"][:, ht * 128:(ht + 1) * 128], in_=PA[:, 512:640]))
            pt_i = ht % 4
            cs, sn = ("cos_h", "sin_h") if ht < 4 else ("cos_o", "sin_o")
            rope_rms_k(ht, None, A[cs][:, pt_i * 32:(pt_i + 1) * 32],
                       A[sn][:, pt_i * 32:(pt_i + 1) * 32], ht < 4)
            if dbg and b == 0 and ht == 7:
                seq.dma("gpsimd", d_kv, A["kvtmp"][:, 0:128])
                seq.dma("gpsimd", d_rb, A["rb"][:, 0:128])
                seq.add("vector", lambda v: v.tensor_copy(out=A["smb"][:, 0:1], in_=A["s_"][:, :]))
                seq.add("vector", lambda v: v.tensor_copy(out=A["smb"][:, 1:2], in_=A["rr"][:, :]))
                seq.dma("gpsimd", d_s, A["smb"][:, 0:2])

        for mc in range(4):
            for kc in range(16):
                seq.add("tensor", lambda t, mc=mc, kc=kc: t.matmul(
                    PA[:, 0:512],
                    A["wqc"][:, kc * 512 + mc * 128: kc * 512 + (mc + 1) * 128],
                    A["xlnT"][:, kc * 512:(kc + 1) * 512],
                    start=(kc == 0), stop=(kc == 15)))
            seq.add("vector", lambda v, mc=mc: v.tensor_copy(
                out=A["qcT"][:, mc * 512:(mc + 1) * 512], in_=PA[:, 0:512]))
        for tt in range(4):
            for qc_c in range(4):
                for nn in range(4):
                    seq.add("tensor", lambda t, qc_c=qc_c, tt=tt, nn=nn: t.matmul(
                        PA[:, nn * 512:(nn + 1) * 512],
                        A["qcT"][:, qc_c * 512 + tt * 128: qc_c * 512 + (tt + 1) * 128],
                        A["wqu"][:, qc_c * 2048 + nn * 512: qc_c * 2048 + (nn + 1) * 512],
                        start=(qc_c == 0), stop=(qc_c == 3)))
            seq.add("scalar", lambda e: e.copy(out=A["qbuf"][:, :], in_=PA[:, :]))
            co = A["cos_o"][:, tt * 32:(tt + 1) * 32].unsqueeze(1).broadcast_to([128, 16, 32])
            si = A["sin_o"][:, tt * 32:(tt + 1) * 32].unsqueeze(1).broadcast_to([128, 16, 32])
            q3 = A["qbuf"][:, :].rearrange("p (h d) -> p h d", h=16)
            for (dst, xsl, tr) in (("ra", (0, 32), co), ("rb", (32, 64), si),
                                   ("rc", (0, 32), si), ("rd", (32, 64), co)):
                seq.addf("vector", lambda v, dst=dst, xsl=xsl, tr=tr, q3=q3: v.tensor_tensor(
                    out=A[dst][:, 0:512].rearrange("p (h d) -> p h d", h=16),
                    in0=q3[:, :, xsl[0]:xsl[1]], in1=tr, op=ALU.mult))
            seq.addf("vector", lambda v, q3=q3: v.tensor_tensor(
                out=q3[:, :, 0:32],
                in0=A["ra"][:, 0:512].rearrange("p (h d) -> p h d", h=16),
                in1=A["rb"][:, 0:512].rearrange("p (h d) -> p h d", h=16), op=ALU.subtract))
            seq.addf("vector", lambda v, q3=q3: v.tensor_tensor(
                out=q3[:, :, 32:64],
                in0=A["rc"][:, 0:512].rearrange("p (h d) -> p h d", h=16),
                in1=A["rd"][:, 0:512].rearrange("p (h d) -> p h d", h=16), op=ALU.add))
            seq.add("scalar", lambda e: e.nop())
            seq.addf("vector", lambda v: v.tensor_tensor(
                out=A["qsq"][:, :], in0=A["qbuf"][:, :], in1=A["qbuf"][:, :], op=ALU.mult))
            seq.add("scalar", lambda e: e.nop())
            seq.addf("vector", lambda v, q3=q3: v.tensor_reduce(
                out=A["qss"][:, :].rearrange("p (h x) -> p h x", x=1),
                in_=A["qsq"][:, :].rearrange("p (h d) -> p h d", h=16),
                axis=AX.X, op=ALU.add))
            seq.add("vector", lambda v: v.nop())
            seq.addf("vector", lambda v: v.tensor_scalar(
                out=A["qss"][:, :], in0=A["qss"][:, :], scalar1=1.0 / 128, scalar2=None,
                op0=ALU.mult))

            def act_qrr(e):
                e.activation(out=A["qss"][:, :], in_=A["qss"][:, :], func=AF.Ln,
                             bias=A["c_eps6"][:, :], scale=1.0)
                e.nop()
                return e.activation(out=A["qrr"][:, :], in_=A["qss"][:, :], func=AF.Exp,
                                    bias=0.0, scale=A["c_m05"][:, :])
            seq.add("scalar", act_qrr)
            rr3 = A["qrr"][:, :].unsqueeze(2).broadcast_to([128, 16, 128])
            qn3 = A["qnw"][:, :].unsqueeze(1).broadcast_to([128, 16, 128])
            seq.addf("vector", lambda v, q3=q3, rr3=rr3: v.tensor_tensor(
                out=A["qsq"][:, :].rearrange("p (h d) -> p h d", h=16),
                in0=q3, in1=rr3, op=ALU.mult))
            seq.addf("vector", lambda v, qn3=qn3: v.tensor_tensor(
                out=A["qbf"][:, :].rearrange("p (h d) -> p h d", h=16),
                in0=A["qsq"][:, :].rearrange("p (h d) -> p h d", h=16),
                in1=qn3, op=ALU.mult))
            if dbg and b == 0 and tt == 0:
                seq.dma("gpsimd", d_qbuf, A["qbuf"][:, :])
                seq.dma("gpsimd", d_qss, A["qss"][:, :])
                seq.dma("gpsimd", d_qrr, A["qrr"][:, :])
                seq.dma("gpsimd", d_qn, A["qsq"][:, :])
            emit_transposes(seq, PT, [
                (A["qbf"][:, h * 128:(h + 1) * 128],
                 A["qT"][:, h * 512 + tt * 128: h * 512 + (tt + 1) * 128])
                for h in range(16)])

        for h in range(16):
            for qt in range(4):
                seq.add("tensor", lambda t, h=h, qt=qt: t.matmul(
                    PA[:, 0:512], A["qT"][:, h * 512 + qt * 128: h * 512 + (qt + 1) * 128],
                    A["kT"][:, qt * 128: qt * 128 + 512], start=True, stop=True))
                seq.add("tensor", lambda t, h=h, qt=qt: t.matmul(
                    PA[:, 512:640], A["qT"][:, h * 512 + qt * 128: h * 512 + (qt + 1) * 128],
                    A["kT"][:, qt * 128 + 512: qt * 128 + 640], start=True, stop=True))
                seq.add("vector", lambda v, qt=qt: v.scalar_tensor_tensor(
                    out=A["smb"][:, :], in0=PA[:, 0:640], scalar=SCALE,
                    in1=A["mask"][:, qt * 640:(qt + 1) * 640], op0=ALU.mult, op1=ALU.add))
                seq.add("scalar", lambda e: e.nop())
                seq.add("vector", lambda v: v.reduce_max(
                    out=A["nm"][:, :], in_=A["smb"][:, :], axis=AX.X, negate=True))
                seq.addf("vector", lambda v, h=h: v.tensor_tensor(
                    out=A["nm2"][:, :], in0=A["nm"][:, :], in1=A["negsink"][:, h:h + 1],
                    op=ALU.min))
                seq.add("scalar", lambda e: e.activation(
                    out=A["E"][:, :], in_=A["smb"][:, :], func=AF.Exp, bias=A["nm2"][:, :],
                    scale=1.0, accum_out=A["dsum"][:, :]))
                seq.add("scalar", lambda e, h=h: e.activation(
                    out=A["es"][:, :], in_=A["sinkb"][:, h:h + 1], func=AF.Exp,
                    bias=A["nm2"][:, :], scale=1.0))
                seq.addf("vector", lambda v: v.tensor_tensor(
                    out=A["dsum"][:, :], in0=A["dsum"][:, :], in1=A["es"][:, :], op=ALU.add))
                seq.addf("vector", lambda v: v.reciprocal(out=A["rr"][:, :], in_=A["dsum"][:, :]))
                seq.addf("vector", lambda v: v.tensor_scalar(
                    out=A["E"][:, :], in0=A["E"][:, :], scalar1=A["rr"][:, :], scalar2=None,
                    op0=ALU.mult))
                emit_transposes(seq, PT, [
                    (A["E"][:, sc_ * 128:(sc_ + 1) * 128],
                     A["eT"][:, sc_ * 128:(sc_ + 1) * 128]) for sc_ in range(5)])
                for sc_ in range(5):
                    seq.add("tensor", lambda t, sc_=sc_, qt=qt: t.matmul(
                        PA[:, 1024:1152],
                        A["vsb"][:, (qt + sc_) * 128: (qt + sc_ + 1) * 128],
                        A["eT"][:, sc_ * 128:(sc_ + 1) * 128],
                        start=(sc_ == 0), stop=(sc_ == 4)))
                seq.add("vector", lambda v, h=h, qt=qt: v.tensor_copy(
                    out=A["attnT"][:, h * 512 + qt * 128: h * 512 + (qt + 1) * 128],
                    in_=PA[:, 1024:1152]))

        if dbg and b == 0:
            seq.dma("gpsimd", dbgt["kT"], A["kT"][:, :])
            seq.dma("gpsimd", dbgt["vsb"], A["vsb"][:, :])
            seq.dma("gpsimd", dbgt["qT"], A["qT"][:, :])
            seq.dma("gpsimd", dbgt["attnT"], A["attnT"][:, :])
            seq.dma("gpsimd", dbgt["xlnT"], A["xlnT"][:, :])
            seq.dma("gpsimd", dbgt["E"], A["E"][:, :])
            seq.dma("gpsimd", dbgt["smb0"], A["smb"][:, :])
        for g in range(4):
            for ic in range(4):
                seq.dma("gpsimd", A["w1g"][:, ic * 2048:(ic + 1) * 2048],
                        w1[g, ic * 128:(ic + 1) * 128, :])
            for oc in range(16):
                for ic in range(4):
                    seq.add("tensor", lambda t, oc=oc, ic=ic, g=g: t.matmul(
                        PB[:, 0:512],
                        A["w1g"][:, ic * 2048 + oc * 128: ic * 2048 + (oc + 1) * 128],
                        A["attnT"][:, (4 * g + ic) * 512:(4 * g + ic + 1) * 512],
                        start=(ic == 0), stop=(ic == 3)))
                seq.add("vector", lambda v, oc=oc: v.tensor_copy(
                    out=A["out1T"][:, oc * 512:(oc + 1) * 512], in_=PB[:, 0:512]))
            for quar in range(4):
                for gic in range(16):
                    seq.dma("gpsimd", A["w2gh"][:, gic * 512:(gic + 1) * 512],
                            w2[g * 2048 + gic * 128: g * 2048 + (gic + 1) * 128,
                               quar * 512:(quar + 1) * 512])
                for tt in range(4):
                    for gic in range(16):
                        seq.add("tensor", lambda t, gic=gic, tt=tt: t.matmul(
                            PB[:, 512:1024],
                            A["out1T"][:, gic * 512 + tt * 128: gic * 512 + (tt + 1) * 128],
                            A["w2gh"][:, gic * 512:(gic + 1) * 512],
                            start=(gic == 0), stop=(gic == 15)))
                    if g == 0:
                        seq.add("vector", lambda v, tt=tt, quar=quar: v.tensor_copy(
                            out=A["y2acc"][:, tt * 2048 + quar * 512: tt * 2048 + (quar + 1) * 512],
                            in_=PB[:, 512:1024]))
                    else:
                        seq.add("vector", lambda v, tt=tt, quar=quar: v.tensor_tensor(
                            out=A["y2acc"][:, tt * 2048 + quar * 512: tt * 2048 + (quar + 1) * 512],
                            in0=A["y2acc"][:, tt * 2048 + quar * 512: tt * 2048 + (quar + 1) * 512],
                            in1=PB[:, 512:1024], op=ALU.add))

        if dbg and b == 0:
            seq.dma("gpsimd", dbgt["out1T"], A["out1T"][:, :])
            seq.dma("gpsimd", d_y2, A["y2acc"][:, :])
        for tt in range(4):
            rt = b * 4 + tt
            emit_ln_stats(seq, A, A["y2acc"][:, tt * 2048:(tt + 1) * 2048], A["lnb"])
            seq.dma("sync", A["xtile"][:, :], xs[rt * 128:(rt + 1) * 128, :])
            seq.addf("vector", lambda v, tt=tt: v.tensor_scalar(
                out=A["ysb"][:, :], in0=A["y2acc"][:, tt * 2048:(tt + 1) * 2048],
                scalar1=A["negmean"][:, :], scalar2=A["rstd"][:, :],
                op0=ALU.add, op1=ALU.mult))
            seq.add("vector", lambda v: v.tensor_tensor(
                out=A["xtile"][:, :], in0=A["xtile"][:, :], in1=A["ysb"][:, :], op=ALU.add))
            seq.dma("gpsimd", xo[rt * 128:(rt + 1) * 128, :], A["xtile"][:, :])

    al.close()


def l0attn_host_prep(inputs):
    shared = {
        "wqc": np.ascontiguousarray(np.asarray(inputs["L0_attn_wq_comp"], np.float32)),
        "wqu": np.ascontiguousarray(np.asarray(inputs["L0_attn_wq_up"], np.float32)),
        "wk": np.ascontiguousarray(np.asarray(inputs["L0_attn_wk"], np.float32)),
        "wv": np.ascontiguousarray(np.asarray(inputs["L0_attn_wv"], np.float32)),
        "w1": np.ascontiguousarray(np.asarray(inputs["L0_attn_out_w1"], np.float32)),
        "w2": np.ascontiguousarray(np.asarray(inputs["L0_attn_out_w2"], np.float32)),
        "qnw": np.ascontiguousarray(np.broadcast_to(
            np.asarray(inputs["L0_attn_q_norm_w"], np.float32)[None, :], (128, 128))),
        "knw": np.ascontiguousarray(np.broadcast_to(
            np.asarray(inputs["L0_attn_k_norm_w"], np.float32)[None, :], (128, 128))),
        "sinkb": np.ascontiguousarray(np.broadcast_to(
            np.asarray(inputs["L0_attn_sink_logit"], np.float32)[None, :], (128, 16))),
    }
    inv = 1.0 / (10000.0 ** (np.arange(0, 64, 2, dtype=np.float32) / 64.0))
    def rope(p0):
        pos = (p0 + np.arange(512, dtype=np.float32))[:, None]
        ang = pos * inv[None, :]
        return np.cos(ang).astype(np.float32), np.sin(ang).astype(np.float32)
    i_ = np.arange(128)[:, None]
    jj = np.arange(640)[None, :]
    base_valid = (jj > i_) & (jj <= i_ + 512)
    per_core = []
    for c in range(8):
        m = np.empty((4, 128, 640), np.float32)
        for qt in range(4):
            valid = base_valid.copy()
            if c == 0:
                valid = valid & (jj >= 512 - qt * 128)
            m[qt] = np.where(valid, 0.0, -1e30)
        cos_o, sin_o = rope(512.0 * c)
        cos_h, sin_h = rope(512.0 * (c - 1))
        per_core.append({"mask": m, "cos_o": cos_o, "sin_o": sin_o,
                         "cos_h": cos_h, "sin_h": sin_h})
    return shared, per_core


def run_l0attn(x, inputs, n_cores=8, dbg=False):
    shared, per_core = l0attn_host_prep(inputs)
    raise NotImplementedError
    in_maps = []
    for c in range(n_cores):
        xsh = np.ascontiguousarray(
            x[:, c * 512:(c + 1) * 512, :].reshape(2048, 2048)).astype(np.float32)
        if c == 0:
            xhh = np.zeros((2048, 2048), np.float32)
        else:
            xhh = np.ascontiguousarray(
                x[:, (c - 1) * 512:c * 512, :].reshape(2048, 2048)).astype(np.float32)
        m = {"xs": xsh, "xh": xhh}
        m.update(shared)
        m.update(per_core[c])
        in_maps.append(m)
    res = run_bass_kernel_spmd(nc, in_maps, core_ids=list(range(n_cores)))
    out = np.empty((4, 4096, 2048), np.float32)
    for c in range(n_cores):
        out[:, c * 512:(c + 1) * 512, :] = res.results[c]["xo"].reshape(4, 512, 2048)
    if dbg:
        return out, res.results
    return out


def build_fused():
    nc = bass.Bass("TRN2", target_bir_lowering=False)
    seq = Seq(nc)
    xs0 = nc.dram_tensor("xs", [2048, 2048], F32, kind="ExternalInput").ap()
    x1d = nc.dram_tensor("x1d", [2048, 2048], F32).ap()
    x2d = nc.dram_tensor("x2d", [2048, 2048], F32).ap()
    x3d = nc.dram_tensor("x3d", [2048, 2048], F32).ap()
    xo = nc.dram_tensor("xo", [2048, 2048], F32, kind="ExternalOutput").ap()
    emit_l0attn(nc, seq, xs0, x1d, pfx="a_")
    emit_l0ffn(nc, seq, x1d, x2d, pfx="b_")
    emit_l1attn(nc, seq, x2d, x3d, pfx="c_")
    emit_l1ffn(nc, seq, x3d, xo, pfx="d_")
    seq.build()
    return nc


def kernel(**inputs):
    x = np.ascontiguousarray(np.asarray(inputs["L0_attn_h"], np.float32))
    n_cores = 8
    shared_a, per_core_a = l0attn_host_prep(inputs)
    shared_b = l0ffn_host_prep(inputs)
    shared_c = {
        "wqkvT": np.ascontiguousarray(np.asarray(inputs["L1_attn_weight_qkv"], np.float32).T),
        "wpT": np.ascontiguousarray(np.asarray(inputs["L1_attn_weight_proj"], np.float32).T),
    }
    shared_d = l1ffn_host_prep(inputs)
    nc = build_fused()
    in_maps = []
    for c in range(n_cores):
        xsh = np.ascontiguousarray(
            x[:, c * 512:(c + 1) * 512, :].reshape(2048, 2048)).astype(np.float32)
        if c == 0:
            xhh = np.zeros((2048, 2048), np.float32)
        else:
            xhh = np.ascontiguousarray(
                x[:, (c - 1) * 512:c * 512, :].reshape(2048, 2048)).astype(np.float32)
        m = {"xs": xsh, "xh": xhh}
        m.update(shared_a)
        m.update(per_core_a[c])
        m.update(shared_b)
        m.update(shared_c)
        m.update(shared_d)
        in_maps.append(m)
    res = run_bass_kernel_spmd(nc, in_maps, core_ids=list(range(n_cores)))
    out = np.empty((4, 4096, 2048), np.float32)
    for c in range(n_cores):
        out[:, c * 512:(c + 1) * 512, :] = res.results[c]["xo"].reshape(4, 512, 2048)
    return out


_NC_CACHE = {}


def _get_nc(name, builder):
    if name not in _NC_CACHE:
        _NC_CACHE[name] = builder()
    return _NC_CACHE[name]


def kernel_multilaunch(**inputs):
    x = np.ascontiguousarray(np.asarray(inputs["L0_attn_h"], np.float32))
    x = run_l0attn(x, inputs)
    x = run_l0ffn(x, inputs)
    x = run_l1attn(x, inputs)
    x = run_l1ffn(x, inputs)
    return np.ascontiguousarray(x.astype(np.float32))


# version 10 (final)
# speedup vs baseline: 1.2643x; 1.1593x over previous
import numpy as np
import concourse.bass as bass
import concourse.mybir as mybir
from concourse.bass_utils import run_bass_kernel_spmd

F32 = mybir.dt.float32
BF16 = mybir.dt.bfloat16
I32 = mybir.dt.int32
F16 = mybir.dt.float16
AF = mybir.ActivationFunctionType
ALU = mybir.AluOpType
AX = mybir.AxisListType

ENGINES = ("sync", "gpsimd", "tensor", "vector", "scalar")


class Seq:

    def __init__(self, nc):
        self.nc = nc
        self._dma_counts = {}
        self.items = []

    def add(self, eng, fn, dma=False):
        self.items.append((eng, fn, dma))

    def dma(self, eng, out, in_, **kw):
        self.add(eng, lambda e: e.dma_start(out=out, in_=in_, **kw), dma=True)

    def addf(self, eng, fn):
        self.add(eng, fn)
        self.add("scalar" if eng != "scalar" else "vector", lambda e: e.nop())

    def build(self, extra_engines=()):
        nc = self.nc
        n = len(self.items)
        incs = [False] * n
        for i, (eng, fn, dma) in enumerate(self.items):
            if dma:
                incs[i] = True
            elif i == n - 1 or self.items[i + 1][0] != eng:
                incs[i] = True
        sems = {}
        sem_ctx = []
        for e in ENGINES:
            g = nc.semaphore(f"sq_{e}")
            sems[e] = g.__enter__()
            sem_ctx.append(g)
        counts = {e: 0 for e in ENGINES}
        last_compute = None
        waited = {e: 0 for e in ENGINES}
        import os
        relax = os.environ.get("SEQ_RELAX", "1") == "1"
        plan = []
        for i, (eng, fn, dma) in enumerate(self.items):
            waits = []
            if i > 0:
                peng, pfn, pdma = self.items[i - 1]
                if not relax:
                    if peng != eng or pdma:
                        waits.append((sems[peng], counts[peng]))
                elif dma:
                    if pdma and peng == eng:
                        pass
                    else:
                        waits.append((sems[peng], counts[peng]))
                else:
                    if pdma:
                        for e2 in ("sync", "gpsimd", "scalar", "vector", "tensor"):
                            c2 = self._dma_counts.get(e2, 0)
                            if c2 > waited.get(("c", eng, e2), 0):
                                waits.append((sems[e2], counts[e2]))
                                waited[("c", eng, e2)] = c2
                        if last_compute is not None and last_compute[0] != eng:
                            waits.append((sems[last_compute[0]], last_compute[1]))
                    elif peng != eng:
                        waits.append((sems[peng], counts[peng]))
            inc = None
            if incs[i]:
                amt = 16 if dma else 1
                counts[eng] += amt
                inc = (sems[eng], amt)
                if dma:
                    self._dma_counts[eng] = self._dma_counts.get(eng, 0) + 16
            if not dma:
                last_compute = (eng, counts[eng])
            plan.append((waits, inc))

        per_eng = {e: [] for e in ENGINES}
        for i, (eng, fn, dma) in enumerate(self.items):
            per_eng[eng].append((self.items[i], plan[i]))

        waiter = "gpsimd"
        final_waits = [(sems[e], counts[e]) for e in ENGINES
                       if counts[e] > 0 and e != waiter]
        final_self = counts[waiter]

        with nc.Block() as block:
            for e in ENGINES:
                entries = per_eng[e]
                if not entries and e != waiter:
                    continue

                def make(entries, is_waiter):
                    def f(engobj):
                        for (eng, fn, dma), (waits, inc) in entries:
                            for (sem, val) in waits:
                                if val > 0:
                                    engobj.wait_ge(sem, val)
                            inst = fn(engobj)
                            if inc is not None:
                                inst.then_inc(inc[0], inc[1])
                        if is_waiter:
                            for (sem, val) in final_waits:
                                engobj.wait_ge(sem, val)
                            if final_self > 0:
                                engobj.wait_ge(sems[waiter], final_self)
                    return f

                getattr(block, e)(make(entries, e == waiter))
        for g in sem_ctx:
            g.__exit__(None, None, None)
        return self


class Sbuf:

    def __init__(self, nc, pfx=""):
        self.nc = nc
        self.pfx = pfx
        self.ctx = []

    def sb(self, name, shape, dt):
        g = self.nc.sbuf_tensor(self.pfx + name, shape, dt)
        t = g.__enter__()
        self.ctx.append(g)
        return t

    def ps(self, name, shape, dt=F32):
        g = self.nc.psum_tensor(self.pfx + name, shape, dt)
        t = g.__enter__()
        self.ctx.append(g)
        return t

    def close(self):
        for g in reversed(self.ctx):
            g.__exit__(None, None, None)


def emit_transposes(seq, PT, pairs, batch=8):
    for base in range(0, len(pairs), batch):
        grp = pairs[base:base + batch]

        def tf(t, grp=grp, ident=_IDENT[0]):
            r = None
            for s, (src, dst) in enumerate(grp):
                r = t.transpose(PT[:, s * 128:(s + 1) * 128], src, ident)
            return r
        seq.add("tensor", tf)

        def vf(v, grp=grp):
            r = None
            for s, (src, dst) in enumerate(grp):
                r = v.tensor_copy(out=dst, in_=PT[:, s * 128:(s + 1) * 128])
            return r
        seq.add("vector", vf)


_IDENT = [None]


def emit_consts(seq, A):
    def f(v):
        v.memset(A["c_eps5"][:, :], 1e-5)
        v.memset(A["c_eps6"][:, :], 1e-6)
        return v.memset(A["c_m05"][:, :], -0.5)
    seq.add("vector", f)

    seq.add("gpsimd", lambda g: g.memset(A["ident"][:, :], 1.0))
    seq.add("gpsimd", lambda g: g.affine_select(
        out=A["ident"][:, :], in_=A["ident"][:, :],
        pattern=[[1, 128]], compare_op=ALU.is_equal, fill=0.0,
        base=0, channel_multiplier=-1))
    _IDENT[0] = A["ident"][:, :]


def emit_ln_stats(seq, A, src_ap, scratch, pfx=""):
    D = src_ap.shape[-1]
    s_, ssq = A["s_"], A["ssq"]
    mean, negmean, t1, var, rstd = A["mean"], A["negmean"], A["t1"], A["var"], A["rstd"]

    seq.add("scalar", lambda e: e.activation(out=scratch[:, 0:D], in_=src_ap, func=AF.Copy,
                                             accum_out=s_[:, :]))
    seq.add("scalar", lambda e: e.activation(out=scratch[:, 0:D], in_=src_ap, func=AF.Square,
                                             accum_out=ssq[:, :]))

    def dve(v):
        v.tensor_scalar(out=mean[:, :], in0=s_[:, :], scalar1=1.0 / D, scalar2=None, op0=ALU.mult)
        v.tensor_scalar(out=negmean[:, :], in0=s_[:, :], scalar1=-1.0 / D, scalar2=None, op0=ALU.mult)
        v.tensor_scalar(out=t1[:, :], in0=ssq[:, :], scalar1=1.0 / D, scalar2=None, op0=ALU.mult)
        return v.scalar_tensor_tensor(out=var[:, :], in0=mean[:, :], scalar=negmean[:, :],
                                      in1=t1[:, :], op0=ALU.mult, op1=ALU.add)
    seq.add("vector", dve)

    def act2(e):
        e.activation(out=var[:, :], in_=var[:, :], func=AF.Ln, bias=A["c_eps5"][:, :], scale=1.0)
        return e.activation(out=rstd[:, :], in_=var[:, :], func=AF.Exp, bias=0.0,
                            scale=A["c_m05"][:, :])
    seq.add("scalar", act2)


def emit_l1ffn(nc, seq, xs, xo, pfx="d_", dbg=False):
    TOK = 2048
    D = 2048
    HID = 5632
    HC = HID // 128

    wgT = nc.dram_tensor("wgT", [D, HID], I32, kind="ExternalInput").ap()
    wuT = nc.dram_tensor("wuT", [D, HID], I32, kind="ExternalInput").ap()
    wdT = nc.dram_tensor("wdT", [HID, D], I32, kind="ExternalInput").ap()
    sg = nc.dram_tensor("sg", [128, HC], F32, kind="ExternalInput").ap()
    su = nc.dram_tensor("su", [128, HC], F32, kind="ExternalInput").ap()
    sdbc = nc.dram_tensor("sdbc", [128, D], F32, kind="ExternalInput").ap()
    if dbg:
        d_xlnT = nc.dram_tensor("d_xlnT", [128, 16 * TOK], BF16, kind="ExternalOutput").ap()
        d_hT = nc.dram_tensor("d_hT", [128, 44 * 1024], BF16, kind="ExternalOutput").ap()
        d_y = nc.dram_tensor("d_y", [128, D], F32, kind="ExternalOutput").ap()
        d_wg = nc.dram_tensor("d_wg", [128, 2048], BF16, kind="ExternalOutput").ap()
        d_gs = nc.dram_tensor("d_gs", [128, 512], BF16, kind="ExternalOutput").ap()
        d_up = nc.dram_tensor("d_up", [128, 512], BF16, kind="ExternalOutput").ap()
        d_pg = nc.dram_tensor("d_pg", [128, 1024], F32, kind="ExternalOutput").ap()

    al = Sbuf(nc, pfx)
    A = {}
    A["xlnT"] = al.sb("xlnT", [128, 16 * TOK], BF16)
    A["hT"] = al.sb("hT", [128, HC * 1024], BF16)
    A["xtile"] = al.sb("xtile", [128, D], F32)
    A["lnb"] = al.sb("lnb", [128, D], BF16)
    A["scratch"] = A["lnb"]
    A["wcol"] = al.sb("wcol", [128, 16 * 128], I32)
    A["wg_bf"] = al.sb("wg_bf", [128, 16 * 128], BF16)
    A["wu_bf"] = al.sb("wu_bf", [128, 16 * 128], BF16)
    A["wd_i"] = A["wcol"]
    A["wd_bf"] = al.sb("wd_bf", [128, D], BF16)
    A["sg_s"] = al.sb("sg_s", [128, HC], F32)
    A["su_s"] = al.sb("su_s", [128, HC], F32)
    A["sdbc_s"] = al.sb("sdbc_s", [128, D], F32)
    A["gsil"] = al.sb("gsil", [128, 512], BF16)
    A["upb"] = al.sb("upb", [128, 512], BF16)
    A["ysb"] = al.sb("ysb", [128, D], F32)
    A["ident"] = al.sb("ident", [128, 128], BF16)
    A["sprt"] = al.sb("sprt", [128, 4], F32)
    A["ssqp"] = al.sb("ssqp", [128, 4], F32)
    for nm in ["c_eps5", "c_eps6", "c_m05", "s_", "ssq", "mean", "negmean", "t1", "var", "rstd"]:
        A[nm] = al.sb(nm, [128, 1], F32)
    P1 = al.ps("P1", [128, 2048], F32)
    PT = al.ps("PT", [128, 1024], BF16)

    emit_consts(seq, A)
    seq.dma("sync", A["sg_s"][:, :], sg)
    seq.dma("sync", A["su_s"][:, :], su)
    seq.dma("sync", A["sdbc_s"][:, :], sdbc)

    for rt in range(16):
        seq.dma("sync", A["xtile"][:, :], xs[rt * 128:(rt + 1) * 128, :])
        emit_ln_stats(seq, A, A["xtile"][:, :], A["scratch"])
        seq.add("vector", lambda v, A=A: v.tensor_scalar(
            out=A["lnb"][:, :], in0=A["xtile"][:, :], scalar1=A["negmean"][:, :],
            scalar2=A["rstd"][:, :], op0=ALU.add, op1=ALU.mult))
        emit_transposes(seq, PT, [
            (A["lnb"][:, c * 128:(c + 1) * 128],
             A["xlnT"][:, c * TOK + rt * 128: c * TOK + (rt + 1) * 128])
            for c in range(16)])

    for grp in range(2):
        for hc in range(HC):
            for (wT, wbf) in ((wgT, "wg_bf"), (wuT, "wu_bf")):
                for q in range(4):
                    seq.dma("sync",
                            A["wcol"][:, q * 4 * 128: (q + 1) * 4 * 128]
                            .rearrange("p (c h) -> p c h", c=4),
                            wT[q * 512:(q + 1) * 512, hc * 128:(hc + 1) * 128]
                            .rearrange("(c p) h -> p c h", p=128))
                seq.add("vector", lambda v, A=A, wbf=wbf: v.tensor_scalar(
                    out=A[wbf][:, :], in0=A["wcol"][:, :], scalar1=128.0, scalar2=None,
                    op0=ALU.subtract))
            for (wbf, off) in (("wg_bf", 0), ("wu_bf", 1024)):
                for kc in range(16):
                    for tg in range(2):
                        seq.add("tensor", lambda t, A=A, wbf=wbf, kc=kc, tg=tg, off=off, grp=grp: t.matmul(
                            P1[:, off + tg * 512: off + (tg + 1) * 512],
                            A[wbf][:, kc * 128:(kc + 1) * 128],
                            A["xlnT"][:, kc * TOK + grp * 1024 + tg * 512: kc * TOK + grp * 1024 + (tg + 1) * 512],
                            start=(kc == 0), stop=(kc == 15)))
            for tg in range(2):
                seq.add("scalar", lambda e, A=A, tg=tg, hc=hc: e.activation(
                    out=A["gsil"][:, :], in_=P1[:, tg * 512:(tg + 1) * 512], func=AF.Silu,
                    bias=0.0, scale=A["sg_s"][:, hc:hc + 1]))
                seq.add("vector", lambda v, A=A, tg=tg, hc=hc: v.tensor_scalar(
                    out=A["upb"][:, :], in0=P1[:, 1024 + tg * 512: 1024 + (tg + 1) * 512],
                    scalar1=A["su_s"][:, hc:hc + 1], scalar2=None, op0=ALU.mult))
                seq.add("vector", lambda v, A=A, tg=tg, hc=hc: v.tensor_tensor(
                    out=A["hT"][:, hc * 1024 + tg * 512: hc * 1024 + (tg + 1) * 512],
                    in0=A["gsil"][:, :], in1=A["upb"][:, :], op=ALU.mult))

        if dbg and grp == 1:
            seq.dma("gpsimd", d_xlnT, A["xlnT"][:, :])
            seq.dma("gpsimd", d_hT, A["hT"][:, :])
            seq.dma("gpsimd", d_wg, A["wg_bf"][:, :])
            seq.dma("gpsimd", d_gs, A["gsil"][:, :])
            seq.dma("gpsimd", d_up, A["upb"][:, :])
            seq.add("vector", lambda v, A=A: v.tensor_copy(out=A["ysb"][:, 0:1024], in_=P1[:, 0:1024]))
            seq.dma("gpsimd", d_pg, A["ysb"][:, 0:1024])
        for tt_ in range(8):
            rt = grp * 8 + tt_
            for hc in range(HC):
                seq.dma("sync", A["wd_i"][:, :], wdT[hc * 128:(hc + 1) * 128, :])
                seq.add("vector", lambda v, A=A: v.tensor_scalar(
                    out=A["wd_bf"][:, :], in0=A["wd_i"][:, :], scalar1=128.0, scalar2=None,
                    op0=ALU.subtract))
                for nn in range(4):
                    seq.add("tensor", lambda t, A=A, hc=hc, tt_=tt_, nn=nn: t.matmul(
                        P1[:, nn * 512:(nn + 1) * 512],
                        A["hT"][:, hc * 1024 + tt_ * 128: hc * 1024 + (tt_ + 1) * 128],
                        A["wd_bf"][:, nn * 512:(nn + 1) * 512],
                        start=(hc == 0), stop=(hc == HC - 1)))
            for nn in range(4):
                seq.add("vector", lambda v, A=A, nn=nn: v.scalar_tensor_tensor(
                    out=A["ysb"][:, nn * 512:(nn + 1) * 512], in0=P1[:, nn * 512:(nn + 1) * 512],
                    scalar=1.0, in1=A["sdbc_s"][:, nn * 512:(nn + 1) * 512],
                    op0=ALU.mult, op1=ALU.mult))
            if dbg and rt == 15:
                seq.dma("gpsimd", d_y, A["ysb"][:, :])
            emit_ln_stats(seq, A, A["ysb"][:, :], A["scratch"])
            seq.dma("sync", A["xtile"][:, :], xs[rt * 128:(rt + 1) * 128, :])
            seq.add("vector", lambda v, A=A: v.tensor_scalar(
                out=A["ysb"][:, :], in0=A["ysb"][:, :], scalar1=A["negmean"][:, :],
                scalar2=A["rstd"][:, :], op0=ALU.add, op1=ALU.mult))
            seq.add("vector", lambda v, A=A: v.tensor_tensor(
                out=A["xtile"][:, :], in0=A["xtile"][:, :], in1=A["ysb"][:, :], op=ALU.add))
            seq.dma("gpsimd", xo[rt * 128:(rt + 1) * 128, :], A["xtile"][:, :])

    al.close()


def l1ffn_host_prep(inputs):
    wg = inputs["L1_ffn_wq_gate"]
    wu = inputs["L1_ffn_wq_up"]
    wd = inputs["L1_ffn_wq_down"]
    shared = {
        "wgT": np.ascontiguousarray(wg.T).astype(np.int32),
        "wuT": np.ascontiguousarray(wu.T).astype(np.int32),
        "wdT": np.ascontiguousarray(wd.T).astype(np.int32),
        "sg": np.ascontiguousarray(
            np.asarray(inputs["L1_ffn_s_gate"], np.float32).reshape(44, 128).T),
        "su": np.ascontiguousarray(
            np.asarray(inputs["L1_ffn_s_up"], np.float32).reshape(44, 128).T),
        "sdbc": np.ascontiguousarray(np.broadcast_to(
            np.asarray(inputs["L1_ffn_s_down"], np.float32)[None, :], (128, 2048))),
    }
    return shared


def run_l1ffn(x, inputs, n_cores=8, dbg=False):
    shared = l1ffn_host_prep(inputs)
    raise NotImplementedError
    in_maps = []
    for c in range(n_cores):
        xs = np.ascontiguousarray(
            x[:, c * 512:(c + 1) * 512, :].reshape(2048, 2048)).astype(np.float32)
        m = {"xs": xs}
        m.update(shared)
        in_maps.append(m)
    res = run_bass_kernel_spmd(nc, in_maps, core_ids=list(range(n_cores)))
    out = np.empty((4, 4096, 2048), np.float32)
    for c in range(n_cores):
        out[:, c * 512:(c + 1) * 512, :] = res.results[c]["xo"].reshape(4, 512, 2048)
    if dbg:
        return out, res.results
    return out


def emit_lnT(seq, A, xs, rt, dest, dest_col, dest_stride, PT):
    seq.dma("sync", A["xtile"][:, :], xs[rt * 128:(rt + 1) * 128, :])
    emit_ln_stats(seq, A, A["xtile"][:, :], A["lnb"])
    seq.add("vector", lambda v: v.tensor_scalar(
        out=A["lnb"][:, :], in0=A["xtile"][:, :], scalar1=A["negmean"][:, :],
        scalar2=A["rstd"][:, :], op0=ALU.add, op1=ALU.mult))
    emit_transposes(seq, PT, [
        (A["lnb"][:, c * 128:(c + 1) * 128],
         dest[:, c * dest_stride + dest_col: c * dest_stride + dest_col + 128])
        for c in range(16)])


def emit_dequant_fp4(seq, A, pk_ap, scf_ap, ngroups, C, wc_dest):
    for j in range(8):
        seq.add("vector", lambda v, j=j: v.tensor_scalar(
            out=A["dq_i"][:, 0:C], in0=pk_ap, scalar1=4 * j, scalar2=7,
            op0=ALU.logical_shift_right, op1=ALU.bitwise_and))
        seq.add("vector", lambda v: v.tensor_copy(
            out=A["dq_mf"][:, 0:C], in_=A["dq_i"][:, 0:C]))
        seq.add("vector", lambda v, j=j: v.tensor_scalar(
            out=A["dq_i"][:, 0:C], in0=pk_ap, scalar1=4 * j + 3, scalar2=1,
            op0=ALU.logical_shift_right, op1=ALU.bitwise_and))
        seq.add("vector", lambda v: v.tensor_scalar(
            out=A["dq_sgn"][:, 0:C], in0=A["dq_i"][:, 0:C], scalar1=-2.0, scalar2=1.0,
            op0=ALU.mult, op1=ALU.add))
        seq.add("scalar", lambda e: e.activation(
            out=A["dq_r1"][:, 0:C], in_=A["dq_mf"][:, 0:C], func=AF.Relu,
            bias=A["c_m2"][:, :], scale=0.5))
        seq.add("scalar", lambda e: e.activation(
            out=A["dq_r2"][:, 0:C], in_=A["dq_mf"][:, 0:C], func=AF.Relu,
            bias=A["c_m6"][:, :], scale=1.0))
        seq.add("vector", lambda v: v.scalar_tensor_tensor(
            out=A["dq_v"][:, 0:C], in0=A["dq_mf"][:, 0:C], scalar=0.5,
            in1=A["dq_r1"][:, 0:C], op0=ALU.mult, op1=ALU.add))
        seq.add("vector", lambda v: v.tensor_tensor(
            out=A["dq_v"][:, 0:C], in0=A["dq_v"][:, 0:C], in1=A["dq_r2"][:, 0:C],
            op=ALU.add))
        seq.add("vector", lambda v: v.tensor_tensor(
            out=A["dq_v"][:, 0:C], in0=A["dq_v"][:, 0:C], in1=A["dq_sgn"][:, 0:C],
            op=ALU.mult))
        seq.add("vector", lambda v, j=j: v.scalar_tensor_tensor(
            out=wc_dest[:, j::8].rearrange("p (g r) -> p g r", g=ngroups),
            in0=A["dq_v"][:, 0:C].rearrange("p (g r) -> p g r", g=ngroups),
            scalar=1.0,
            in1=scf_ap.unsqueeze(2).broadcast_to([128, ngroups, 16]),
            op0=ALU.mult, op1=ALU.mult))


def emit_l0ffn(nc, seq, xs, xo, pfx="b_"):
    TOK = 2048
    D = 2048
    I = 5632
    GUC = 88
    IC = 44

    gup = nc.dram_tensor("gup", [2 * I, 256], I32, kind="ExternalInput").ap()
    gusT = nc.dram_tensor("gusT", [2 * I, 16], F16, kind="ExternalInput").ap()
    dnp = nc.dram_tensor("dnp", [D, 704], I32, kind="ExternalInput").ap()
    dnsT = nc.dram_tensor("dnsT", [D, IC], F16, kind="ExternalInput").ap()
    wguT_d = nc.dram_tensor("wguT_d", [D, 2 * I], BF16).ap()
    wdnT_d = nc.dram_tensor("wdnT_d", [I, D], BF16).ap()

    al = Sbuf(nc, pfx)
    A = {}
    A["xlnT"] = al.sb("xlnT", [128, 16 * 1024], BF16)
    A["hT"] = al.sb("hT", [128, IC * 1024], BF16)
    A["xtile"] = al.sb("xtile", [128, D], F32)
    A["lnb"] = al.sb("lnb", [128, D], BF16)
    A["ysb"] = al.sb("ysb", [128, D], F32)
    A["pk"] = al.sb("pk", [128, 704], I32)
    A["sc16"] = al.sb("sc16", [128, IC], F16)
    A["scf"] = al.sb("scf", [128, IC], F32)
    A["dq_i"] = al.sb("dq_i", [128, 704], I32)
    A["dq_mf"] = al.sb("dq_mf", [128, 704], F32)
    A["dq_sgn"] = al.sb("dq_sgn", [128, 704], F32)
    A["dq_r1"] = al.sb("dq_r1", [128, 704], F32)
    A["dq_r2"] = al.sb("dq_r2", [128, 704], F32)
    A["dq_v"] = al.sb("dq_v", [128, 704], F32)
    A["Wc"] = al.sb("Wc", [128, I], BF16)
    A["WcT"] = al.sb("WcT", [128, IC * 128], BF16)
    A["wstage_g"] = al.sb("wstage_g", [128, 16 * 128], BF16)
    A["wstage_u"] = al.sb("wstage_u", [128, 16 * 128], BF16)
    A["wdstage"] = al.sb("wdstage", [128, D], BF16)
    A["gsil"] = al.sb("gsil", [128, 512], BF16)
    A["upb"] = al.sb("upb", [128, 512], BF16)
    A["ident"] = al.sb("ident", [128, 128], BF16)
    for nm in ["c_eps5", "c_eps6", "c_m05", "c_m2", "c_m6", "c_e11",
               "s_", "ssq", "mean", "negmean", "t1", "var", "rstd"]:
        A[nm] = al.sb(nm, [128, 1], F32)
    PB = al.ps("PB", [128, 2048], F32)
    PT = al.ps("PT", [128, 1024], BF16)

    emit_consts(seq, A)
    seq.add("vector", lambda v: v.memset(A["c_m2"][:, :], -2.0))
    seq.add("vector", lambda v: v.memset(A["c_m6"][:, :], -6.0))
    seq.add("vector", lambda v: v.memset(A["c_e11"][:, :], 1.000005e-11))

    for gc in range(GUC):
        seq.dma("sync", A["pk"][:, 0:256], gup[gc * 128:(gc + 1) * 128, :])
        seq.dma("sync", A["sc16"][:, 0:16], gusT[gc * 128:(gc + 1) * 128, :])
        seq.add("vector", lambda v: v.tensor_copy(out=A["scf"][:, 0:16], in_=A["sc16"][:, 0:16]))
        emit_dequant_fp4(seq, A, A["pk"][:, 0:256], A["scf"][:, 0:16], 16, 256,
                         A["Wc"][:, 0:2048])
        emit_transposes(seq, PT, [
            (A["Wc"][:, kc * 128:(kc + 1) * 128], A["WcT"][:, kc * 128:(kc + 1) * 128])
            for kc in range(16)])
        for half in range(2):
            seq.dma("gpsimd",
                    wguT_d[half * 1024:(half + 1) * 1024, gc * 128:(gc + 1) * 128]
                    .rearrange("(k p) o -> p k o", p=128),
                    A["WcT"][:, half * 1024:(half + 1) * 1024]
                    .rearrange("p (k o) -> p k o", k=8))

    for oc in range(16):
        seq.dma("sync", A["pk"][:, 0:704], dnp[oc * 128:(oc + 1) * 128, :])
        seq.dma("sync", A["sc16"][:, 0:IC], dnsT[oc * 128:(oc + 1) * 128, :])
        seq.add("vector", lambda v: v.tensor_copy(out=A["scf"][:, 0:IC], in_=A["sc16"][:, 0:IC]))
        emit_dequant_fp4(seq, A, A["pk"][:, 0:704], A["scf"][:, 0:IC], IC, 704,
                         A["Wc"][:, 0:I])
        emit_transposes(seq, PT, [
            (A["Wc"][:, kc * 128:(kc + 1) * 128], A["WcT"][:, kc * 128:(kc + 1) * 128])
            for kc in range(IC)])
        for half in range(4):
            seq.dma("gpsimd",
                    wdnT_d[half * 1408:(half + 1) * 1408, oc * 128:(oc + 1) * 128]
                    .rearrange("(k p) o -> p k o", p=128),
                    A["WcT"][:, half * 1408:(half + 1) * 1408]
                    .rearrange("p (k o) -> p k o", k=11))

    for grp in range(2):
        for rtl in range(8):
            rt = grp * 8 + rtl
            emit_lnT(seq, A, xs, rt, A["xlnT"], rtl * 128, 1024, PT)
        for gcp in range(IC):
            for (gc2, wst) in ((gcp, "wstage_g"), (IC + gcp, "wstage_u")):
                for half in range(2):
                    seq.dma("sync",
                            A[wst][:, half * 1024:(half + 1) * 1024]
                            .rearrange("p (k o) -> p k o", k=8),
                            wguT_d[half * 1024:(half + 1) * 1024, gc2 * 128:(gc2 + 1) * 128]
                            .rearrange("(k p) o -> p k o", p=128))
            for (wst, off) in (("wstage_g", 0), ("wstage_u", 1024)):
                for kc in range(16):
                    for tg in range(2):
                        seq.add("tensor", lambda t, wst=wst, kc=kc, tg=tg, off=off, grp=grp: t.matmul(
                            PB[:, off + tg * 512: off + (tg + 1) * 512],
                            A[wst][:, kc * 128:(kc + 1) * 128],
                            A["xlnT"][:, kc * 1024 + tg * 512: kc * 1024 + (tg + 1) * 512],
                            start=(kc == 0), stop=(kc == 15)))
            for tg in range(2):
                seq.add("scalar", lambda e, tg=tg: e.activation(
                    out=A["gsil"][:, :], in_=PB[:, tg * 512:(tg + 1) * 512], func=AF.Silu,
                    bias=0.0, scale=1.0))
                seq.add("vector", lambda v, tg=tg: v.tensor_copy(
                    out=A["upb"][:, :], in_=PB[:, 1024 + tg * 512:1024 + (tg + 1) * 512]))
                seq.add("vector", lambda v, tg=tg, gcp=gcp: v.tensor_tensor(
                    out=A["hT"][:, gcp * 1024 + tg * 512: gcp * 1024 + (tg + 1) * 512],
                    in0=A["gsil"][:, :], in1=A["upb"][:, :], op=ALU.mult))
        for tt_ in range(8):
            rt = grp * 8 + tt_
            for ic in range(IC):
                seq.dma("sync", A["wdstage"][:, :], wdnT_d[ic * 128:(ic + 1) * 128, :])
                for nn in range(4):
                    seq.add("tensor", lambda t, ic=ic, tt_=tt_, nn=nn: t.matmul(
                        PB[:, nn * 512:(nn + 1) * 512],
                        A["hT"][:, ic * 1024 + tt_ * 128: ic * 1024 + (tt_ + 1) * 128],
                        A["wdstage"][:, nn * 512:(nn + 1) * 512],
                        start=(ic == 0), stop=(ic == IC - 1)))
            seq.add("scalar", lambda e: e.activation(
                out=A["ysb"][:, :], in_=PB[:, :], func=AF.Copy, accum_out=A["s_"][:, :]))
            seq.add("scalar", lambda e: e.activation(
                out=A["lnb"][:, :], in_=A["ysb"][:, :], func=AF.Square, accum_out=A["ssq"][:, :]))

            def dve_stats(v):
                v.tensor_scalar(out=A["mean"][:, :], in0=A["s_"][:, :], scalar1=1.0 / D,
                                scalar2=None, op0=ALU.mult)
                v.tensor_scalar(out=A["negmean"][:, :], in0=A["s_"][:, :], scalar1=-1.0 / D,
                                scalar2=None, op0=ALU.mult)
                v.tensor_scalar(out=A["t1"][:, :], in0=A["ssq"][:, :],
                                scalar1=(1.0 + 1e-5) / D, scalar2=None, op0=ALU.mult)
                return v.scalar_tensor_tensor(out=A["var"][:, :], in0=A["mean"][:, :],
                                              scalar=A["negmean"][:, :], in1=A["t1"][:, :],
                                              op0=ALU.mult, op1=ALU.add)
            seq.add("vector", dve_stats)

            def act_rstd(e):
                e.activation(out=A["var"][:, :], in_=A["var"][:, :], func=AF.Ln,
                             bias=A["c_e11"][:, :], scale=1.0)
                return e.activation(out=A["rstd"][:, :], in_=A["var"][:, :], func=AF.Exp,
                                    bias=0.0, scale=A["c_m05"][:, :])
            seq.add("scalar", act_rstd)
            seq.dma("sync", A["xtile"][:, :], xs[rt * 128:(rt + 1) * 128, :])
            seq.add("vector", lambda v: v.tensor_scalar(
                out=A["ysb"][:, :], in0=A["ysb"][:, :], scalar1=A["negmean"][:, :],
                scalar2=A["rstd"][:, :], op0=ALU.add, op1=ALU.mult))
            seq.add("vector", lambda v: v.tensor_tensor(
                out=A["xtile"][:, :], in0=A["xtile"][:, :], in1=A["ysb"][:, :], op=ALU.add))
            seq.dma("gpsimd", xo[rt * 128:(rt + 1) * 128, :], A["xtile"][:, :])

    al.close()


def l0ffn_host_prep(inputs):
    return {
        "gup": np.ascontiguousarray(inputs["L0_ffn_gate_up_packed"]).astype(np.int32),
        "gusT": np.ascontiguousarray(np.asarray(inputs["L0_ffn_gate_up_scales"]).T),
        "dnp": np.ascontiguousarray(inputs["L0_ffn_down_packed"]).astype(np.int32),
        "dnsT": np.ascontiguousarray(np.asarray(inputs["L0_ffn_down_scales"]).T),
    }


def run_l0ffn(x, inputs, n_cores=8, dbg=False):
    shared = l0ffn_host_prep(inputs)
    raise NotImplementedError
    in_maps = []
    for c in range(n_cores):
        xsh = np.ascontiguousarray(
            x[:, c * 512:(c + 1) * 512, :].reshape(2048, 2048)).astype(np.float32)
        m = {"xs": xsh}
        m.update(shared)
        in_maps.append(m)
    res = run_bass_kernel_spmd(nc, in_maps, core_ids=list(range(n_cores)))
    out = np.empty((4, 4096, 2048), np.float32)
    for c in range(n_cores):
        out[:, c * 512:(c + 1) * 512, :] = res.results[c]["xo"].reshape(4, 512, 2048)
    if dbg:
        return out, res.results
    return out


def emit_l1attn(nc, seq, xs, xo, pfx="c_", dbg=False):
    TOK, D = 2048, 2048
    SCALE = 1.0 / float(np.sqrt(128.0))

    wqkvT = nc.dram_tensor("wqkvT", [D, 3072], F32, kind="ExternalInput").ap()
    wpT = nc.dram_tensor("wpT", [D, D], F32, kind="ExternalInput").ap()
    if dbg:
        d_qkv = nc.dram_tensor("d_qkv", [128, 4 * 3072], BF16, kind="ExternalOutput").ap()
        d_S = nc.dram_tensor("d_S", [128, 160], F32, kind="ExternalOutput").ap()
        d_E = nc.dram_tensor("d_E", [128, 160], BF16, kind="ExternalOutput").ap()
        d_att = nc.dram_tensor("d_att", [128, 4 * 2048], BF16, kind="ExternalOutput").ap()
        d_mx = nc.dram_tensor("d_mx", [128, 64], F32, kind="ExternalOutput").ap()
        d_Dm = nc.dram_tensor("d_Dm", [128, 160], F32, kind="ExternalOutput").ap()

    al = Sbuf(nc, pfx)
    A = {}
    A["xlnT"] = al.sb("xlnT", [128, 16 * TOK], BF16)
    A["qkv"] = al.sb("qkv", [128, 4 * 3072], BF16)
    A["att_tm"] = al.sb("att_tm", [128, 4 * 2048], BF16)
    A["attnT"] = al.sb("attnT", [128, 16 * 512], BF16)
    A["wq_st"] = al.sb("wq_st", [128, 2 * 3072], BF16)
    A["wp_st"] = al.sb("wp_st", [128, 2 * 2048], BF16)
    A["prod"] = al.sb("prod", [128, 2048], BF16)
    A["acc"] = al.sb("acc", [128, 2048], F32)
    A["tmp2"] = al.sb("tmp2", [128, 2048], F32)
    A["S"] = al.sb("S", [128, 160], F32)
    A["Dm"] = al.sb("Dm", [128, 160], F32)
    A["E"] = al.sb("E", [128, 160], BF16)
    A["mx"] = al.sb("mx", [128, 64], F32)
    A["den"] = al.sb("den", [128, 64], F32)
    A["rden"] = al.sb("rden", [128, 64], F32)
    A["xtile"] = al.sb("xtile", [128, D], F32)
    A["lnb"] = al.sb("lnb", [128, D], BF16)
    A["ysb"] = al.sb("ysb", [128, D], F32)
    A["ident"] = al.sb("ident", [128, 128], BF16)
    for nm in ["c_eps5", "c_eps6", "c_m05", "s_", "ssq", "mean", "negmean", "t1", "var", "rstd"]:
        A[nm] = al.sb(nm, [128, 1], F32)
    PQ = al.ps("PQ", [128, 3072], F32)
    PT = al.ps("PT", [128, 1024], BF16)

    pairs = [(i, j) for i in range(4) for j in range(i + 1)]

    def qkv4(b):
        return A["qkv"][:, b * 3072:(b + 1) * 3072].rearrange(
            "p (g x d) -> p g x d", g=4, x=6)

    emit_consts(seq, A)

    for rt in range(16):
        emit_lnT(seq, A, xs, rt, A["xlnT"], rt * 128, TOK, PT)

    for pt in range(4):
        for i in range(4):
            rt = i * 4 + pt
            for cp in range(8):
                seq.dma("gpsimd",
                        A["wq_st"][:, :].rearrange("p (t n) -> p t n", t=2),
                        wqkvT[cp * 256:(cp + 1) * 256, :].rearrange("(t p) n -> p t n", p=128))
                for half in range(2):
                    c = cp * 2 + half

                    def qmm(t, c=c, rt=rt, half=half):
                        r = None
                        for nn in range(6):
                            r = t.matmul(
                                PQ[:, nn * 512:(nn + 1) * 512],
                                A["xlnT"][:, c * TOK + rt * 128: c * TOK + (rt + 1) * 128],
                                A["wq_st"][:, half * 3072 + nn * 512: half * 3072 + (nn + 1) * 512],
                                start=(c == 0), stop=(c == 15))
                        return r
                    seq.add("tensor", qmm)
            seq.add("vector", lambda v, i=i: v.tensor_copy(
                out=A["qkv"][:, i * 3072:(i + 1) * 3072], in_=PQ[:, :]))

        for pidx, (i, j) in enumerate(pairs):
            qa = qkv4(i)[:, :, 0:4, :]
            ka = qkv4(j)[:, :, 4:5, :].broadcast_to([128, 4, 4, 128])
            seq.add("vector", lambda v, qa=qa, ka=ka: v.tensor_tensor(
                out=A["prod"][:, :].rearrange("p (g x d) -> p g x d", g=4, x=4),
                in0=qa, in1=ka, op=ALU.mult))
            seq.add("vector", lambda v, pidx=pidx: v.tensor_reduce(
                out=A["S"][:, pidx * 16:(pidx + 1) * 16].rearrange("p (g x) -> p g x", g=4),
                in_=A["prod"][:, :].rearrange("p (g x d) -> p g x d", g=4, x=4),
                axis=AX.X, op=ALU.add))
        def scale_S(v):
            v.nop()
            return v.tensor_scalar(
                out=A["S"][:, :], in0=A["S"][:, :], scalar1=SCALE, scalar2=None, op0=ALU.mult)
        seq.add("vector", scale_S)

        def softmax(v):
            pofs = {(i, j): pairs.index((i, j)) * 16 for (i, j) in pairs}
            for i in range(4):
                mslice = A["mx"][:, i * 16:(i + 1) * 16]
                v.tensor_copy(out=mslice, in_=A["S"][:, pofs[(i, 0)]:pofs[(i, 0)] + 16])
                v.nop()
                for j in range(1, i + 1):
                    v.tensor_tensor(out=mslice, in0=mslice,
                                    in1=A["S"][:, pofs[(i, j)]:pofs[(i, j)] + 16],
                                    op=ALU.max)
                    v.nop()
            for pidx, (i, j) in enumerate(pairs):
                v.tensor_tensor(out=A["Dm"][:, pidx * 16:(pidx + 1) * 16],
                                in0=A["S"][:, pidx * 16:(pidx + 1) * 16],
                                in1=A["mx"][:, i * 16:(i + 1) * 16], op=ALU.subtract)
                v.nop()
            return v.tensor_copy(out=A["den"][:, :], in_=A["mx"][:, :])
        seq.add("vector", softmax)
        seq.add("scalar", lambda e: e.activation(
            out=A["E"][:, :], in_=A["Dm"][:, :], func=AF.Exp, bias=0.0, scale=1.0))

        def denom(v):
            pofs = {(i, j): pairs.index((i, j)) * 16 for (i, j) in pairs}
            for i in range(4):
                dslice = A["den"][:, i * 16:(i + 1) * 16]
                v.tensor_copy(out=dslice, in_=A["E"][:, pofs[(i, 0)]:pofs[(i, 0)] + 16])
                v.nop()
                for j in range(1, i + 1):
                    v.tensor_tensor(out=dslice, in0=dslice,
                                    in1=A["E"][:, pofs[(i, j)]:pofs[(i, j)] + 16], op=ALU.add)
                    v.nop()
            v.nop()
            return v.reciprocal(out=A["rden"][:, :], in_=A["den"][:, :])
        seq.add("vector", denom)

        for i in range(4):
            for j in range(i + 1):
                pidx = pairs.index((i, j))
                va = qkv4(j)[:, :, 5:6, :].broadcast_to([128, 4, 4, 128])
                ea = A["E"][:, pidx * 16:(pidx + 1) * 16].rearrange(
                    "p (g x) -> p g x", g=4).unsqueeze(3).broadcast_to([128, 4, 4, 128])
                dst = A["acc"] if j == 0 else A["tmp2"]
                seq.add("vector", lambda v, va=va, ea=ea, dst=dst: v.tensor_tensor(
                    out=dst[:, :].rearrange("p (g x d) -> p g x d", g=4, x=4),
                    in0=va, in1=ea, op=ALU.mult))
                if j > 0:
                    seq.add("vector", lambda v: v.tensor_tensor(
                        out=A["acc"][:, :], in0=A["acc"][:, :], in1=A["tmp2"][:, :], op=ALU.add))
            ra = A["rden"][:, i * 16:(i + 1) * 16].unsqueeze(2).broadcast_to([128, 16, 128])
            seq.add("vector", lambda v, i=i, ra=ra: v.tensor_tensor(
                out=A["att_tm"][:, i * 2048:(i + 1) * 2048].rearrange("p (h d) -> p h d", h=16),
                in0=A["acc"][:, :].rearrange("p (h d) -> p h d", h=16),
                in1=ra, op=ALU.mult))

        if dbg and pt == 3:
            seq.dma("gpsimd", d_qkv, A["qkv"][:, :])
            seq.dma("gpsimd", d_S, A["S"][:, :])
            seq.dma("gpsimd", d_E, A["E"][:, :])
            seq.dma("gpsimd", d_mx, A["mx"][:, :])
            seq.dma("gpsimd", d_Dm, A["Dm"][:, :])
            seq.dma("gpsimd", d_att, A["att_tm"][:, :])
        emit_transposes(seq, PT, [
            (A["att_tm"][:, i * 2048 + c * 128: i * 2048 + (c + 1) * 128],
             A["attnT"][:, c * 512 + i * 128: c * 512 + (i + 1) * 128])
            for i in range(4) for c in range(16)])

        for i in range(4):
            rt = i * 4 + pt
            for cp in range(8):
                seq.dma("gpsimd",
                        A["wp_st"][:, :].rearrange("p (t n) -> p t n", t=2),
                        wpT[cp * 256:(cp + 1) * 256, :].rearrange("(t p) n -> p t n", p=128))
                for half in range(2):
                    c = cp * 2 + half

                    def pmm(t, i=i, c=c, half=half):
                        r = None
                        for nn in range(4):
                            r = t.matmul(
                                PQ[:, nn * 512:(nn + 1) * 512],
                                A["attnT"][:, c * 512 + i * 128: c * 512 + (i + 1) * 128],
                                A["wp_st"][:, half * 2048 + nn * 512: half * 2048 + (nn + 1) * 512],
                                start=(c == 0), stop=(c == 15))
                        return r
                    seq.add("tensor", pmm)
            seq.add("scalar", lambda e: e.activation(
                out=A["ysb"][:, :], in_=PQ[:, 0:2048], func=AF.Copy, accum_out=A["s_"][:, :]))
            seq.add("scalar", lambda e: e.activation(
                out=A["lnb"][:, :], in_=A["ysb"][:, :], func=AF.Square, accum_out=A["ssq"][:, :]))

            def dve_stats(v):
                v.tensor_scalar(out=A["mean"][:, :], in0=A["s_"][:, :], scalar1=1.0 / D,
                                scalar2=None, op0=ALU.mult)
                v.tensor_scalar(out=A["negmean"][:, :], in0=A["s_"][:, :], scalar1=-1.0 / D,
                                scalar2=None, op0=ALU.mult)
                v.tensor_scalar(out=A["t1"][:, :], in0=A["ssq"][:, :], scalar1=1.0 / D,
                                scalar2=None, op0=ALU.mult)
                return v.scalar_tensor_tensor(out=A["var"][:, :], in0=A["mean"][:, :],
                                              scalar=A["negmean"][:, :], in1=A["t1"][:, :],
                                              op0=ALU.mult, op1=ALU.add)
            seq.add("vector", dve_stats)

            def act_rstd(e):
                e.activation(out=A["var"][:, :], in_=A["var"][:, :], func=AF.Ln,
                             bias=A["c_eps5"][:, :], scale=1.0)
                return e.activation(out=A["rstd"][:, :], in_=A["var"][:, :], func=AF.Exp,
                                    bias=0.0, scale=A["c_m05"][:, :])
            seq.add("scalar", act_rstd)
            seq.dma("sync", A["xtile"][:, :], xs[rt * 128:(rt + 1) * 128, :])
            seq.add("vector", lambda v: v.tensor_scalar(
                out=A["ysb"][:, :], in0=A["ysb"][:, :], scalar1=A["negmean"][:, :],
                scalar2=A["rstd"][:, :], op0=ALU.add, op1=ALU.mult))
            seq.add("vector", lambda v: v.tensor_tensor(
                out=A["xtile"][:, :], in0=A["xtile"][:, :], in1=A["ysb"][:, :], op=ALU.add))
            seq.dma("gpsimd", xo[rt * 128:(rt + 1) * 128, :], A["xtile"][:, :])

    al.close()


def run_l1attn(x, inputs, n_cores=8, dbg=False):
    shared = {
        "wqkvT": np.ascontiguousarray(np.asarray(inputs["L1_attn_weight_qkv"], np.float32).T),
        "wpT": np.ascontiguousarray(np.asarray(inputs["L1_attn_weight_proj"], np.float32).T),
    }
    raise NotImplementedError
    in_maps = []
    for c in range(n_cores):
        xsh = np.ascontiguousarray(
            x[:, c * 512:(c + 1) * 512, :].reshape(2048, 2048)).astype(np.float32)
        m = {"xs": xsh}
        m.update(shared)
        in_maps.append(m)
    res = run_bass_kernel_spmd(nc, in_maps, core_ids=list(range(n_cores)))
    out = np.empty((4, 4096, 2048), np.float32)
    for c in range(n_cores):
        out[:, c * 512:(c + 1) * 512, :] = res.results[c]["xo"].reshape(4, 512, 2048)
    if dbg:
        return out, res.results
    return out


def emit_l0attn(nc, seq, xs, xo, pfx="a_", dbg=False):
    TOK, D = 2048, 2048
    SCALE = 0.08838834764831845
    NEG = -1e30

    xh = nc.dram_tensor("xh", [TOK, D], F32, kind="ExternalInput").ap()
    wqc = nc.dram_tensor("wqc", [D, 512], F32, kind="ExternalInput").ap()
    wqu = nc.dram_tensor("wqu", [512, D], F32, kind="ExternalInput").ap()
    wk = nc.dram_tensor("wk", [D, 128], F32, kind="ExternalInput").ap()
    wv = nc.dram_tensor("wv", [D, 128], F32, kind="ExternalInput").ap()
    w1 = nc.dram_tensor("w1", [4, 512, 2048], F32, kind="ExternalInput").ap()
    w2 = nc.dram_tensor("w2", [8192, 2048], F32, kind="ExternalInput").ap()
    qnw = nc.dram_tensor("qnw", [128, 128], F32, kind="ExternalInput").ap()
    knw = nc.dram_tensor("knw", [128, 128], F32, kind="ExternalInput").ap()
    sinkb = nc.dram_tensor("sinkb", [128, 16], F32, kind="ExternalInput").ap()
    mask = nc.dram_tensor("mask", [4, 128, 640], F32, kind="ExternalInput").ap()
    cos_o = nc.dram_tensor("cos_o", [512, 32], F32, kind="ExternalInput").ap()
    sin_o = nc.dram_tensor("sin_o", [512, 32], F32, kind="ExternalInput").ap()
    cos_h = nc.dram_tensor("cos_h", [512, 32], F32, kind="ExternalInput").ap()
    sin_h = nc.dram_tensor("sin_h", [512, 32], F32, kind="ExternalInput").ap()
    if dbg:
        dbgt = {}
        for nm_, sh in [("kT", [128, 1024]), ("vsb", [128, 1024]), ("qT", [128, 8192]),
                        ("attnT", [128, 8192]), ("out1T", [128, 8192]), ("xlnT", [128, 8192]),
                        ("E", [128, 640]), ("smb0", [128, 640])]:
            dbgt[nm_] = nc.dram_tensor("d_" + nm_, sh,
                                       F32 if nm_ in ("smb0",) else BF16, kind="ExternalOutput").ap()
        d_y2 = nc.dram_tensor("d_y2", [128, 8192], BF16, kind="ExternalOutput").ap()
        d_qbuf = nc.dram_tensor("d_qbuf", [128, 2048], F32, kind="ExternalOutput").ap()
        d_qss = nc.dram_tensor("d_qss", [128, 16], F32, kind="ExternalOutput").ap()
        d_qrr = nc.dram_tensor("d_qrr", [128, 16], F32, kind="ExternalOutput").ap()
        d_qn = nc.dram_tensor("d_qn", [128, 2048], F32, kind="ExternalOutput").ap()
        d_kv = nc.dram_tensor("d_kv", [128, 128], F32, kind="ExternalOutput").ap()
        d_rb = nc.dram_tensor("d_rb", [128, 128], F32, kind="ExternalOutput").ap()
        d_s = nc.dram_tensor("d_s", [128, 2], F32, kind="ExternalOutput").ap()

    al = Sbuf(nc, pfx)
    A = {}
    A["mask"] = al.sb("m_mask", [128, 4 * 640], F32)
    A["cos_o"] = al.sb("m_cos_o", [128, 4 * 32], F32)
    A["sin_o"] = al.sb("m_sin_o", [128, 4 * 32], F32)
    A["cos_h"] = al.sb("m_cos_h", [128, 4 * 32], F32)
    A["sin_h"] = al.sb("m_sin_h", [128, 4 * 32], F32)
    A["qnw"] = al.sb("m_qnw", [128, 128], F32)
    A["knw"] = al.sb("m_knw", [128, 128], F32)
    A["sinkb"] = al.sb("m_sinkb", [128, 16], F32)
    A["negsink"] = al.sb("m_negsink", [128, 16], F32)
    A["wqc"] = al.sb("m_wqc", [128, 16 * 512], BF16)
    A["wqu"] = al.sb("m_wqu", [128, 4 * 2048], BF16)
    A["wk"] = al.sb("m_wk", [128, 16 * 128], BF16)
    A["wv"] = al.sb("m_wv", [128, 16 * 128], BF16)
    A["xlnT"] = al.sb("xlnT", [128, 16 * 512], BF16)
    A["lnTt"] = al.sb("lnTt", [128, 16 * 128], BF16)
    A["kT"] = al.sb("kT", [128, 1024], BF16)
    A["vsb"] = al.sb("vsb", [128, 8 * 128], BF16)
    A["kvtmp"] = al.sb("kvtmp", [128, 256], F32)
    A["U"] = al.sb("Ubuf", [128, 2048], F32)
    A["ra"] = A["U"][:, 0:512]
    A["rb"] = A["U"][:, 512:1024]
    A["rc"] = A["U"][:, 1024:1536]
    A["rd"] = A["U"][:, 1536:2048]
    A["qcT"] = al.sb("qcT", [128, 4 * 512], BF16)
    A["qbuf"] = al.sb("qbuf", [128, 2048], F32)
    A["qss"] = al.sb("qss", [128, 16], F32)
    A["qrr"] = al.sb("qrr", [128, 16], F32)
    A["qbf"] = al.sb("qbf", [128, 2048], BF16)
    A["qT"] = al.sb("qT", [128, 16 * 512], BF16)
    A["smb"] = al.sb("smb", [128, 640], F32)
    A["E"] = al.sb("E", [128, 640], BF16)
    A["eT"] = al.sb("eT", [128, 640], BF16)
    A["attnT"] = al.sb("attnT", [128, 16 * 512], BF16)
    A["w1g"] = al.sb("w1g", [128, 4 * 2048], BF16)
    A["out1T"] = al.sb("out1T", [128, 16 * 512], BF16)
    A["w2gh"] = al.sb("w2gh", [128, 16 * 512], BF16)
    A["y2acc"] = al.sb("y2acc", [128, 4 * 2048], BF16)
    A["xtile"] = A["qbuf"]
    A["lnb"] = al.sb("lnb", [128, D], BF16)
    A["ysb"] = A["U"]
    A["qsq"] = A["U"]
    A["ident"] = al.sb("ident", [128, 128], BF16)
    A["_stats"] = al.sb("_stats", [128, 16], F32)
    for _i, nm_ in enumerate(["dsum", "es", "nm", "nm2", "rr", "c_eps5", "c_eps6", "c_m05",
                              "s_", "ssq", "mean", "negmean", "t1", "var", "rstd"]):
        A[nm_] = A["_stats"][:, _i:_i + 1]
    PA = al.ps("PA", [128, 2048], F32)
    PB = al.ps("PB", [128, 1536], F32)
    PT = al.ps("PT2", [128, 1024], BF16)

    emit_consts(seq, A)

    seq.dma("sync", A["mask"][:, :].rearrange("p (q w) -> p q w", q=4), mask.transpose([1, 0, 2]))
    for (dst, src) in (("cos_o", cos_o), ("sin_o", sin_o), ("cos_h", cos_h), ("sin_h", sin_h)):
        seq.dma("sync", A[dst][:, :].rearrange("p (q w) -> p q w", q=4),
                src.rearrange("(q p) w -> p q w", p=128))
    seq.dma("sync", A["qnw"][:, :], qnw)
    seq.dma("sync", A["knw"][:, :], knw)
    seq.dma("sync", A["sinkb"][:, :], sinkb)
    seq.add("vector", lambda v: v.tensor_scalar(
        out=A["negsink"][:, :], in0=A["sinkb"][:, :], scalar1=-1.0, scalar2=None, op0=ALU.mult))
    for kc in range(16):
        seq.dma("gpsimd", A["wqc"][:, kc * 512:(kc + 1) * 512], wqc[kc * 128:(kc + 1) * 128, :])
        seq.dma("gpsimd", A["wk"][:, kc * 128:(kc + 1) * 128], wk[kc * 128:(kc + 1) * 128, :])
        seq.dma("gpsimd", A["wv"][:, kc * 128:(kc + 1) * 128], wv[kc * 128:(kc + 1) * 128, :])
    for qc_c in range(4):
        seq.dma("gpsimd", A["wqu"][:, qc_c * 2048:(qc_c + 1) * 2048],
                wqu[qc_c * 128:(qc_c + 1) * 128, :])

    def rope_rms_k(ht, src128, costile, sintile, is_halo):
        k = A["kvtmp"]
        seq.add("vector", lambda v: v.tensor_tensor(out=A["ra"][:, 0:32], in0=k[:, 0:32],
                                                    in1=costile, op=ALU.mult))
        seq.add("vector", lambda v: v.tensor_tensor(out=A["rb"][:, 0:32], in0=k[:, 32:64],
                                                    in1=sintile, op=ALU.mult))
        seq.add("vector", lambda v: v.tensor_tensor(out=A["rc"][:, 0:32], in0=k[:, 0:32],
                                                    in1=sintile, op=ALU.mult))
        seq.addf("vector", lambda v: v.tensor_tensor(out=A["rd"][:, 0:32], in0=k[:, 32:64],
                                                     in1=costile, op=ALU.mult))
        seq.addf("vector", lambda v: v.tensor_tensor(out=k[:, 0:32], in0=A["ra"][:, 0:32],
                                                     in1=A["rb"][:, 0:32], op=ALU.subtract))
        seq.addf("vector", lambda v: v.tensor_tensor(out=k[:, 32:64], in0=A["rc"][:, 0:32],
                                                     in1=A["rd"][:, 0:32], op=ALU.add))
        seq.add("scalar", lambda e: e.activation(out=A["ra"][:, 0:128], in_=k[:, 0:128],
                                                 func=AF.Square, accum_out=A["s_"][:, :]))
        seq.add("vector", lambda v: v.tensor_scalar(out=A["t1"][:, :], in0=A["s_"][:, :],
                                                    scalar1=1.0 / 128, scalar2=None, op0=ALU.mult))

        def act_rr(e):
            e.activation(out=A["var"][:, :], in_=A["t1"][:, :], func=AF.Ln,
                         bias=A["c_eps6"][:, :], scale=1.0)
            return e.activation(out=A["rr"][:, :], in_=A["var"][:, :], func=AF.Exp,
                                bias=0.0, scale=A["c_m05"][:, :])
        seq.add("scalar", act_rr)
        seq.addf("vector", lambda v: v.scalar_tensor_tensor(
            out=A["rb"][:, 0:128], in0=k[:, 0:128], scalar=A["rr"][:, :], in1=A["knw"][:, :],
            op0=ALU.mult, op1=ALU.mult))
        seq.add("vector", lambda v: v.tensor_copy(out=A["lnb"][:, 0:128], in_=A["rb"][:, 0:128]))
        seq.add("tensor", lambda t: t.transpose(PT[:, 0:128], A["lnb"][:, 0:128], A["ident"][:, :]))
        seq.add("vector", lambda v, ht=ht: v.tensor_copy(
            out=A["kT"][:, ht * 128:(ht + 1) * 128], in_=PT[:, 0:128]))

    for b in range(4):
        for tt in range(4):
            rt = b * 4 + tt
            emit_lnT(seq, A, xs, rt, A["xlnT"], tt * 128, 512, PT)
        for ht in range(8):
            if ht < 4:
                rt = b * 4 + ht
                emit_lnT(seq, A, xh, rt, A["lnTt"], 0, 128, PT)
                lnsrc, coff = A["lnTt"], lambda kc: kc * 128
            else:
                tt = ht - 4
                lnsrc, coff = A["xlnT"], lambda kc, tt=tt: kc * 512 + tt * 128
            for kc in range(16):
                seq.add("tensor", lambda t, lnsrc=lnsrc, coff=coff, kc=kc: t.matmul(
                    PA[:, 0:128], lnsrc[:, coff(kc):coff(kc) + 128],
                    A["wk"][:, kc * 128:(kc + 1) * 128], start=(kc == 0), stop=(kc == 15)))
                seq.add("tensor", lambda t, lnsrc=lnsrc, coff=coff, kc=kc: t.matmul(
                    PA[:, 512:640], lnsrc[:, coff(kc):coff(kc) + 128],
                    A["wv"][:, kc * 128:(kc + 1) * 128], start=(kc == 0), stop=(kc == 15)))
            seq.add("scalar", lambda e: e.copy(out=A["kvtmp"][:, 0:128], in_=PA[:, 0:128]))
            seq.add("vector", lambda v, ht=ht: v.tensor_copy(
                out=A["vsb"][:, ht * 128:(ht + 1) * 128], in_=PA[:, 512:640]))
            pt_i = ht % 4
            cs, sn = ("cos_h", "sin_h") if ht < 4 else ("cos_o", "sin_o")
            rope_rms_k(ht, None, A[cs][:, pt_i * 32:(pt_i + 1) * 32],
                       A[sn][:, pt_i * 32:(pt_i + 1) * 32], ht < 4)
            if dbg and b == 0 and ht == 7:
                seq.dma("gpsimd", d_kv, A["kvtmp"][:, 0:128])
                seq.dma("gpsimd", d_rb, A["rb"][:, 0:128])
                seq.add("vector", lambda v: v.tensor_copy(out=A["smb"][:, 0:1], in_=A["s_"][:, :]))
                seq.add("vector", lambda v: v.tensor_copy(out=A["smb"][:, 1:2], in_=A["rr"][:, :]))
                seq.dma("gpsimd", d_s, A["smb"][:, 0:2])

        for mc in range(4):
            for kc in range(16):
                seq.add("tensor", lambda t, mc=mc, kc=kc: t.matmul(
                    PA[:, 0:512],
                    A["wqc"][:, kc * 512 + mc * 128: kc * 512 + (mc + 1) * 128],
                    A["xlnT"][:, kc * 512:(kc + 1) * 512],
                    start=(kc == 0), stop=(kc == 15)))
            seq.add("vector", lambda v, mc=mc: v.tensor_copy(
                out=A["qcT"][:, mc * 512:(mc + 1) * 512], in_=PA[:, 0:512]))
        for tt in range(4):
            for qc_c in range(4):
                for nn in range(4):
                    seq.add("tensor", lambda t, qc_c=qc_c, tt=tt, nn=nn: t.matmul(
                        PA[:, nn * 512:(nn + 1) * 512],
                        A["qcT"][:, qc_c * 512 + tt * 128: qc_c * 512 + (tt + 1) * 128],
                        A["wqu"][:, qc_c * 2048 + nn * 512: qc_c * 2048 + (nn + 1) * 512],
                        start=(qc_c == 0), stop=(qc_c == 3)))
            seq.add("scalar", lambda e: e.copy(out=A["qbuf"][:, :], in_=PA[:, :]))
            co = A["cos_o"][:, tt * 32:(tt + 1) * 32].unsqueeze(1).broadcast_to([128, 16, 32])
            si = A["sin_o"][:, tt * 32:(tt + 1) * 32].unsqueeze(1).broadcast_to([128, 16, 32])
            q3 = A["qbuf"][:, :].rearrange("p (h d) -> p h d", h=16)
            for (dst, xsl, tr) in (("ra", (0, 32), co), ("rb", (32, 64), si),
                                   ("rc", (0, 32), si), ("rd", (32, 64), co)):
                seq.addf("vector", lambda v, dst=dst, xsl=xsl, tr=tr, q3=q3: v.tensor_tensor(
                    out=A[dst][:, 0:512].rearrange("p (h d) -> p h d", h=16),
                    in0=q3[:, :, xsl[0]:xsl[1]], in1=tr, op=ALU.mult))
            seq.addf("vector", lambda v, q3=q3: v.tensor_tensor(
                out=q3[:, :, 0:32],
                in0=A["ra"][:, 0:512].rearrange("p (h d) -> p h d", h=16),
                in1=A["rb"][:, 0:512].rearrange("p (h d) -> p h d", h=16), op=ALU.subtract))
            seq.addf("vector", lambda v, q3=q3: v.tensor_tensor(
                out=q3[:, :, 32:64],
                in0=A["rc"][:, 0:512].rearrange("p (h d) -> p h d", h=16),
                in1=A["rd"][:, 0:512].rearrange("p (h d) -> p h d", h=16), op=ALU.add))
            seq.add("scalar", lambda e: e.nop())
            seq.addf("vector", lambda v: v.tensor_tensor(
                out=A["qsq"][:, :], in0=A["qbuf"][:, :], in1=A["qbuf"][:, :], op=ALU.mult))
            seq.add("scalar", lambda e: e.nop())
            seq.addf("vector", lambda v, q3=q3: v.tensor_reduce(
                out=A["qss"][:, :].rearrange("p (h x) -> p h x", x=1),
                in_=A["qsq"][:, :].rearrange("p (h d) -> p h d", h=16),
                axis=AX.X, op=ALU.add))
            seq.add("vector", lambda v: v.nop())
            seq.addf("vector", lambda v: v.tensor_scalar(
                out=A["qss"][:, :], in0=A["qss"][:, :], scalar1=1.0 / 128, scalar2=None,
                op0=ALU.mult))

            def act_qrr(e):
                e.activation(out=A["qss"][:, :], in_=A["qss"][:, :], func=AF.Ln,
                             bias=A["c_eps6"][:, :], scale=1.0)
                e.nop()
                return e.activation(out=A["qrr"][:, :], in_=A["qss"][:, :], func=AF.Exp,
                                    bias=0.0, scale=A["c_m05"][:, :])
            seq.add("scalar", act_qrr)
            rr3 = A["qrr"][:, :].unsqueeze(2).broadcast_to([128, 16, 128])
            qn3 = A["qnw"][:, :].unsqueeze(1).broadcast_to([128, 16, 128])
            seq.addf("vector", lambda v, q3=q3, rr3=rr3: v.tensor_tensor(
                out=A["qsq"][:, :].rearrange("p (h d) -> p h d", h=16),
                in0=q3, in1=rr3, op=ALU.mult))
            seq.addf("vector", lambda v, qn3=qn3: v.tensor_tensor(
                out=A["qbf"][:, :].rearrange("p (h d) -> p h d", h=16),
                in0=A["qsq"][:, :].rearrange("p (h d) -> p h d", h=16),
                in1=qn3, op=ALU.mult))
            if dbg and b == 0 and tt == 0:
                seq.dma("gpsimd", d_qbuf, A["qbuf"][:, :])
                seq.dma("gpsimd", d_qss, A["qss"][:, :])
                seq.dma("gpsimd", d_qrr, A["qrr"][:, :])
                seq.dma("gpsimd", d_qn, A["qsq"][:, :])
            emit_transposes(seq, PT, [
                (A["qbf"][:, h * 128:(h + 1) * 128],
                 A["qT"][:, h * 512 + tt * 128: h * 512 + (tt + 1) * 128])
                for h in range(16)])

        for h in range(16):
            for qt in range(4):
                seq.add("tensor", lambda t, h=h, qt=qt: t.matmul(
                    PA[:, 0:512], A["qT"][:, h * 512 + qt * 128: h * 512 + (qt + 1) * 128],
                    A["kT"][:, qt * 128: qt * 128 + 512], start=True, stop=True))
                seq.add("tensor", lambda t, h=h, qt=qt: t.matmul(
                    PA[:, 512:640], A["qT"][:, h * 512 + qt * 128: h * 512 + (qt + 1) * 128],
                    A["kT"][:, qt * 128 + 512: qt * 128 + 640], start=True, stop=True))
                seq.add("vector", lambda v, qt=qt: v.scalar_tensor_tensor(
                    out=A["smb"][:, :], in0=PA[:, 0:640], scalar=SCALE,
                    in1=A["mask"][:, qt * 640:(qt + 1) * 640], op0=ALU.mult, op1=ALU.add))
                seq.add("scalar", lambda e: e.nop())
                seq.add("vector", lambda v: v.reduce_max(
                    out=A["nm"][:, :], in_=A["smb"][:, :], axis=AX.X, negate=True))
                seq.addf("vector", lambda v, h=h: v.tensor_tensor(
                    out=A["nm2"][:, :], in0=A["nm"][:, :], in1=A["negsink"][:, h:h + 1],
                    op=ALU.min))
                seq.add("scalar", lambda e: e.activation(
                    out=A["E"][:, :], in_=A["smb"][:, :], func=AF.Exp, bias=A["nm2"][:, :],
                    scale=1.0, accum_out=A["dsum"][:, :]))
                seq.add("scalar", lambda e, h=h: e.activation(
                    out=A["es"][:, :], in_=A["sinkb"][:, h:h + 1], func=AF.Exp,
                    bias=A["nm2"][:, :], scale=1.0))
                seq.addf("vector", lambda v: v.tensor_tensor(
                    out=A["dsum"][:, :], in0=A["dsum"][:, :], in1=A["es"][:, :], op=ALU.add))
                seq.addf("vector", lambda v: v.reciprocal(out=A["rr"][:, :], in_=A["dsum"][:, :]))
                seq.addf("vector", lambda v: v.tensor_scalar(
                    out=A["E"][:, :], in0=A["E"][:, :], scalar1=A["rr"][:, :], scalar2=None,
                    op0=ALU.mult))
                emit_transposes(seq, PT, [
                    (A["E"][:, sc_ * 128:(sc_ + 1) * 128],
                     A["eT"][:, sc_ * 128:(sc_ + 1) * 128]) for sc_ in range(5)])
                for sc_ in range(5):
                    seq.add("tensor", lambda t, sc_=sc_, qt=qt: t.matmul(
                        PA[:, 1024:1152],
                        A["vsb"][:, (qt + sc_) * 128: (qt + sc_ + 1) * 128],
                        A["eT"][:, sc_ * 128:(sc_ + 1) * 128],
                        start=(sc_ == 0), stop=(sc_ == 4)))
                seq.add("vector", lambda v, h=h, qt=qt: v.tensor_copy(
                    out=A["attnT"][:, h * 512 + qt * 128: h * 512 + (qt + 1) * 128],
                    in_=PA[:, 1024:1152]))

        if dbg and b == 0:
            seq.dma("gpsimd", dbgt["kT"], A["kT"][:, :])
            seq.dma("gpsimd", dbgt["vsb"], A["vsb"][:, :])
            seq.dma("gpsimd", dbgt["qT"], A["qT"][:, :])
            seq.dma("gpsimd", dbgt["attnT"], A["attnT"][:, :])
            seq.dma("gpsimd", dbgt["xlnT"], A["xlnT"][:, :])
            seq.dma("gpsimd", dbgt["E"], A["E"][:, :])
            seq.dma("gpsimd", dbgt["smb0"], A["smb"][:, :])
        for g in range(4):
            for ic in range(4):
                seq.dma("gpsimd", A["w1g"][:, ic * 2048:(ic + 1) * 2048],
                        w1[g, ic * 128:(ic + 1) * 128, :])
            for oc in range(16):
                for ic in range(4):
                    seq.add("tensor", lambda t, oc=oc, ic=ic, g=g: t.matmul(
                        PB[:, 0:512],
                        A["w1g"][:, ic * 2048 + oc * 128: ic * 2048 + (oc + 1) * 128],
                        A["attnT"][:, (4 * g + ic) * 512:(4 * g + ic + 1) * 512],
                        start=(ic == 0), stop=(ic == 3)))
                seq.add("vector", lambda v, oc=oc: v.tensor_copy(
                    out=A["out1T"][:, oc * 512:(oc + 1) * 512], in_=PB[:, 0:512]))
            for quar in range(4):
                for gic in range(16):
                    seq.dma("gpsimd", A["w2gh"][:, gic * 512:(gic + 1) * 512],
                            w2[g * 2048 + gic * 128: g * 2048 + (gic + 1) * 128,
                               quar * 512:(quar + 1) * 512])
                for tt in range(4):
                    for gic in range(16):
                        seq.add("tensor", lambda t, gic=gic, tt=tt: t.matmul(
                            PB[:, 512:1024],
                            A["out1T"][:, gic * 512 + tt * 128: gic * 512 + (tt + 1) * 128],
                            A["w2gh"][:, gic * 512:(gic + 1) * 512],
                            start=(gic == 0), stop=(gic == 15)))
                    if g == 0:
                        seq.add("vector", lambda v, tt=tt, quar=quar: v.tensor_copy(
                            out=A["y2acc"][:, tt * 2048 + quar * 512: tt * 2048 + (quar + 1) * 512],
                            in_=PB[:, 512:1024]))
                    else:
                        seq.add("vector", lambda v, tt=tt, quar=quar: v.tensor_tensor(
                            out=A["y2acc"][:, tt * 2048 + quar * 512: tt * 2048 + (quar + 1) * 512],
                            in0=A["y2acc"][:, tt * 2048 + quar * 512: tt * 2048 + (quar + 1) * 512],
                            in1=PB[:, 512:1024], op=ALU.add))

        if dbg and b == 0:
            seq.dma("gpsimd", dbgt["out1T"], A["out1T"][:, :])
            seq.dma("gpsimd", d_y2, A["y2acc"][:, :])
        for tt in range(4):
            rt = b * 4 + tt
            emit_ln_stats(seq, A, A["y2acc"][:, tt * 2048:(tt + 1) * 2048], A["lnb"])
            seq.dma("sync", A["xtile"][:, :], xs[rt * 128:(rt + 1) * 128, :])
            seq.addf("vector", lambda v, tt=tt: v.tensor_scalar(
                out=A["ysb"][:, :], in0=A["y2acc"][:, tt * 2048:(tt + 1) * 2048],
                scalar1=A["negmean"][:, :], scalar2=A["rstd"][:, :],
                op0=ALU.add, op1=ALU.mult))
            seq.add("vector", lambda v: v.tensor_tensor(
                out=A["xtile"][:, :], in0=A["xtile"][:, :], in1=A["ysb"][:, :], op=ALU.add))
            seq.dma("gpsimd", xo[rt * 128:(rt + 1) * 128, :], A["xtile"][:, :])

    al.close()


def l0attn_host_prep(inputs):
    shared = {
        "wqc": np.ascontiguousarray(np.asarray(inputs["L0_attn_wq_comp"], np.float32)),
        "wqu": np.ascontiguousarray(np.asarray(inputs["L0_attn_wq_up"], np.float32)),
        "wk": np.ascontiguousarray(np.asarray(inputs["L0_attn_wk"], np.float32)),
        "wv": np.ascontiguousarray(np.asarray(inputs["L0_attn_wv"], np.float32)),
        "w1": np.ascontiguousarray(np.asarray(inputs["L0_attn_out_w1"], np.float32)),
        "w2": np.ascontiguousarray(np.asarray(inputs["L0_attn_out_w2"], np.float32)),
        "qnw": np.ascontiguousarray(np.broadcast_to(
            np.asarray(inputs["L0_attn_q_norm_w"], np.float32)[None, :], (128, 128))),
        "knw": np.ascontiguousarray(np.broadcast_to(
            np.asarray(inputs["L0_attn_k_norm_w"], np.float32)[None, :], (128, 128))),
        "sinkb": np.ascontiguousarray(np.broadcast_to(
            np.asarray(inputs["L0_attn_sink_logit"], np.float32)[None, :], (128, 16))),
    }
    inv = 1.0 / (10000.0 ** (np.arange(0, 64, 2, dtype=np.float32) / 64.0))
    def rope(p0):
        pos = (p0 + np.arange(512, dtype=np.float32))[:, None]
        ang = pos * inv[None, :]
        return np.cos(ang).astype(np.float32), np.sin(ang).astype(np.float32)
    i_ = np.arange(128)[:, None]
    jj = np.arange(640)[None, :]
    base_valid = (jj > i_) & (jj <= i_ + 512)
    per_core = []
    for c in range(8):
        m = np.empty((4, 128, 640), np.float32)
        for qt in range(4):
            valid = base_valid.copy()
            if c == 0:
                valid = valid & (jj >= 512 - qt * 128)
            m[qt] = np.where(valid, 0.0, -1e30)
        cos_o, sin_o = rope(512.0 * c)
        cos_h, sin_h = rope(512.0 * (c - 1))
        per_core.append({"mask": m, "cos_o": cos_o, "sin_o": sin_o,
                         "cos_h": cos_h, "sin_h": sin_h})
    return shared, per_core


def run_l0attn(x, inputs, n_cores=8, dbg=False):
    shared, per_core = l0attn_host_prep(inputs)
    raise NotImplementedError
    in_maps = []
    for c in range(n_cores):
        xsh = np.ascontiguousarray(
            x[:, c * 512:(c + 1) * 512, :].reshape(2048, 2048)).astype(np.float32)
        if c == 0:
            xhh = np.zeros((2048, 2048), np.float32)
        else:
            xhh = np.ascontiguousarray(
                x[:, (c - 1) * 512:c * 512, :].reshape(2048, 2048)).astype(np.float32)
        m = {"xs": xsh, "xh": xhh}
        m.update(shared)
        m.update(per_core[c])
        in_maps.append(m)
    res = run_bass_kernel_spmd(nc, in_maps, core_ids=list(range(n_cores)))
    out = np.empty((4, 4096, 2048), np.float32)
    for c in range(n_cores):
        out[:, c * 512:(c + 1) * 512, :] = res.results[c]["xo"].reshape(4, 512, 2048)
    if dbg:
        return out, res.results
    return out


def build_fused():
    nc = bass.Bass("TRN2", target_bir_lowering=False)
    seq = Seq(nc)
    xs0 = nc.dram_tensor("xs", [2048, 2048], F32, kind="ExternalInput").ap()
    x1d = nc.dram_tensor("x1d", [2048, 2048], F32).ap()
    x2d = nc.dram_tensor("x2d", [2048, 2048], F32).ap()
    x3d = nc.dram_tensor("x3d", [2048, 2048], F32).ap()
    xo = nc.dram_tensor("xo", [2048, 2048], F32, kind="ExternalOutput").ap()
    emit_l0attn(nc, seq, xs0, x1d, pfx="a_")
    emit_l0ffn(nc, seq, x1d, x2d, pfx="b_")
    emit_l1attn(nc, seq, x2d, x3d, pfx="c_")
    emit_l1ffn(nc, seq, x3d, xo, pfx="d_")
    seq.build()
    return nc


def kernel(**inputs):
    x = np.ascontiguousarray(np.asarray(inputs["L0_attn_h"], np.float32))
    n_cores = 8
    shared_a, per_core_a = l0attn_host_prep(inputs)
    shared_b = l0ffn_host_prep(inputs)
    shared_c = {
        "wqkvT": np.ascontiguousarray(np.asarray(inputs["L1_attn_weight_qkv"], np.float32).T),
        "wpT": np.ascontiguousarray(np.asarray(inputs["L1_attn_weight_proj"], np.float32).T),
    }
    shared_d = l1ffn_host_prep(inputs)
    nc = build_fused()
    in_maps = []
    for c in range(n_cores):
        xsh = np.ascontiguousarray(
            x[:, c * 512:(c + 1) * 512, :].reshape(2048, 2048)).astype(np.float32)
        if c == 0:
            xhh = np.zeros((2048, 2048), np.float32)
        else:
            xhh = np.ascontiguousarray(
                x[:, (c - 1) * 512:c * 512, :].reshape(2048, 2048)).astype(np.float32)
        m = {"xs": xsh, "xh": xhh}
        m.update(shared_a)
        m.update(per_core_a[c])
        m.update(shared_b)
        m.update(shared_c)
        m.update(shared_d)
        in_maps.append(m)
    res = run_bass_kernel_spmd(nc, in_maps, core_ids=list(range(n_cores)))
    out = np.empty((4, 4096, 2048), np.float32)
    for c in range(n_cores):
        out[:, c * 512:(c + 1) * 512, :] = res.results[c]["xo"].reshape(4, 512, 2048)
    return out


_NC_CACHE = {}


def _get_nc(name, builder):
    if name not in _NC_CACHE:
        _NC_CACHE[name] = builder()
    return _NC_CACHE[name]


def kernel_multilaunch(**inputs):
    x = np.ascontiguousarray(np.asarray(inputs["L0_attn_h"], np.float32))
    x = run_l0attn(x, inputs)
    x = run_l0ffn(x, inputs)
    x = run_l1attn(x, inputs)
    x = run_l1ffn(x, inputs)
    return np.ascontiguousarray(x.astype(np.float32))
